# version 08
# baseline (speedup 1.0000x reference)
import contextlib
import numpy as np
import ml_dtypes
import concourse.bass as bass
import concourse.mybir as mybir
from concourse.bass_utils import run_bass_kernel_spmd
from concourse.libnrt import physical_to_logical_core_id

F32 = mybir.dt.float32
BF16 = mybir.dt.bfloat16
AF = mybir.ActivationFunctionType
OP = mybir.AluOpType

NCORE = 8
L = 8192
D = 2048
E = 4096
N = 16
R = 128
I = 8192
LC = L // NCORE
EC = E // NCORE
NET = EC // 128
CH = 1024
NCH = L // CH
EPS = 1e-5

ENGINES = ("sync", "gpsimd", "vector", "scalar", "tensor")


class P:

    def __init__(self, nc, stack, semnames):
        self.nc = nc
        self.prog = {e: [] for e in ENGINES}
        self.sems = {s: stack.enter_context(nc.semaphore(s)) for s in semnames}
        self.c = {s: 0 for s in semnames}
        self._last = {}
        self.nbar = 0

    def wait(self, eng, sem, val):
        if val is None or val <= 0:
            return
        self.prog[eng].append(lambda e, s=sem, v=val: e.wait_ge(self.sems[s], v))

    CHAIN = {"vector": "vv", "scalar": "aa", "gpsimd": "gg"}

    def emit(self, eng, builder, inc=None, inck=1):
        chain_sem = self.CHAIN.get(eng)
        dma_like = (inc in ("dgp", "dsy", "dsc") or (inc or "").startswith("cc")
                    or (inc or "").startswith("fw") or (inc or "").startswith("sw"))
        if chain_sem is not None and not dma_like:
            self.wait(eng, chain_sem, self._last.get(eng, 0))
            if inc is None:
                inc, inck = chain_sem, 1
            assert inc == chain_sem, (eng, inc)
        if inc is not None:
            self.c[inc] += inck
            v = self.c[inc]

            def fn(e, b=builder, s=inc, k=inck):
                b(e).then_inc(self.sems[s], k)
            self.prog[eng].append(fn)
            if chain_sem is not None and not dma_like:
                self._last[eng] = v
            return v
        self.prog[eng].append(lambda e, b=builder: b(e))
        return None

    def ack(self, eng, qsem, asem):
        self.wait(eng, qsem, self.c[qsem])
        self.c[asem] += 1
        v = self.c[asem]
        self.prog[eng].append(lambda e, s=asem: e.sem_inc(self.sems[s], 1))
        return v

    def barrier(self, dma_flush=()):
        self.nbar += 1
        target = 5 * self.nbar
        flush = {}
        for eng, s in dma_flush:
            flush.setdefault(eng, []).append((s, self.c[s]))
        for eng in ENGINES:
            for s, v in flush.get(eng, []):
                self.wait(eng, s, v)
            self.prog[eng].append(lambda e: e.sem_inc(self.sems["bar"], 1))
            self.wait(eng, "bar", target)


def build_nc(debug=False, preamble="clear_sled"):
    nc = bass.Bass("TRN2", target_bir_lowering=False)

    def din(name, shape, dtype=F32):
        return nc.dram_tensor(name, shape, dtype, kind="ExternalInput").ap()

    resid = din("resid", [LC, D])
    nw_rep = din("nw_rep", [128, D])
    inwT = din("inwT", [D, EC])
    skwT = din("skwT", [D, EC])
    wd1T = din("wd1T", [EC, R])
    wd2T = din("wd2T", [R, EC])
    wbcT = din("wbcT", [EC, 32])
    evec = din("evec", [EC, 24])
    sel_in = din("sel_in", [32, 32 * 128], BF16)
    wguT_sl = din("wguT_sl", [128, 32768])
    wdnT_sl = din("wdnT_sl", [I // NCORE, D])
    owT_sl = din("owT_sl", [E // NCORE, D])
    out = nc.dram_tensor("out", [LC, D], F32, kind="ExternalOutput").ap()

    ag_rn_in = nc.dram_tensor("ag_rn_in", [D, LC], BF16)
    RNT_ALL = nc.dram_tensor("RNT_ALL", [NCORE * D, LC], BF16, addr_space="Shared")
    XNT_SP = nc.dram_tensor("XNT_SP", [D, LC], BF16)
    ag_wgu_in = nc.dram_tensor("ag_wgu_in", [128, 32768], BF16)
    WGUT = nc.dram_tensor("WGUT", [128, 262144], BF16, addr_space="Shared")
    ag_wdn_in = nc.dram_tensor("ag_wdn_in", [I // NCORE, D], BF16)
    WDNT = nc.dram_tensor("WDNT", [I, D], BF16, addr_space="Shared")
    ag_ow_in = nc.dram_tensor("ag_ow_in", [E // NCORE, D], BF16)
    OWT = nc.dram_tensor("OWT", [E, D], BF16, addr_space="Shared")
    ar_in = nc.dram_tensor("ar_in", [R + 32, L], F32)
    ar_out = nc.dram_tensor("ar_out", [R + 32, L], F32, addr_space="Shared")
    ag_y_in = nc.dram_tensor("ag_y_in", [EC, L], BF16)
    Y_ALL = nc.dram_tensor("Y_ALL", [E, L], BF16, addr_space="Shared")
    XN_SP = nc.dram_tensor("XN_SP", [LC, D], F32)
    SK2_SP = nc.dram_tensor("SK2_SP", [EC, L], F32)
    OUT2_SP = nc.dram_tensor("OUT2_SP", [LC, D], F32)
    X_SP = nc.dram_tensor("X_SP", [EC, L], BF16)
    H_SP = nc.dram_tensor("H_SP", [I, LC], BF16)

    dbg = {}
    if debug:
        for nm, shape, dt in [
            ("dbg_xn", [LC, D], F32), ("dbg_rn", [LC, D], BF16),
            ("dbg_d1bc", [R + 32, L], F32), ("dbg_x", [EC, L], BF16),
            ("dbg_sk2", [EC, L], F32), ("dbg_delta", [EC, L], F32),
            ("dbg_y", [EC, L], BF16), ("dbg_out2", [LC, D], F32),
            ("dbg_scan", [128, 6 * CH], F32),
        ]:
            dbg[nm] = nc.dram_tensor(nm, shape, dt, kind="ExternalOutput").ap()

    semnames = ["bar", "dsy", "dgp", "dsc", "asy", "agp", "asc", "mm", "vv", "aa", "gg",
                "fw0", "fw1", "fw2", "fw3", "sw0", "sw1", "sw2", "sw3",
                "cc_rn", "cc_wgu", "cc_wdn", "cc_ow", "cc_ar", "cc_y"]

    with contextlib.ExitStack() as g:
        p = P(nc, g, semnames)
        W, EM = p.wait, p.emit

        def gack():
            return p.ack("gpsimd", "dgp", "agp")

        def sack():
            return p.ack("sync", "dsy", "asy")

        nums = sorted(s.num for s in p.sems.values())
        assert nums == list(range(nums[0], nums[0] + len(nums))), nums
        if preamble == "clear_sled":
            for _ in range(40):
                nc.gpsimd.nop(cycle_cnt=2400)
            nc.gpsimd.sem_clear(range(nums[0], nums[-1] + 1))
            for eng in (nc.sync, nc.vector, nc.scalar, nc.tensor):
                for _ in range(120):
                    eng.nop(cycle_cnt=2400)
            nc.all_engine_barrier()

        sbg = lambda name, shape, dtype=F32: g.enter_context(nc.sbuf_tensor(name, shape, dtype))
        evs = sbg("evs", [128, NET * 24])
        asb = sbg("asb", [128, NET * 16])
        carry = sbg("carry", [128, NET * N])

        def ecol(et, c):
            return evs[:, et * 24 + c:et * 24 + c + 1]

        with contextlib.ExitStack() as ph:
            sb = lambda name, shape, dtype=F32: ph.enter_context(nc.sbuf_tensor(name, shape, dtype))
            nwr = sb("nwr", [128, D])
            rsd = [sb(f"rsd{i}", [128, D]) for i in range(2)]
            sq_scr = sb("sq_scr", [128, D])
            xn_t = [sb(f"xn_t{i}", [128, D]) for i in range(2)]
            rn_bf = [sb(f"rn_bf{i}", [128, D], BF16) for i in range(2)]
            xn_bf = [sb(f"xn_bf{i}", [128, D], BF16) for i in range(2)]
            rnT_sb = [sb(f"rnT_sb{i}", [128, D], BF16) for i in range(2)]
            xnT_sb = [sb(f"xnT_sb{i}", [128, D], BF16) for i in range(2)]
            idn = sb("idn", [128, 128], BF16)
            stat = sb("stat", [128, 8])
            wf32 = sb("wf32", [128, 4 * 2048])
            wbf = sb("wbf", [128, 4 * 2048], BF16)
            pt = [ph.enter_context(nc.psum_tensor(f"pt{i}", [128, 128], BF16)) for i in range(4)]

            EM("gpsimd", lambda e: e.dma_start(out=nwr[:, :], in_=nw_rep), inc="dgp", inck=16)
            EM("gpsimd", lambda e: e.dma_start(
                out=evs[:, :].rearrange("p (t c) -> p t c", t=NET),
                in_=evec.rearrange("(t p) c -> p t c", p=128)), inc="dgp", inck=16)
            a_ev = gack()
            EM("gpsimd", lambda e: e.memset(idn[:, :], 0.0), inc="gg")
            idv = EM("gpsimd", lambda e: e.affine_select(
                out=idn[:, :], in_=idn[:, :], compare_op=OP.not_equal, fill=1.0,
                base=0, pattern=[[-1, 128]], channel_multiplier=1), inc="gg")
            W("scalar", "agp", a_ev)
            EM("scalar", lambda e: e.activation(
                asb[:, :].rearrange("p (t n) -> p t n", t=NET),
                evs[:, :].rearrange("p (t c) -> p t c", t=NET)[:, :, 8:24], AF.Exp), inc="aa")
            W("vector", "aa", p.c["aa"])
            EM("vector", lambda e: e.tensor_scalar_mul(asb[:, :], asb[:, :], -1.0), inc="vv")

            cast_srcs = []
            for k in range(16):
                cast_srcs.append((wguT_sl, ag_wgu_in, 0, k * 2048))
            for rt in range(8):
                cast_srcs.append((wdnT_sl, ag_wdn_in, rt * 128, 0))
            for rt in range(4):
                cast_srcs.append((owT_sl, ag_ow_in, rt * 128, 0))
            castv, swc = {}, {}
            cu = [0]

            def emit_cast_unit():
                k = cu[0]
                if k >= len(cast_srcs):
                    return
                cu[0] += 1
                s = k % 4
                src_t, dst_t, r0, c0 = cast_srcs[k]
                if (k - 4) in castv:
                    W("gpsimd", "vv", castv[k - 4])
                EM("gpsimd", lambda e, src_t=src_t, r0=r0, c0=c0, s=s: e.dma_start(
                    out=wf32[:, s * 2048:(s + 1) * 2048],
                    in_=src_t[r0:r0 + 128, c0:c0 + 2048]), inc=f"fw{s}", inck=16)
                fcnt = p.c[f"fw{s}"]
                W("vector", f"fw{s}", fcnt)
                if (k - 4) in swc:
                    W("vector", f"sw{s}", swc[k - 4])
                castv[k] = EM("vector", lambda e, s=s: e.tensor_copy(
                    wbf[:, s * 2048:(s + 1) * 2048], wf32[:, s * 2048:(s + 1) * 2048]), inc="vv")
                W("sync", "vv", castv[k])
                EM("sync", lambda e, dst_t=dst_t, r0=r0, c0=c0, s=s: e.dma_start(
                    out=dst_t.ap()[r0:r0 + 128, c0:c0 + 2048],
                    in_=wbf[:, s * 2048:(s + 1) * 2048]), inc=f"sw{s}", inck=16)
                swc[k] = p.c[f"sw{s}"]

            rsd_ack = {}
            for t in range(2):
                EM("sync", lambda e, t=t: e.dma_start(
                    out=rsd[t % 2][:, :], in_=resid[t * 128:(t + 1) * 128, :]), inc="dsy", inck=16)
                rsd_ack[t] = sack()

            tT_ack = {}
            xnf = {}
            for t in range(8):
                b = t % 2
                W("vector", "asy", rsd_ack[t])
                rs1 = EM("vector", lambda e, b=b: e.reduce_sum(
                    stat[:, 0:1], rsd[b][:, :], axis=mybir.AxisListType.X), inc="vv")
                W("vector", "vv", rs1)
                EM("vector", lambda e: e.tensor_scalar_mul(stat[:, 0:1], stat[:, 0:1], 1.0 / D))
                sqv = EM("vector", lambda e, b=b: e.tensor_mul(sq_scr[:, :], rsd[b][:, :], rsd[b][:, :]),
                         inc="vv")
                W("vector", "vv", sqv)
                rs2 = EM("vector", lambda e: e.reduce_sum(
                    stat[:, 1:2], sq_scr[:, :], axis=mybir.AxisListType.X), inc="vv")
                W("vector", "vv", rs2)
                EM("vector", lambda e: e.tensor_scalar_mul(stat[:, 1:2], stat[:, 1:2], 1.0 / D))
                EM("vector", lambda e: e.tensor_mul(stat[:, 7:8], stat[:, 0:1], stat[:, 0:1]))
                EM("vector", lambda e: e.tensor_sub(stat[:, 2:3], stat[:, 1:2], stat[:, 7:8]))
                vv1 = EM("vector", lambda e: e.tensor_scalar(
                    stat[:, 3:4], stat[:, 2:3], 1.0, EPS, OP.mult, OP.add), inc="vv")
                W("scalar", "vv", vv1)
                EM("scalar", lambda e: e.activation(stat[:, 4:5], stat[:, 3:4], AF.Ln))
                aa1 = EM("scalar", lambda e: e.activation(stat[:, 5:6], stat[:, 4:5], AF.Exp, scale=-0.5),
                         inc="aa")
                aa1b = EM("scalar", lambda e: e.activation(stat[:, 6:7], stat[:, 4:5], AF.Exp, scale=-1.0),
                          inc="aa")
                W("vector", "aa", aa1b)
                if t >= 2:
                    W("vector", "asy", xnf[t - 2])
                xnv = EM("vector", lambda e, b=b: e.tensor_scalar(
                    xn_t[b][:, :], rsd[b][:, :], stat[:, 0:1], stat[:, 5:6], OP.subtract, OP.mult), inc="vv")
                if t + 2 < 8:
                    W("sync", "vv", xnv)
                    EM("sync", lambda e, t=t: e.dma_start(
                        out=rsd[t % 2][:, :], in_=resid[(t + 2) * 128:(t + 3) * 128, :]), inc="dsy", inck=16)
                    rsd_ack[t + 2] = sack()
                EM("vector", lambda e: e.tensor_mul(stat[:, 6:7], stat[:, 2:3], stat[:, 6:7]))
                vv2 = EM("vector", lambda e: e.tensor_scalar(
                    stat[:, 6:7], stat[:, 6:7], 1.0, EPS, OP.mult, OP.add), inc="vv")
                W("scalar", "vv", vv2)
                EM("scalar", lambda e: e.activation(stat[:, 6:7], stat[:, 6:7], AF.Ln))
                aa2 = EM("scalar", lambda e: e.activation(stat[:, 6:7], stat[:, 6:7], AF.Exp, scale=-0.5),
                         inc="aa")
                W("vector", "aa", aa2)
                rnv = EM("vector", lambda e, b=b: e.scalar_tensor_tensor(
                    rn_bf[b][:, :], xn_t[b][:, :], stat[:, 6:7], nwr[:, :], OP.mult, OP.mult), inc="vv")
                vd = EM("vector", lambda e, b=b: e.tensor_copy(xn_bf[b][:, :], xn_t[b][:, :]), inc="vv")
                W("sync", "vv", vd)
                EM("sync", lambda e, b=b, t=t: e.dma_start(
                    out=XN_SP.ap()[t * 128:(t + 1) * 128, :], in_=xn_t[b][:, :]), inc="dsy", inck=16)
                xnf[t] = sack()
                W("tensor", "vv", vd)
                if t >= 2:
                    W("tensor", "asy", tT_ack[t - 2])
                tcp = {}
                if t == 0:
                    W("tensor", "gg", idv)
                for half, (srcb, dstb) in enumerate([(rn_bf[b], rnT_sb[b]), (xn_bf[b], xnT_sb[b])]):
                    for dt in range(16):
                        j = half * 16 + dt
                        pj = j % 4
                        if (j - 4) in tcp:
                            W("tensor", "aa", tcp[j - 4])
                        tmm = EM("tensor", lambda e, srcb=srcb, dt=dt, pj=pj: e.transpose(
                            pt[pj][:, :], srcb[:, dt * 128:(dt + 1) * 128], idn[:, :]), inc="mm")
                        W("scalar", "mm", tmm)
                        tcp[j] = EM("scalar", lambda e, dstb=dstb, dt=dt, pj=pj: e.activation(
                            dstb[:, dt * 128:(dt + 1) * 128], pt[pj][:, :], AF.Identity), inc="aa")
                W("sync", "aa", tcp[31])
                EM("sync", lambda e, b=b, t=t: e.dma_start(
                    out=ag_rn_in.ap()[:, t * 128:(t + 1) * 128].rearrange("(d p) l -> p d l", p=128),
                    in_=rnT_sb[b][:, :].rearrange("p (d l) -> p d l", d=16)), inc="dsy", inck=16)
                EM("sync", lambda e, b=b, t=t: e.dma_start(
                    out=XNT_SP.ap()[:, t * 128:(t + 1) * 128].rearrange("(d p) l -> p d l", p=128),
                    in_=xnT_sb[b][:, :].rearrange("p (d l) -> p d l", d=16)), inc="dsy", inck=16)
                tT_ack[t] = sack()
                for _ in range(4):
                    emit_cast_unit()

            W("gpsimd", "asy", tT_ack[7])
            EM("gpsimd", lambda e: e.collective_compute(
                "AllGather", OP.bypass, replica_groups=[list(range(NCORE))],
                ins=[ag_rn_in.ap().opt()], outs=[RNT_ALL.ap().opt()]), inc="cc_rn")
            while cu[0] < len(cast_srcs):
                emit_cast_unit()
            for s in range(4):
                W("gpsimd", f"sw{s}", p.c[f"sw{s}"])
            EM("gpsimd", lambda e: e.collective_compute(
                "AllGather", OP.bypass, replica_groups=[list(range(NCORE))],
                ins=[ag_wgu_in.ap().opt()], outs=[WGUT.ap().opt()]), inc="cc_wgu")
            p.barrier(dma_flush=[("gpsimd", "dgp"), ("sync", "dsy")])

        with contextlib.ExitStack() as ph:
            sb = lambda name, shape, dtype=F32: ph.enter_context(nc.sbuf_tensor(name, shape, dtype))
            psb = lambda name, shape: ph.enter_context(nc.psum_tensor(name, shape, F32))
            wstage = sb("wstage", [128, 16 * 512])
            inw_bf = sb("inw_bf", [128, 16 * 512], BF16)
            skw_bf = sb("skw_bf", [128, 16 * 512], BF16)
            wd1_bf = sb("wd1_bf", [128, NET * 128], BF16)
            wbc_bf = sb("wbc_bf", [128, NET * 32], BF16)
            rhs = [sb(f"rhs{i}", [128, 16 * 512], BF16) for i in range(2)]
            xin_buf = [sb(f"xinb{et}", [128, 515]) for et in range(NET)]
            acc = [sb(f"acc{et}", [128, 512]) for et in range(NET)]
            sk2_t = [[sb(f"sk2t{i}_{et}", [128, 512]) for et in range(NET)] for i in range(2)]
            d1e = [sb(f"d1e{i}", [128, 512]) for i in range(2)]
            bce = [sb(f"bce{i}", [128, 512]) for i in range(2)]
            x_bf = sb("x_bf", [128, NET * L], BF16)
            ps_xin = [psb(f"ps_xin{i}", [128, 512]) for i in range(2)]
            ps_skip = [psb(f"ps_skip{i}", [128, 512]) for i in range(2)]
            ps_d1 = psb("ps_d1", [128, 512])
            ps_bc = psb("ps_bc", [128, 512])

            def load_cast(src, w, sl, nfree):
                EM("gpsimd", lambda e, src=src, sl=sl, nfree=nfree: e.dma_start(
                    out=wstage[:, 0:nfree].rearrange("p (t c) -> p t c", t=sl),
                    in_=src.rearrange("(t p) c -> p t c", p=128)), inc="dgp", inck=16)
                a_ld = gack()
                W("vector", "agp", a_ld)
                cv = EM("vector", lambda e, w=w, nfree=nfree: e.tensor_copy(w[:, 0:nfree], wstage[:, 0:nfree]),
                        inc="vv")
                W("gpsimd", "vv", cv)

            load_cast(inwT, inw_bf, 16, 16 * 512)
            load_cast(skwT, skw_bf, 16, 16 * 512)
            load_cast(wd1T, wd1_bf, NET, NET * 128)
            load_cast(wbcT, wbc_bf, NET, NET * 32)
            for et in range(NET - 1):
                EM("vector", lambda e, et=et: e.memset(xin_buf[et][:, 0:3], 0.0))
            v_w = EM("vector", lambda e: e.memset(xin_buf[NET - 1][:, 0:3], 0.0), inc="vv")

            W("sync", "cc_rn", 1)
            NLC = 16
            mm_xin, mm_skip, evac_xin, evac_skip = {}, {}, {}, {}
            xsilu, sk2_ack = {}, {}
            rhs_cnt = {}

            def rhs_load(lc):
                s = lc % 2
                c, hh = lc // 2, lc % 2
                EM("sync", lambda e, s=s, c=c, hh=hh: e.dma_start(
                    out=rhs[s][:, :].rearrange("p (d l) -> p d l", d=16),
                    in_=RNT_ALL.ap()[c * D:(c + 1) * D, hh * 512:(hh + 1) * 512]
                    .rearrange("(d p) l -> p d l", p=128)), inc=f"fw{s}", inck=16)
                rhs_cnt[lc] = p.c[f"fw{s}"]

            d1mm, bcmm, d1ev, bcev, d1_ack = {}, {}, {}, {}, {}
            rhs_load(0)
            for lc in range(NLC):
                b = lc % 2
                if lc + 1 < NLC:
                    if lc >= 1:
                        W("sync", "mm", mm_skip[(lc - 1, NET - 1)])
                    rhs_load(lc + 1)
                W("tensor", f"fw{b}", rhs_cnt[lc])
                if lc == 0:
                    W("tensor", "vv", v_w)
                for et in range(NET):
                    pb = et % 2
                    prev = (lc, et - 2) if et >= 2 else (lc - 1, et + 2)
                    if prev in evac_xin:
                        W("tensor", "vv", evac_xin[prev])
                    for dt in range(16):
                        mm_xin[(lc, et)] = EM("tensor", lambda e, et=et, dt=dt, b=b, pb=pb: e.matmul(
                            ps_xin[pb][:, :], inw_bf[:, dt * 512 + et * 128:dt * 512 + (et + 1) * 128],
                            rhs[b][:, dt * 512:(dt + 1) * 512], start=(dt == 0), stop=(dt == 15)),
                            inc="mm" if dt == 15 else None)
                    if prev in evac_skip:
                        W("tensor", "aa", evac_skip[prev])
                    for dt in range(16):
                        mm_skip[(lc, et)] = EM("tensor", lambda e, et=et, dt=dt, b=b, pb=pb: e.matmul(
                            ps_skip[pb][:, :], skw_bf[:, dt * 512 + et * 128:dt * 512 + (et + 1) * 128],
                            rhs[b][:, dt * 512:(dt + 1) * 512], start=(dt == 0), stop=(dt == 15)),
                            inc="mm" if dt == 15 else None)

                    W("vector", "mm", mm_xin[(lc, et)])
                    if (lc - 1, et) in xsilu:
                        W("vector", "aa", xsilu[(lc - 1, et)])
                    evv = EM("vector", lambda e, et=et, pb=pb: e.tensor_copy(
                        xin_buf[et][:, 3:515], ps_xin[pb][:, :]), inc="vv")
                    W("vector", "vv", evv)
                    cv0 = EM("vector", lambda e, et=et: e.tensor_scalar(
                        acc[et][:, :], xin_buf[et][:, 0:512], ecol(et, 0), None, OP.mult), inc="vv")
                    for k in range(1, 4):
                        W("vector", "vv", cv0)
                        cv0 = EM("vector", lambda e, et=et, k=k: e.scalar_tensor_tensor(
                            acc[et][:, :], xin_buf[et][:, k:k + 512], ecol(et, k),
                            acc[et][:, :], OP.mult, OP.add), inc="vv")
                    evac_xin[(lc, et)] = cv0
                    EM("vector", lambda e, et=et: e.tensor_copy(
                        xin_buf[et][:, 0:3], xin_buf[et][:, 512:515]), inc="vv")
                    W("scalar", "vv", evac_xin[(lc, et)])
                    xsilu[(lc, et)] = EM("scalar", lambda e, et=et, lc=lc: e.activation(
                        x_bf[:, et * L + lc * 512:et * L + (lc + 1) * 512], acc[et][:, :], AF.Silu,
                        bias=ecol(et, 4)), inc="aa")
                    W("scalar", "mm", mm_skip[(lc, et)])
                    if (lc - 2, et) in sk2_ack:
                        W("scalar", "agp", sk2_ack[(lc - 2, et)])
                    evac_skip[(lc, et)] = EM("scalar", lambda e, et=et, pb=pb, b=b: e.activation(
                        sk2_t[b][et][:, :], ps_skip[pb][:, :], AF.Silu), inc="aa")
                    W("gpsimd", "aa", evac_skip[(lc, et)])
                    EM("gpsimd", lambda e, et=et, lc=lc, b=b: e.dma_start(
                        out=SK2_SP.ap()[et * 128:(et + 1) * 128, lc * 512:(lc + 1) * 512],
                        in_=sk2_t[b][et][:, :]), inc="dgp", inck=16)
                    sk2_ack[(lc, et)] = gack()

                W("tensor", "aa", xsilu[(lc, NET - 1)])
                if lc >= 1:
                    W("tensor", "vv", d1ev[lc - 1])
                for et in range(NET):
                    d1mm[lc] = EM("tensor", lambda e, et=et, lc=lc: e.matmul(
                        ps_d1[:, :], wd1_bf[:, et * 128:(et + 1) * 128],
                        x_bf[:, et * L + lc * 512:et * L + (lc + 1) * 512],
                        start=(et == 0), stop=(et == NET - 1)),
                        inc="mm" if et == NET - 1 else None)
                if lc >= 1:
                    W("tensor", "vv", bcev[lc - 1])
                for et in range(NET):
                    bcmm[lc] = EM("tensor", lambda e, et=et, lc=lc: e.matmul(
                        ps_bc[0:32, :], wbc_bf[:, et * 32:(et + 1) * 32],
                        x_bf[:, et * L + lc * 512:et * L + (lc + 1) * 512],
                        start=(et == 0), stop=(et == NET - 1)),
                        inc="mm" if et == NET - 1 else None)
                W("vector", "mm", d1mm[lc])
                if lc >= 2:
                    W("vector", "agp", d1_ack[lc - 2])
                d1ev[lc] = EM("vector", lambda e, b=b: e.tensor_copy(d1e[b][:, :], ps_d1[:, :]), inc="vv")
                W("vector", "mm", bcmm[lc])
                bcev[lc] = EM("vector", lambda e, b=b: e.tensor_copy(bce[b][0:32, :], ps_bc[0:32, :]),
                              inc="vv")
                W("gpsimd", "vv", bcev[lc])
                EM("gpsimd", lambda e, lc=lc, b=b: e.dma_start(
                    out=ar_in.ap()[0:R, lc * 512:(lc + 1) * 512], in_=d1e[b][:, :]), inc="dgp", inck=16)
                EM("gpsimd", lambda e, lc=lc, b=b: e.dma_start(
                    out=ar_in.ap()[R:R + 32, lc * 512:(lc + 1) * 512], in_=bce[b][0:32, :]), inc="dgp", inck=16)
                d1_ack[lc] = gack()

            x_ready = p.c["aa"]
            W("gpsimd", "dgp", p.c["dgp"])
            EM("gpsimd", lambda e: e.collective_compute(
                "AllReduce", OP.add, replica_groups=[list(range(NCORE))],
                ins=[ar_in.ap().opt()], outs=[ar_out.ap().opt()]), inc="cc_ar")
            EM("gpsimd", lambda e: e.collective_compute(
                "AllGather", OP.bypass, replica_groups=[list(range(NCORE))],
                ins=[ag_wdn_in.ap().opt()], outs=[WDNT.ap().opt()]), inc="cc_wdn")
            EM("gpsimd", lambda e: e.collective_compute(
                "AllGather", OP.bypass, replica_groups=[list(range(NCORE))],
                ins=[ag_ow_in.ap().opt()], outs=[OWT.ap().opt()]), inc="cc_ow")
            W("gpsimd", "aa", x_ready)
            for et in range(NET):
                EM("gpsimd", lambda e, et=et: e.dma_start(
                    out=X_SP.ap()[et * 128:(et + 1) * 128, :], in_=x_bf[:, et * L:(et + 1) * L]),
                    inc="dgp", inck=16)
            p.barrier(dma_flush=[("gpsimd", "dgp"), ("sync", "dsy")])

        with contextlib.ExitStack() as ph:
            sb = lambda name, shape, dtype=F32: ph.enter_context(nc.sbuf_tensor(name, shape, dtype))
            psb = lambda name, shape: ph.enter_context(nc.psum_tensor(name, shape, F32))
            d1t_f = sb("d1t_f", [128, CH])
            d1t_bf = sb("d1t_bf", [128, CH], BF16)
            bmcs = sb("bmcs", [32, CH])
            bmc_bf = sb("bmc_bf", [32, 2 * CH], BF16)
            wd2_bf = sb("wd2_bf", [128, EC], BF16)
            delta = sb("delta", [128, NET * CH])
            xc_bf = [sb(f"xc_bf{i}", [128, NET * CH], BF16) for i in range(2)]
            u_bf = sb("u_bf", [128, NET * CH], BF16)
            y_acc = sb("y_acc", [128, NET * CH])
            sk2c = [sb(f"sk2c{i}", [128, CH]) for i in range(2)]
            p_buf = sb("p_buf", [128, 2 * CH])
            ug_buf = sb("ug_buf", [128, 2 * CH], BF16)
            h_buf = sb("h_buf", [128, 2 * CH])
            prod = [sb(f"prod{i}", [128, CH]) for i in range(3)]
            brep = sb("brep", [128, 2 * CH], BF16)
            crep = sb("crep", [128, 2 * CH], BF16)
            t0 = sb("t0", [128, CH])
            t1 = sb("t1", [128, CH])
            y_bf = [sb(f"y_bf{i}", [128, CH], BF16) for i in range(2)]
            sel_bf = sb("sel_bf", [32, 32 * 128], BF16)
            ps_dp = psb("ps_dp", [128, CH])
            ps_b = psb("ps_b", [128, 512])
            ps_c = psb("ps_c", [128, 512])
            xnt = sb("xnt", [128, 16 * 512], BF16)
            wg_g = [sb(f"wg_g{i}", [128, 16 * 128], BF16) for i in range(3)]
            wg_u = [sb(f"wg_u{i}", [128, 16 * 128], BF16) for i in range(3)]
            sg = [sb(f"sg{i}", [128, 512]) for i in range(2)]
            h_st = [sb(f"h_st{i}", [128, 512], BF16) for i in range(2)]
            h_in = [sb(f"h_in{i}", [128, 2 * 512], BF16) for i in range(3)]
            wdn_t = [sb(f"wdn_t{i}", [128, 2 * 512], BF16) for i in range(3)]
            o2_t = [sb(f"o2_t{i}", [128, 512]) for i in range(2)]
            pf = [psb(f"pf{i}", [128, 512]) for i in range(4)]

            EM("gpsimd", lambda e: e.dma_start(out=sel_bf[:, :], in_=sel_in), inc="dgp", inck=16)
            EM("gpsimd", lambda e: e.dma_start(out=d1t_f[:, 0:EC], in_=wd2T), inc="dgp", inck=16)
            a_w = gack()
            W("vector", "agp", a_w)
            wcv = EM("vector", lambda e: e.tensor_copy(wd2_bf[:, :], d1t_f[:, 0:EC]), inc="vv")
            W("sync", "cc_ar", 1)
            W("sync", "vv", wcv)
            W("tensor", "agp", a_w)
            W("tensor", "vv", wcv)
            xnt_ack = [None]

            dp_mm, dp_chain, delta_ok, pexp = {}, {}, {}, {}
            brep_ok = {}
            ug_done, tts_done, prod_done, add_done = {}, {}, {}, {}
            stt_done, ymul_done, ybf_ack, sk_ack = {}, {}, {}, {}
            d1_cast, xc_ack = {}, {}
            prod_war = {}
            gu_mm, silu_done, hmult_done, hst_ack = {}, {}, {}, {}
            dn_mm, o2_evac, o2_ack = {}, {}, {}
            dn_state = {"gpr": 0}

            def gate_units(lh):
                if lh == 1:
                    W("sync", "mm", gu_mm[(0, 63)])
                for it in range(64):
                    if it == 0:
                        EM("sync", lambda e, lh=lh: e.dma_start(
                            out=xnt[:, :].rearrange("p (d l) -> p d l", d=16),
                            in_=XNT_SP.ap()[:, lh * 512:(lh + 1) * 512]
                            .rearrange("(d p) l -> p d l", p=128)), inc="dsy", inck=16)
                        xnt_ack[0] = sack()
                        if lh == 0 and it == 0:
                            pass
                        W("tensor", "asy", xnt_ack[0])
                    b = it % 3
                    b2 = it % 2
                    pg, pu = pf[2 * b2], pf[2 * b2 + 1]
                    prevk = (lh, it - 2) if it >= 2 else ((lh - 1, 62 + it) if lh >= 1 else None)
                    prevk3 = (lh, it - 3) if it >= 3 else ((lh - 1, 61 + it) if lh >= 1 else None)
                    if lh == 0 and it == 0:
                        W("scalar", "cc_wgu", 1)
                    if prevk3 in gu_mm:
                        W("scalar", "mm", gu_mm[prevk3])
                    EM("scalar", lambda e, it=it, b=b: e.dma_start(
                        out=wg_g[b][:, :],
                        in_=WGUT.ap()[it:it + 1, :].rearrange("r (p c) -> (r p) c", p=128)),
                        inc=f"fw{b}", inck=16)
                    EM("scalar", lambda e, it=it, b=b: e.dma_start(
                        out=wg_u[b][:, :],
                        in_=WGUT.ap()[64 + it:65 + it, :].rearrange("r (p c) -> (r p) c", p=128)),
                        inc=f"fw{b}", inck=16)
                    W("tensor", f"fw{b}", p.c[f"fw{b}"])
                    if prevk in silu_done:
                        W("tensor", "aa", silu_done[prevk])
                        W("tensor", "vv", hmult_done[prevk])
                    for dt in range(16):
                        EM("tensor", lambda e, dt=dt, b=b, pg=pg: e.matmul(
                            pg[:, :], wg_g[b][:, dt * 128:(dt + 1) * 128],
                            xnt[:, dt * 512:(dt + 1) * 512], start=(dt == 0), stop=(dt == 15)))
                    for dt in range(16):
                        gu_mm[(lh, it)] = EM("tensor", lambda e, dt=dt, b=b, pu=pu: e.matmul(
                            pu[:, :], wg_u[b][:, dt * 128:(dt + 1) * 128],
                            xnt[:, dt * 512:(dt + 1) * 512], start=(dt == 0), stop=(dt == 15)),
                            inc="mm" if dt == 15 else None)
                    W("scalar", "mm", gu_mm[(lh, it)])
                    if prevk in hmult_done:
                        W("scalar", "vv", hmult_done[prevk])
                    silu_done[(lh, it)] = EM("scalar", lambda e, b2=b2, pg=pg: e.activation(
                        sg[b2][:, :], pg[:, :], AF.Silu), inc="aa")
                    W("vector", "aa", silu_done[(lh, it)])
                    if prevk in hst_ack:
                        W("vector", "asy", hst_ack[prevk])
                    hmult_done[(lh, it)] = EM("vector", lambda e, b2=b2, pu=pu: e.tensor_mul(
                        h_st[b2][:, :], sg[b2][:, :], pu[:, :]), inc="vv")
                    W("sync", "vv", hmult_done[(lh, it)])
                    EM("sync", lambda e, it=it, b2=b2, lh=lh: e.dma_start(
                        out=H_SP.ap()[it * 128:(it + 1) * 128, lh * 512:(lh + 1) * 512],
                        in_=h_st[b2][:, :]), inc="dsy", inck=16)
                    hst_ack[(lh, it)] = sack()
                    yield

            def down_units():
                for lh in range(2):
                    for dc in range(4):
                        for pr in range(32):
                            it = 2 * pr
                            gpr = dn_state["gpr"]
                            dn_state["gpr"] += 1
                            b = gpr % 3
                            if (gpr - 3) in dn_mm:
                                W("scalar", "mm", dn_mm[gpr - 3])
                            if dc == 0 and pr == 0:
                                W("scalar", "cc_wdn", 1)
                                W("scalar", "asy", hst_ack[(lh, 63)])
                            EM("scalar", lambda e, it=it, dc=dc, b=b: e.dma_start(
                                out=wdn_t[b][:, :].rearrange("p (i c) -> p i c", i=2),
                                in_=WDNT.ap()[it * 128:(it + 2) * 128, dc * 512:(dc + 1) * 512]
                                .rearrange("(i p) c -> p i c", p=128)), inc=f"fw{b}", inck=16)
                            EM("scalar", lambda e, it=it, lh=lh, b=b: e.dma_start(
                                out=h_in[b][:, :].rearrange("p (i c) -> p i c", i=2),
                                in_=H_SP.ap()[it * 128:(it + 2) * 128, lh * 512:(lh + 1) * 512]
                                .rearrange("(i p) c -> p i c", p=128)), inc=f"fw{b}", inck=16)
                            W("tensor", f"fw{b}", p.c[f"fw{b}"])
                            if it == 0:
                                if lh == 0 and dc == 0:
                                    W("tensor", "vv", hmult_done[(1, 63)])
                                    W("tensor", "aa", silu_done[(1, 63)])
                                else:
                                    W("tensor", "aa", o2_evac[3])
                            for ii in range(2):
                                for lt2 in range(4):
                                    dn_mm[gpr] = EM(
                                        "tensor", lambda e, ii=ii, lt2=lt2, b=b, it=it: e.matmul(
                                            pf[lt2][:, :],
                                            h_in[b][:, ii * 512 + lt2 * 128:ii * 512 + (lt2 + 1) * 128],
                                            wdn_t[b][:, ii * 512:(ii + 1) * 512],
                                            start=(it + ii == 0), stop=(it + ii == 63)),
                                        inc="mm" if (ii == 1 and lt2 == 3) else None)
                            yield
                        W("scalar", "mm", dn_mm[dn_state["gpr"] - 1])
                        for lt2 in range(4):
                            ob = lt2 % 2
                            okey = (lt2 - 2) if lt2 >= 2 else (lt2 + 2)
                            if okey in o2_ack:
                                W("scalar", "asy", o2_ack[okey])
                            o2_evac[lt2] = EM("scalar", lambda e, lt2=lt2, ob=ob: e.activation(
                                o2_t[ob][:, :], pf[lt2][:, :], AF.Identity), inc="aa")
                            W("sync", "aa", o2_evac[lt2])
                            EM("sync", lambda e, lh=lh, dc=dc, lt2=lt2, ob=ob: e.dma_start(
                                out=OUT2_SP.ap()[lh * 512 + lt2 * 128:lh * 512 + (lt2 + 1) * 128,
                                                 dc * 512:(dc + 1) * 512], in_=o2_t[ob][:, :]),
                                inc="dsy", inck=16)
                            o2_ack[lt2] = sack()
                        yield

            import itertools as _it
            ffn_iter = _it.chain(gate_units(0), gate_units(1), down_units())
            ffn_left = [64 + 64 + 2 * 4 * 33]

            def pull():
                if ffn_left[0] > 0:
                    next(ffn_iter)
                    ffn_left[0] -= 1

            bmc_cp, bc_mm_last = {}, {}
            a_bmc_d, a_d1_d, a_xc_d = {}, {}, {}

            def chunk_loads(cnk):
                c0 = cnk * CH
                if cnk >= 1:
                    W("sync", "vv", bmc_cp[cnk - 1])
                EM("sync", lambda e, c0=c0: e.dma_start(
                    out=bmcs[:, :], in_=ar_out.ap()[R:R + 32, c0:c0 + CH]), inc="dsy", inck=16)
                a_bmc_d[cnk] = sack()
                if cnk >= 1:
                    W("sync", "vv", d1_cast[cnk - 1])
                EM("sync", lambda e, c0=c0: e.dma_start(
                    out=d1t_f[:, :], in_=ar_out.ap()[0:R, c0:c0 + CH]), inc="dsy", inck=16)
                a_d1_d[cnk] = sack()
                if cnk >= 2:
                    W("sync", "vv", stt_done[(cnk - 2, NET - 1)])
                EM("sync", lambda e, c0=c0, xb=cnk % 2: e.dma_start(
                    out=xc_bf[xb][:, :].rearrange("p (et l) -> p et l", et=NET),
                    in_=X_SP.ap()[:, c0:c0 + CH].rearrange("(et p) l -> p et l", p=128)),
                    inc="dsy", inck=16)
                a_xc_d[cnk] = sack()

            chunk_loads(0)
            giter = [0]
            for cnk in range(NCH):
                c0 = cnk * CH
                cb2 = cnk % 2
                xb = cnk % 2
                W("vector", "asy", a_bmc_d[cnk])
                if cnk >= 2:
                    W("vector", "mm", bc_mm_last[cnk - 2])
                bmc_cp[cnk] = EM("vector", lambda e, cb2=cb2: e.tensor_copy(
                    bmc_bf[:, cb2 * CH:(cb2 + 1) * CH], bmcs[:, :]), inc="vv")
                W("vector", "asy", a_d1_d[cnk])
                if cnk >= 1:
                    W("vector", "mm", dp_mm[(cnk - 1, NET - 1)])
                d1_cast[cnk] = EM("vector", lambda e: e.tensor_copy(d1t_bf[:, :], d1t_f[:, :]), inc="vv")
                W("tensor", "vv", d1_cast[cnk])
                for et in range(NET):
                    pk = (cnk, et - 1) if et >= 1 else (cnk - 1, NET - 1)
                    if pk in dp_chain:
                        W("tensor", "aa", dp_chain[pk])
                    for hh in range(2):
                        dp_mm[(cnk, et)] = EM("tensor", lambda e, et=et, hh=hh: e.matmul(
                            ps_dp[:, hh * 512:(hh + 1) * 512], wd2_bf[:, et * 128:(et + 1) * 128],
                            d1t_bf[:, hh * 512:(hh + 1) * 512], start=True, stop=True),
                            inc="mm" if hh == 1 else None)
                    W("scalar", "mm", dp_mm[(cnk, et)])
                    if pk in delta_ok:
                        W("scalar", "vv", delta_ok[pk])
                    EM("scalar", lambda e, et=et: e.activation(t0[:, :], ps_dp[:, :], AF.Relu, bias=ecol(et, 5)))
                    EM("scalar", lambda e, et=et: e.activation(t1[:, :], ps_dp[:, :], AF.Abs, bias=ecol(et, 5)))
                    EM("scalar", lambda e: e.activation(t1[:, :], t1[:, :], AF.Exp, scale=-1.0))
                    dp_chain[(cnk, et)] = EM("scalar", lambda e: e.activation(t1[:, :], t1[:, :], AF.Ln, bias=1.0),
                                             inc="aa")
                    W("vector", "aa", dp_chain[(cnk, et)])
                    if cnk >= 1:
                        W("vector", "aa", pexp[(cnk - 1, N - 1, et)])
                    delta_ok[(cnk, et)] = EM("vector", lambda e, et=et: e.tensor_add(
                        delta[:, et * CH:(et + 1) * CH], t0[:, :], t1[:, :]), inc="vv")
                W("vector", "asy", a_xc_d[cnk])
                for et in range(NET):
                    EM("vector", lambda e, et=et, xb=xb: e.tensor_mul(
                        u_bf[:, et * CH:(et + 1) * CH], delta[:, et * CH:(et + 1) * CH],
                        xc_bf[xb][:, et * CH:(et + 1) * CH]), inc="vv")

                pending_adds = []

                def flush_add():
                    s2, et2, pb2 = pending_adds.pop(0)
                    W("vector", "gg", prod_done[(cnk, s2)])
                    add_done[(cnk, s2)] = EM("vector", lambda e, et2=et2, pb2=pb2: e.tensor_add(
                        y_acc[:, et2 * CH:(et2 + 1) * CH], y_acc[:, et2 * CH:(et2 + 1) * CH],
                        prod[pb2][:, :]), inc="vv")
                    prod_war[pb2] = add_done[(cnk, s2)]

                for n in range(N):
                    nb = n % 2
                    gk = (cnk, n - 2) if n >= 2 else ((cnk - 1, N - 2 + n) if cnk >= 1 else None)
                    if gk is not None and gk in brep_ok:
                        s_last = (gk[1]) * NET + NET - 1
                        W("tensor", "vv", ug_done[(gk[0], s_last)])
                        W("tensor", "gg", prod_done[(gk[0], s_last)])
                    pk = (cnk, n - 1) if n >= 1 else ((cnk - 1, N - 1) if cnk >= 1 else None)
                    if pk in brep_ok:
                        W("tensor", "aa", brep_ok[pk][1])
                    if n == 0:
                        W("tensor", "vv", bmc_cp[cnk])
                    m1 = EM("tensor", lambda e, n=n, cb2=cb2: e.matmul(
                        ps_b[:, :], sel_bf[0:32, n * 128:(n + 1) * 128],
                        bmc_bf[0:32, cb2 * CH:cb2 * CH + 512], start=True, stop=True), inc="mm")
                    m2 = EM("tensor", lambda e, n=n, cb2=cb2: e.matmul(
                        ps_c[:, :], sel_bf[0:32, n * 128:(n + 1) * 128],
                        bmc_bf[0:32, cb2 * CH + 512:(cb2 + 1) * CH], start=True, stop=True), inc="mm")
                    W("scalar", "mm", m2)
                    if gk is not None and gk in brep_ok:
                        s_last = (gk[1]) * NET + NET - 1
                        W("scalar", "vv", ug_done[(gk[0], s_last)])
                        W("scalar", "gg", prod_done[(gk[0], s_last)])
                    eb0 = EM("scalar", lambda e, nb=nb: e.activation(
                        brep[:, nb * CH:nb * CH + 512], ps_b[:, :], AF.Identity), inc="aa")
                    eb1 = EM("scalar", lambda e, nb=nb: e.activation(
                        brep[:, nb * CH + 512:(nb + 1) * CH], ps_c[:, :], AF.Identity), inc="aa")
                    W("tensor", "aa", eb0)
                    m3 = EM("tensor", lambda e, n=n, cb2=cb2: e.matmul(
                        ps_b[:, :], sel_bf[0:32, (16 + n) * 128:(17 + n) * 128],
                        bmc_bf[0:32, cb2 * CH:cb2 * CH + 512], start=True, stop=True), inc="mm")
                    W("tensor", "aa", eb1)
                    m4 = EM("tensor", lambda e, n=n, cb2=cb2: e.matmul(
                        ps_c[:, :], sel_bf[0:32, (16 + n) * 128:(17 + n) * 128],
                        bmc_bf[0:32, cb2 * CH + 512:(cb2 + 1) * CH], start=True, stop=True), inc="mm")
                    bc_mm_last[cnk] = m4
                    W("scalar", "mm", m4)
                    EM("scalar", lambda e, nb=nb: e.activation(
                        crep[:, nb * CH:nb * CH + 512], ps_b[:, :], AF.Identity), inc="aa")
                    ec1 = EM("scalar", lambda e, nb=nb: e.activation(
                        crep[:, nb * CH + 512:(nb + 1) * CH], ps_c[:, :], AF.Identity), inc="aa")
                    brep_ok[(cnk, n)] = (eb1, ec1)

                    for et in range(NET):
                        s = n * NET + et
                        sb_ = s % 2
                        pb = s % 3
                        cidx = et * N + n
                        if n == 0 and et == 0:
                            W("scalar", "vv", delta_ok[(cnk, NET - 1)])
                        tk = (cnk, s - 2) if s >= 2 else ((cnk - 1, N * NET - 2 + s) if cnk >= 1 else None)
                        if tk in tts_done:
                            W("scalar", "vv", tts_done[tk])
                        pexp[(cnk, n, et)] = EM("scalar", lambda e, et=et, n=n, sb_=sb_: e.activation(
                            p_buf[:, sb_ * CH:(sb_ + 1) * CH], delta[:, et * CH:(et + 1) * CH], AF.Exp,
                            scale=asb[:, et * 16 + n:et * 16 + n + 1]), inc="aa")
                        W("vector", "aa", brep_ok[(cnk, n)][0])
                        ug_done[(cnk, s)] = EM("vector", lambda e, et=et, nb=nb, sb_=sb_: e.tensor_mul(
                            ug_buf[:, sb_ * CH:(sb_ + 1) * CH], u_bf[:, et * CH:(et + 1) * CH],
                            brep[:, nb * CH:(nb + 1) * CH]), inc="vv")
                        W("vector", "aa", pexp[(cnk, n, et)])
                        if tk in prod_done:
                            W("vector", "gg", prod_done[tk])
                        if cnk == 0:
                            tts_done[(cnk, s)] = EM("vector", lambda e, sb_=sb_: e.tensor_tensor_scan(
                                h_buf[:, sb_ * CH:(sb_ + 1) * CH], p_buf[:, sb_ * CH:(sb_ + 1) * CH],
                                ug_buf[:, sb_ * CH:(sb_ + 1) * CH], 0.0, OP.mult, OP.add), inc="vv")
                        else:
                            tts_done[(cnk, s)] = EM("vector", lambda e, sb_=sb_, cidx=cidx: e.tensor_tensor_scan(
                                h_buf[:, sb_ * CH:(sb_ + 1) * CH], p_buf[:, sb_ * CH:(sb_ + 1) * CH],
                                ug_buf[:, sb_ * CH:(sb_ + 1) * CH], carry[:, cidx:cidx + 1],
                                OP.mult, OP.add), inc="vv")
                        EM("vector", lambda e, sb_=sb_, cidx=cidx: e.tensor_copy(
                            carry[:, cidx:cidx + 1], h_buf[:, sb_ * CH + CH - 1:sb_ * CH + CH]))
                        W("gpsimd", "vv", tts_done[(cnk, s)])
                        W("gpsimd", "aa", brep_ok[(cnk, n)][1])
                        if n == 0:
                            prod_done[(cnk, s)] = EM("gpsimd", lambda e, et=et, nb=nb, sb_=sb_: e.tensor_mul(
                                y_acc[:, et * CH:(et + 1) * CH], h_buf[:, sb_ * CH:(sb_ + 1) * CH],
                                crep[:, nb * CH:(nb + 1) * CH]), inc="gg")
                        else:
                            if pb in prod_war:
                                W("gpsimd", "vv", prod_war[pb])
                            prod_done[(cnk, s)] = EM("gpsimd", lambda e, nb=nb, sb_=sb_, pb=pb: e.tensor_mul(
                                prod[pb][:, :], h_buf[:, sb_ * CH:(sb_ + 1) * CH],
                                crep[:, nb * CH:(nb + 1) * CH]), inc="gg")
                            pending_adds.append((s, et, pb))
                        if len(pending_adds) > 2:
                            flush_add()
                        if giter[0] >= 8:
                            pull()
                        if giter[0] % 64 == 48:
                            if cnk + 1 < NCH:
                                chunk_loads(cnk + 1)
                        giter[0] += 1
                while pending_adds:
                    flush_add()

                for et in range(NET):
                    yb = et % 2
                    skkey = (cnk, et - 2) if et >= 2 else ((cnk - 1, et + 2) if cnk >= 1 else None)
                    if skkey in ymul_done:
                        W("sync", "gg", ymul_done[skkey])
                    EM("sync", lambda e, et=et, c0=c0, yb=yb: e.dma_start(
                        out=sk2c[yb][:, :], in_=SK2_SP.ap()[et * 128:(et + 1) * 128, c0:c0 + CH]),
                        inc="dsy", inck=16)
                    sk_ack[(cnk, et)] = sack()
                    stt_done[(cnk, et)] = EM("vector", lambda e, et=et, xb=xb: e.scalar_tensor_tensor(
                        y_acc[:, et * CH:(et + 1) * CH], xc_bf[xb][:, et * CH:(et + 1) * CH], ecol(et, 6),
                        y_acc[:, et * CH:(et + 1) * CH], OP.mult, OP.add), inc="vv")
                    W("gpsimd", "vv", stt_done[(cnk, et)])
                    W("gpsimd", "asy", sk_ack[(cnk, et)])
                    if skkey in ybf_ack:
                        W("gpsimd", "asy", ybf_ack[skkey])
                    ymul_done[(cnk, et)] = EM("gpsimd", lambda e, et=et, yb=yb: e.tensor_mul(
                        y_bf[yb][:, :], y_acc[:, et * CH:(et + 1) * CH], sk2c[yb][:, :]), inc="gg")
                    W("sync", "gg", ymul_done[(cnk, et)])
                    EM("sync", lambda e, et=et, c0=c0, yb=yb: e.dma_start(
                        out=ag_y_in.ap()[et * 128:(et + 1) * 128, c0:c0 + CH], in_=y_bf[yb][:, :]),
                        inc="dsy", inck=16)
                    ybf_ack[(cnk, et)] = sack()

            while ffn_left[0] > 0:
                pull()

            W("gpsimd", "asy", ybf_ack[(NCH - 1, NET - 1)])
            EM("gpsimd", lambda e: e.collective_compute(
                "AllGather", OP.bypass, replica_groups=[list(range(NCORE))],
                ins=[ag_y_in.ap().opt()], outs=[Y_ALL.ap().opt()]), inc="cc_y")
            p.barrier(dma_flush=[("gpsimd", "dgp"), ("sync", "dsy"), ("scalar", "dsc")])

        with contextlib.ExitStack() as ph:
            sb = lambda name, shape, dtype=F32: ph.enter_context(nc.sbuf_tensor(name, shape, dtype))
            psb = lambda name, shape: ph.enter_context(nc.psum_tensor(name, shape, F32))
            y_sb = sb("y_sb", [128, 32 * LC], BF16)
            owt_t = [sb(f"owt_t{i}", [128, 512], BF16) for i in range(2)]
            xn_t4 = sb("xn_t4", [128, 512])
            o2_t4 = sb("o2_t4", [128, 512])
            g_t4 = sb("g_t4", [128, 512])
            rs_t4 = sb("rs_t4", [128, 512])
            s_t4 = sb("s_t4", [128, 512])
            out_t4 = sb("out_t4", [128, 512])
            ps_o = [psb(f"ps_o{i}", [128, 512]) for i in range(8)]

            W("sync", "cc_y", 1)
            W("sync", "cc_ow", 1)

            def y_load(e):
                for arm in e.switch_core_id(n=128):
                    lcore = physical_to_logical_core_id(int(arm) % 128, 0, container_num_cores=NCORE) % NCORE
                    c0 = lcore * LC
                    e.dma_start(
                        out=y_sb[:, :].rearrange("p (et l) -> p et l", et=32),
                        in_=Y_ALL.ap()[:, c0:c0 + LC].rearrange("(et p) l -> p et l", p=128),
                    ).then_inc(p.sems["dsy"], 16)
            p.c["dsy"] += 16
            p.prog["sync"].append(y_load)
            yl_ack = sack()

            ow_mm, ph3_evac, od_ack = {}, {}, {}
            for dc in range(4):
                for et8 in range(32):
                    b = et8 % 2
                    qw = "sync" if b == 0 else "gpsimd"
                    if (dc, et8 - 2) in ow_mm:
                        W(qw, "mm", ow_mm[(dc, et8 - 2)])
                    elif dc > 0 and et8 < 2:
                        W(qw, "mm", ow_mm[(dc - 1, 30 + et8)])
                    q, qs, ak = ("sync", "dsy", "asy") if b == 0 else ("gpsimd", "dgp", "agp")
                    EM(q, lambda e, et8=et8, dc=dc, b=b: e.dma_start(
                        out=owt_t[b][:, :], in_=OWT.ap()[et8 * 128:(et8 + 1) * 128, dc * 512:(dc + 1) * 512]),
                        inc=qs, inck=16)
                    a_ow = p.ack(q, qs, ak)
                    W("tensor", ak, a_ow)
                    W("tensor", "asy", yl_ack)
                    if dc > 0 and et8 == 0:
                        W("tensor", "vv", ph3_evac[dc - 1])
                    for lt in range(8):
                        ow_mm[(dc, et8)] = EM("tensor", lambda e, et8=et8, b=b, lt=lt: e.matmul(
                            ps_o[lt][:, :], y_sb[:, et8 * LC + lt * 128:et8 * LC + (lt + 1) * 128],
                            owt_t[b][:, :], start=(et8 == 0), stop=(et8 == 31)),
                            inc="mm" if lt == 7 else None)
                last_ov = None
                for lt in range(8):
                    prev_ov = last_ov if lt > 0 else (ph3_evac.get(dc - 1) if dc > 0 else None)
                    if prev_ov:
                        W("gpsimd", "vv", prev_ov)
                    W("gpsimd", "mm", ow_mm[(dc, 31)])
                    EM("gpsimd", lambda e, lt=lt, dc=dc: e.dma_start(
                        out=xn_t4[:, :], in_=XN_SP.ap()[lt * 128:(lt + 1) * 128, dc * 512:(dc + 1) * 512]),
                        inc="dgp", inck=16)
                    EM("gpsimd", lambda e, lt=lt, dc=dc: e.dma_start(
                        out=o2_t4[:, :], in_=OUT2_SP.ap()[lt * 128:(lt + 1) * 128, dc * 512:(dc + 1) * 512]),
                        inc="dgp", inck=16)
                    EM("gpsimd", lambda e, lt=lt, dc=dc: e.dma_start(
                        out=rs_t4[:, :], in_=resid[lt * 128:(lt + 1) * 128, dc * 512:(dc + 1) * 512]),
                        inc="dgp", inck=16)
                    a_ld3 = gack()
                    W("scalar", "agp", a_ld3)
                    sg4 = EM("scalar", lambda e: e.activation(g_t4[:, :], o2_t4[:, :], AF.Silu), inc="aa")
                    W("vector", "agp", a_ld3)
                    W("vector", "aa", sg4)
                    W("vector", "mm", ow_mm[(dc, 31)])
                    prev_od = od_ack.get((dc, lt - 1)) or (od_ack.get((dc - 1, 7)) if dc > 0 else None)
                    if prev_od:
                        W("vector", "agp", prev_od)
                    EM("vector", lambda e, lt=lt: e.tensor_add(s_t4[:, :], xn_t4[:, :], ps_o[lt][:, :]))
                    EM("vector", lambda e: e.tensor_mul(s_t4[:, :], s_t4[:, :], g_t4[:, :]))
                    last_ov = EM("vector", lambda e: e.tensor_add(out_t4[:, :], rs_t4[:, :], s_t4[:, :]), inc="vv")
                    W("gpsimd", "vv", last_ov)
                    EM("gpsimd", lambda e, lt=lt, dc=dc: e.dma_start(
                        out=out[lt * 128:(lt + 1) * 128, dc * 512:(dc + 1) * 512], in_=out_t4[:, :]),
                        inc="dgp", inck=16)
                    od_ack[(dc, lt)] = gack()
                ph3_evac[dc] = last_ov

            if debug:
                for src, dst in [(XN_SP, dbg["dbg_xn"]), (ag_rn_in, dbg["dbg_rn"]),
                                 (ar_out, dbg["dbg_d1bc"]), (SK2_SP, dbg["dbg_sk2"]),
                                 (ag_y_in, dbg["dbg_y"]), (OUT2_SP, dbg["dbg_out2"]),
                                 (X_SP, dbg["dbg_x"])]:
                    EM("gpsimd", lambda e, src=src, dst=dst: e.dma_start(out=dst, in_=src.ap()),
                       inc="dgp", inck=16)
            W("gpsimd", "dgp", p.c["dgp"])
            p.barrier(dma_flush=[("gpsimd", "dgp"), ("sync", "dsy")])

        return _finish(nc, p)


def _finish(nc, p):
    with nc.Block() as block:
        @block.sync
        def _(e):
            for fn in p.prog["sync"]:
                fn(e)

        @block.gpsimd
        def _(e):
            for fn in p.prog["gpsimd"]:
                fn(e)

        @block.vector
        def _(e):
            for fn in p.prog["vector"]:
                fn(e)

        @block.scalar
        def _(e):
            for fn in p.prog["scalar"]:
                fn(e)

        @block.tensor
        def _(e):
            for fn in p.prog["tensor"]:
                fn(e)
    return nc


def make_in_maps(inputs):
    x0 = np.asarray(inputs["L0_attn_resid"]).reshape(L, D)
    norm_w = np.asarray(inputs["L0_attn_norm_w"])
    skip_w = np.asarray(inputs["L0_attn_skip_w"])
    in_w = np.asarray(inputs["L0_attn_in_w"])
    conv_w = np.asarray(inputs["L0_attn_conv_w"])
    conv_b = np.asarray(inputs["L0_attn_conv_b"])
    wd1 = np.asarray(inputs["L0_attn_wd1"])
    wd2 = np.asarray(inputs["L0_attn_wd2"])
    wd2_b = np.asarray(inputs["L0_attn_wd2_b"])
    wB = np.asarray(inputs["L0_attn_wB"])
    wC = np.asarray(inputs["L0_attn_wC"])
    A_log = np.asarray(inputs["L0_attn_A_log"])
    W_D = np.asarray(inputs["L0_attn_W_D"])
    out_w = np.asarray(inputs["L0_attn_out_w"])
    wgu = np.asarray(inputs["L0_ffn_w_gate_up"])
    wdn = np.asarray(inputs["L0_ffn_w_down"])

    nw_rep = np.ascontiguousarray(np.broadcast_to(norm_w[None, :], (128, D)).astype(np.float32))
    wguT = np.ascontiguousarray(wgu.T)
    gtile = wguT[:, :I].reshape(16, 128, 64, 128).transpose(2, 1, 0, 3).reshape(64, 262144)
    utile = wguT[:, I:].reshape(16, 128, 64, 128).transpose(2, 1, 0, 3).reshape(64, 262144)
    wgut_tiled = np.ascontiguousarray(np.concatenate([gtile, utile], 0))
    wdnT = np.ascontiguousarray(wdn.T)
    owT = np.ascontiguousarray(out_w.T)
    sel = np.zeros((32, 32 * 128), np.float32)
    for k in range(32):
        sel[k, k * 128:(k + 1) * 128] = 1.0
    sel_bf = sel.astype(ml_dtypes.bfloat16)

    in_maps = []
    for i in range(NCORE):
        ls = slice(i * LC, (i + 1) * LC)
        es = slice(i * EC, (i + 1) * EC)
        ev = np.zeros((EC, 24), np.float32)
        ev[:, 0:4] = conv_w[es]
        ev[:, 4] = conv_b[es]
        ev[:, 5] = wd2_b[es]
        ev[:, 6] = W_D[es]
        ev[:, 8:24] = A_log[es]
        in_maps.append({
            "resid": np.ascontiguousarray(x0[ls]),
            "nw_rep": nw_rep,
            "inwT": np.ascontiguousarray(in_w[es].T),
            "skwT": np.ascontiguousarray(skip_w[es].T),
            "wd1T": np.ascontiguousarray(wd1[:, es].T),
            "wd2T": np.ascontiguousarray(wd2[es].T),
            "wbcT": np.ascontiguousarray(np.concatenate([wB[:, es].T, wC[:, es].T], axis=1)),
            "evec": ev,
            "sel_in": sel_bf,
            "wguT_sl": np.ascontiguousarray(
                wgut_tiled[i * 16:(i + 1) * 16].reshape(128, 32768)),
            "wdnT_sl": np.ascontiguousarray(wdnT[i * (I // NCORE):(i + 1) * (I // NCORE)]),
            "owT_sl": np.ascontiguousarray(owT[i * (E // NCORE):(i + 1) * (E // NCORE)]),
        })
    return in_maps


def kernel(**inputs):
    in_maps = make_in_maps(inputs)
    nc = build_nc()
    res = run_bass_kernel_spmd(nc, in_maps, core_ids=list(range(NCORE)))
    outp = np.concatenate([res.results[i]["out"] for i in range(NCORE)], axis=0)
    return outp.reshape(1, L, D).astype(np.float32)


# version 09
# speedup vs baseline: 1.1430x; 1.1430x over previous
import contextlib
import numpy as np
import ml_dtypes
import concourse.bass as bass
import concourse.mybir as mybir
from concourse.bass_utils import run_bass_kernel_spmd
from concourse.libnrt import physical_to_logical_core_id

F32 = mybir.dt.float32
BF16 = mybir.dt.bfloat16
AF = mybir.ActivationFunctionType
OP = mybir.AluOpType

NCORE = 8
L = 8192
D = 2048
E = 4096
N = 16
R = 128
I = 8192
LC = L // NCORE
EC = E // NCORE
NET = EC // 128
CH = 1024
NCH = L // CH
EPS = 1e-5

ENGINES = ("sync", "gpsimd", "vector", "scalar", "tensor")


class P:

    def __init__(self, nc, stack, semnames):
        self.nc = nc
        self.prog = {e: [] for e in ENGINES}
        self.sems = {s: stack.enter_context(nc.semaphore(s)) for s in semnames}
        self.c = {s: 0 for s in semnames}
        self._last = {}
        self.nbar = 0

    def wait(self, eng, sem, val):
        if val is None or val <= 0:
            return
        self.prog[eng].append(lambda e, s=sem, v=val: e.wait_ge(self.sems[s], v))

    CHAIN = {"vector": "vv", "scalar": "aa", "gpsimd": "gg"}

    def emit(self, eng, builder, inc=None, inck=1):
        chain_sem = self.CHAIN.get(eng)
        dma_like = (inc in ("dgp", "dsy", "dsc") or (inc or "").startswith("cc")
                    or (inc or "").startswith("fw") or (inc or "").startswith("sw"))
        if chain_sem is not None and not dma_like:
            self.wait(eng, chain_sem, self._last.get(eng, 0))
            if inc is None:
                inc, inck = chain_sem, 1
            assert inc == chain_sem, (eng, inc)
        if inc is not None:
            self.c[inc] += inck
            v = self.c[inc]

            def fn(e, b=builder, s=inc, k=inck):
                b(e).then_inc(self.sems[s], k)
            self.prog[eng].append(fn)
            if chain_sem is not None and not dma_like:
                self._last[eng] = v
            return v
        self.prog[eng].append(lambda e, b=builder: b(e))
        return None

    def ack(self, eng, qsem, asem):
        self.wait(eng, qsem, self.c[qsem])
        self.c[asem] += 1
        v = self.c[asem]
        self.prog[eng].append(lambda e, s=asem: e.sem_inc(self.sems[s], 1))
        return v

    def barrier(self, dma_flush=()):
        self.nbar += 1
        target = 5 * self.nbar
        flush = {}
        for eng, s in dma_flush:
            flush.setdefault(eng, []).append((s, self.c[s]))
        for eng in ENGINES:
            for s, v in flush.get(eng, []):
                self.wait(eng, s, v)
            self.prog[eng].append(lambda e: e.sem_inc(self.sems["bar"], 1))
            self.wait(eng, "bar", target)


def build_nc(debug=False, preamble="clear_sled"):
    nc = bass.Bass("TRN2", target_bir_lowering=False)

    def din(name, shape, dtype=F32):
        return nc.dram_tensor(name, shape, dtype, kind="ExternalInput").ap()

    resid = din("resid", [LC, D])
    nw_rep = din("nw_rep", [128, D])
    inwT = din("inwT", [D, EC])
    skwT = din("skwT", [D, EC])
    wd1T = din("wd1T", [EC, R])
    wd2T = din("wd2T", [R, EC])
    wbcT = din("wbcT", [EC, 32])
    evec = din("evec", [EC, 24])
    sel_in = din("sel_in", [32, 32 * 128], BF16)
    wguT_sl = din("wguT_sl", [128, 32768])
    wdnT_sl = din("wdnT_sl", [I // NCORE, D])
    owT_sl = din("owT_sl", [E // NCORE, D])
    out = nc.dram_tensor("out", [LC, D], F32, kind="ExternalOutput").ap()

    ag_rn_in = nc.dram_tensor("ag_rn_in", [D, LC], BF16)
    RNT_ALL = nc.dram_tensor("RNT_ALL", [NCORE * D, LC], BF16, addr_space="Shared")
    XNT_SP = nc.dram_tensor("XNT_SP", [D, LC], BF16)
    ag_wgu_in = nc.dram_tensor("ag_wgu_in", [128, 32768], BF16)
    WGUT = nc.dram_tensor("WGUT", [128, 262144], BF16, addr_space="Shared")
    ag_wdn_in = nc.dram_tensor("ag_wdn_in", [I // NCORE, D], BF16)
    WDNT = nc.dram_tensor("WDNT", [I, D], BF16, addr_space="Shared")
    ag_ow_in = nc.dram_tensor("ag_ow_in", [E // NCORE, D], BF16)
    OWT = nc.dram_tensor("OWT", [E, D], BF16, addr_space="Shared")
    ar_in = nc.dram_tensor("ar_in", [R + 32, L], F32)
    ar_out = nc.dram_tensor("ar_out", [R + 32, L], F32, addr_space="Shared")
    ag_y_in = nc.dram_tensor("ag_y_in", [EC, L], BF16)
    Y_ALL = nc.dram_tensor("Y_ALL", [E, L], BF16, addr_space="Shared")
    XN_SP = nc.dram_tensor("XN_SP", [LC, D], F32)
    SK2_SP = nc.dram_tensor("SK2_SP", [EC, L], BF16)
    OUT2_SP = nc.dram_tensor("OUT2_SP", [LC, D], F32)
    X_SP = nc.dram_tensor("X_SP", [EC, L], BF16)
    H_SP = nc.dram_tensor("H_SP", [I, LC], BF16)

    dbg = {}
    if debug:
        for nm, shape, dt in [
            ("dbg_xn", [LC, D], F32), ("dbg_rn", [LC, D], BF16),
            ("dbg_d1bc", [R + 32, L], F32), ("dbg_x", [EC, L], BF16),
            ("dbg_sk2", [EC, L], F32), ("dbg_delta", [EC, L], F32),
            ("dbg_y", [EC, L], BF16), ("dbg_out2", [LC, D], F32),
            ("dbg_scan", [128, 6 * CH], F32),
        ]:
            dbg[nm] = nc.dram_tensor(nm, shape, dt, kind="ExternalOutput").ap()

    semnames = ["bar", "dsy", "dgp", "dsc", "asy", "agp", "asc", "mm", "vv", "aa", "gg",
                "fw0", "fw1", "fw2", "fw3", "sw0", "sw1", "sw2", "sw3",
                "cc_rn", "cc_wgu", "cc_wdn", "cc_ow", "cc_ar", "cc_y"]

    with contextlib.ExitStack() as g:
        p = P(nc, g, semnames)
        W, EM = p.wait, p.emit

        def gack():
            return p.ack("gpsimd", "dgp", "agp")

        def sack():
            return p.ack("sync", "dsy", "asy")

        nums = sorted(s.num for s in p.sems.values())
        assert nums == list(range(nums[0], nums[0] + len(nums))), nums
        if preamble == "clear_sled":
            for _ in range(40):
                nc.gpsimd.nop(cycle_cnt=2400)
            nc.gpsimd.sem_clear(range(nums[0], nums[-1] + 1))
            for eng in (nc.sync, nc.vector, nc.scalar, nc.tensor):
                for _ in range(120):
                    eng.nop(cycle_cnt=2400)
            nc.all_engine_barrier()

        sbg = lambda name, shape, dtype=F32: g.enter_context(nc.sbuf_tensor(name, shape, dtype))
        evs = sbg("evs", [128, NET * 24])
        asb = sbg("asb", [128, NET * 16])
        carry = sbg("carry", [128, NET * N])

        def ecol(et, c):
            return evs[:, et * 24 + c:et * 24 + c + 1]

        with contextlib.ExitStack() as ph:
            sb = lambda name, shape, dtype=F32: ph.enter_context(nc.sbuf_tensor(name, shape, dtype))
            nwr = sb("nwr", [128, D])
            rsd = [sb(f"rsd{i}", [128, D]) for i in range(2)]
            sq_scr = sb("sq_scr", [128, D])
            xn_t = [sb(f"xn_t{i}", [128, D]) for i in range(2)]
            rn_bf = [sb(f"rn_bf{i}", [128, D], BF16) for i in range(2)]
            xn_bf = [sb(f"xn_bf{i}", [128, D], BF16) for i in range(2)]
            rnT_sb = [sb(f"rnT_sb{i}", [128, D], BF16) for i in range(2)]
            xnT_sb = [sb(f"xnT_sb{i}", [128, D], BF16) for i in range(2)]
            idn = sb("idn", [128, 128], BF16)
            stat = sb("stat", [128, 8])
            wf32 = sb("wf32", [128, 4 * 2048])
            wbf = sb("wbf", [128, 4 * 2048], BF16)
            pt = [ph.enter_context(nc.psum_tensor(f"pt{i}", [128, 128], BF16)) for i in range(4)]

            EM("gpsimd", lambda e: e.dma_start(out=nwr[:, :], in_=nw_rep), inc="dgp", inck=16)
            EM("gpsimd", lambda e: e.dma_start(
                out=evs[:, :].rearrange("p (t c) -> p t c", t=NET),
                in_=evec.rearrange("(t p) c -> p t c", p=128)), inc="dgp", inck=16)
            a_ev = gack()
            EM("gpsimd", lambda e: e.memset(idn[:, :], 0.0), inc="gg")
            idv = EM("gpsimd", lambda e: e.affine_select(
                out=idn[:, :], in_=idn[:, :], compare_op=OP.not_equal, fill=1.0,
                base=0, pattern=[[-1, 128]], channel_multiplier=1), inc="gg")
            W("scalar", "agp", a_ev)
            EM("scalar", lambda e: e.activation(
                asb[:, :].rearrange("p (t n) -> p t n", t=NET),
                evs[:, :].rearrange("p (t c) -> p t c", t=NET)[:, :, 8:24], AF.Exp), inc="aa")
            W("vector", "aa", p.c["aa"])
            EM("vector", lambda e: e.tensor_scalar_mul(asb[:, :], asb[:, :], -1.0), inc="vv")

            cast_srcs = []
            for k in range(16):
                cast_srcs.append((wguT_sl, ag_wgu_in, 0, k * 2048))
            for rt in range(8):
                cast_srcs.append((wdnT_sl, ag_wdn_in, rt * 128, 0))
            for rt in range(4):
                cast_srcs.append((owT_sl, ag_ow_in, rt * 128, 0))
            castv, swc = {}, {}
            cu = [0]

            def emit_cast_unit():
                k = cu[0]
                if k >= len(cast_srcs):
                    return
                cu[0] += 1
                s = k % 4
                src_t, dst_t, r0, c0 = cast_srcs[k]
                if (k - 4) in castv:
                    W("gpsimd", "vv", castv[k - 4])
                EM("gpsimd", lambda e, src_t=src_t, r0=r0, c0=c0, s=s: e.dma_start(
                    out=wf32[:, s * 2048:(s + 1) * 2048],
                    in_=src_t[r0:r0 + 128, c0:c0 + 2048]), inc=f"fw{s}", inck=16)
                fcnt = p.c[f"fw{s}"]
                W("vector", f"fw{s}", fcnt)
                if (k - 4) in swc:
                    W("vector", f"sw{s}", swc[k - 4])
                castv[k] = EM("vector", lambda e, s=s: e.tensor_copy(
                    wbf[:, s * 2048:(s + 1) * 2048], wf32[:, s * 2048:(s + 1) * 2048]), inc="vv")
                W("sync", "vv", castv[k])
                EM("sync", lambda e, dst_t=dst_t, r0=r0, c0=c0, s=s: e.dma_start(
                    out=dst_t.ap()[r0:r0 + 128, c0:c0 + 2048],
                    in_=wbf[:, s * 2048:(s + 1) * 2048]), inc=f"sw{s}", inck=16)
                swc[k] = p.c[f"sw{s}"]

            rsd_ack = {}
            for t in range(2):
                EM("sync", lambda e, t=t: e.dma_start(
                    out=rsd[t % 2][:, :], in_=resid[t * 128:(t + 1) * 128, :]), inc="dsy", inck=16)
                rsd_ack[t] = sack()

            tT_ack = {}
            xnf = {}
            for t in range(8):
                b = t % 2
                W("vector", "asy", rsd_ack[t])
                rs1 = EM("vector", lambda e, b=b: e.reduce_sum(
                    stat[:, 0:1], rsd[b][:, :], axis=mybir.AxisListType.X), inc="vv")
                W("vector", "vv", rs1)
                EM("vector", lambda e: e.tensor_scalar_mul(stat[:, 0:1], stat[:, 0:1], 1.0 / D))
                sqv = EM("vector", lambda e, b=b: e.tensor_mul(sq_scr[:, :], rsd[b][:, :], rsd[b][:, :]),
                         inc="vv")
                W("vector", "vv", sqv)
                rs2 = EM("vector", lambda e: e.reduce_sum(
                    stat[:, 1:2], sq_scr[:, :], axis=mybir.AxisListType.X), inc="vv")
                W("vector", "vv", rs2)
                EM("vector", lambda e: e.tensor_scalar_mul(stat[:, 1:2], stat[:, 1:2], 1.0 / D))
                EM("vector", lambda e: e.tensor_mul(stat[:, 7:8], stat[:, 0:1], stat[:, 0:1]))
                EM("vector", lambda e: e.tensor_sub(stat[:, 2:3], stat[:, 1:2], stat[:, 7:8]))
                vv1 = EM("vector", lambda e: e.tensor_scalar(
                    stat[:, 3:4], stat[:, 2:3], 1.0, EPS, OP.mult, OP.add), inc="vv")
                W("scalar", "vv", vv1)
                EM("scalar", lambda e: e.activation(stat[:, 4:5], stat[:, 3:4], AF.Ln))
                aa1 = EM("scalar", lambda e: e.activation(stat[:, 5:6], stat[:, 4:5], AF.Exp, scale=-0.5),
                         inc="aa")
                aa1b = EM("scalar", lambda e: e.activation(stat[:, 6:7], stat[:, 4:5], AF.Exp, scale=-1.0),
                          inc="aa")
                W("vector", "aa", aa1b)
                if t >= 2:
                    W("vector", "asy", xnf[t - 2])
                xnv = EM("vector", lambda e, b=b: e.tensor_scalar(
                    xn_t[b][:, :], rsd[b][:, :], stat[:, 0:1], stat[:, 5:6], OP.subtract, OP.mult), inc="vv")
                if t + 2 < 8:
                    W("sync", "vv", xnv)
                    EM("sync", lambda e, t=t: e.dma_start(
                        out=rsd[t % 2][:, :], in_=resid[(t + 2) * 128:(t + 3) * 128, :]), inc="dsy", inck=16)
                    rsd_ack[t + 2] = sack()
                EM("vector", lambda e: e.tensor_mul(stat[:, 6:7], stat[:, 2:3], stat[:, 6:7]))
                vv2 = EM("vector", lambda e: e.tensor_scalar(
                    stat[:, 6:7], stat[:, 6:7], 1.0, EPS, OP.mult, OP.add), inc="vv")
                W("scalar", "vv", vv2)
                EM("scalar", lambda e: e.activation(stat[:, 6:7], stat[:, 6:7], AF.Ln))
                aa2 = EM("scalar", lambda e: e.activation(stat[:, 6:7], stat[:, 6:7], AF.Exp, scale=-0.5),
                         inc="aa")
                W("vector", "aa", aa2)
                rnv = EM("vector", lambda e, b=b: e.scalar_tensor_tensor(
                    rn_bf[b][:, :], xn_t[b][:, :], stat[:, 6:7], nwr[:, :], OP.mult, OP.mult), inc="vv")
                vd = EM("vector", lambda e, b=b: e.tensor_copy(xn_bf[b][:, :], xn_t[b][:, :]), inc="vv")
                W("sync", "vv", vd)
                EM("sync", lambda e, b=b, t=t: e.dma_start(
                    out=XN_SP.ap()[t * 128:(t + 1) * 128, :], in_=xn_t[b][:, :]), inc="dsy", inck=16)
                xnf[t] = sack()
                W("tensor", "vv", vd)
                if t >= 2:
                    W("tensor", "asy", tT_ack[t - 2])
                tcp = {}
                if t == 0:
                    W("tensor", "gg", idv)
                for half, (srcb, dstb) in enumerate([(rn_bf[b], rnT_sb[b]), (xn_bf[b], xnT_sb[b])]):
                    for dt in range(16):
                        j = half * 16 + dt
                        pj = j % 4
                        if (j - 4) in tcp:
                            W("tensor", "aa", tcp[j - 4])
                        tmm = EM("tensor", lambda e, srcb=srcb, dt=dt, pj=pj: e.transpose(
                            pt[pj][:, :], srcb[:, dt * 128:(dt + 1) * 128], idn[:, :]), inc="mm")
                        W("scalar", "mm", tmm)
                        tcp[j] = EM("scalar", lambda e, dstb=dstb, dt=dt, pj=pj: e.activation(
                            dstb[:, dt * 128:(dt + 1) * 128], pt[pj][:, :], AF.Identity), inc="aa")
                W("sync", "aa", tcp[31])
                EM("sync", lambda e, b=b, t=t: e.dma_start(
                    out=ag_rn_in.ap()[:, t * 128:(t + 1) * 128].rearrange("(d p) l -> p d l", p=128),
                    in_=rnT_sb[b][:, :].rearrange("p (d l) -> p d l", d=16)), inc="dsy", inck=16)
                EM("sync", lambda e, b=b, t=t: e.dma_start(
                    out=XNT_SP.ap()[:, t * 128:(t + 1) * 128].rearrange("(d p) l -> p d l", p=128),
                    in_=xnT_sb[b][:, :].rearrange("p (d l) -> p d l", d=16)), inc="dsy", inck=16)
                tT_ack[t] = sack()
                for _ in range(4):
                    emit_cast_unit()

            W("gpsimd", "asy", tT_ack[7])
            EM("gpsimd", lambda e: e.collective_compute(
                "AllGather", OP.bypass, replica_groups=[list(range(NCORE))],
                ins=[ag_rn_in.ap().opt()], outs=[RNT_ALL.ap().opt()]), inc="cc_rn")
            while cu[0] < len(cast_srcs):
                emit_cast_unit()
            for s in range(4):
                W("gpsimd", f"sw{s}", p.c[f"sw{s}"])
            EM("gpsimd", lambda e: e.collective_compute(
                "AllGather", OP.bypass, replica_groups=[list(range(NCORE))],
                ins=[ag_wgu_in.ap().opt()], outs=[WGUT.ap().opt()]), inc="cc_wgu")
            p.barrier(dma_flush=[("gpsimd", "dgp"), ("sync", "dsy")])

        with contextlib.ExitStack() as ph:
            sb = lambda name, shape, dtype=F32: ph.enter_context(nc.sbuf_tensor(name, shape, dtype))
            psb = lambda name, shape: ph.enter_context(nc.psum_tensor(name, shape, F32))
            wstage = sb("wstage", [128, 16 * 512])
            inw_bf = sb("inw_bf", [128, 16 * 512], BF16)
            skw_bf = sb("skw_bf", [128, 16 * 512], BF16)
            wd1_bf = sb("wd1_bf", [128, NET * 128], BF16)
            wbc_bf = sb("wbc_bf", [128, NET * 32], BF16)
            rhs = [sb(f"rhs{i}", [128, 16 * 512], BF16) for i in range(2)]
            xin_buf = [sb(f"xinb{et}", [128, 515]) for et in range(NET)]
            acc = [sb(f"acc{et}", [128, 512]) for et in range(NET)]
            sk2_t = [[sb(f"sk2t{i}_{et}", [128, 512], BF16) for et in range(NET)] for i in range(2)]
            d1e = [sb(f"d1e{i}", [128, 512]) for i in range(2)]
            bce = [sb(f"bce{i}", [128, 512]) for i in range(2)]
            x_bf = sb("x_bf", [128, NET * L], BF16)
            ps_xin = [psb(f"ps_xin{i}", [128, 512]) for i in range(2)]
            ps_skip = [psb(f"ps_skip{i}", [128, 512]) for i in range(2)]
            ps_d1 = psb("ps_d1", [128, 512])
            ps_bc = psb("ps_bc", [128, 512])

            def load_cast(src, w, sl, nfree):
                EM("gpsimd", lambda e, src=src, sl=sl, nfree=nfree: e.dma_start(
                    out=wstage[:, 0:nfree].rearrange("p (t c) -> p t c", t=sl),
                    in_=src.rearrange("(t p) c -> p t c", p=128)), inc="dgp", inck=16)
                a_ld = gack()
                W("vector", "agp", a_ld)
                cv = EM("vector", lambda e, w=w, nfree=nfree: e.tensor_copy(w[:, 0:nfree], wstage[:, 0:nfree]),
                        inc="vv")
                W("gpsimd", "vv", cv)

            load_cast(inwT, inw_bf, 16, 16 * 512)
            load_cast(skwT, skw_bf, 16, 16 * 512)
            load_cast(wd1T, wd1_bf, NET, NET * 128)
            load_cast(wbcT, wbc_bf, NET, NET * 32)
            for et in range(NET - 1):
                EM("vector", lambda e, et=et: e.memset(xin_buf[et][:, 0:3], 0.0))
            v_w = EM("vector", lambda e: e.memset(xin_buf[NET - 1][:, 0:3], 0.0), inc="vv")

            W("sync", "cc_rn", 1)
            NLC = 16
            mm_xin, mm_skip, evac_xin, evac_skip = {}, {}, {}, {}
            xsilu, sk2_ack = {}, {}
            rhs_cnt = {}

            def rhs_load(lc):
                s = lc % 2
                c, hh = lc // 2, lc % 2
                EM("sync", lambda e, s=s, c=c, hh=hh: e.dma_start(
                    out=rhs[s][:, :].rearrange("p (d l) -> p d l", d=16),
                    in_=RNT_ALL.ap()[c * D:(c + 1) * D, hh * 512:(hh + 1) * 512]
                    .rearrange("(d p) l -> p d l", p=128)), inc=f"fw{s}", inck=16)
                rhs_cnt[lc] = p.c[f"fw{s}"]

            d1mm, bcmm, d1ev, bcev, d1_ack = {}, {}, {}, {}, {}
            rhs_load(0)
            for lc in range(NLC):
                b = lc % 2
                if lc + 1 < NLC:
                    if lc >= 1:
                        W("sync", "mm", mm_skip[(lc - 1, NET - 1)])
                    rhs_load(lc + 1)
                W("tensor", f"fw{b}", rhs_cnt[lc])
                if lc == 0:
                    W("tensor", "vv", v_w)
                for et in range(NET):
                    pb = et % 2
                    prev = (lc, et - 2) if et >= 2 else (lc - 1, et + 2)
                    if prev in evac_xin:
                        W("tensor", "vv", evac_xin[prev])
                    for dt in range(16):
                        mm_xin[(lc, et)] = EM("tensor", lambda e, et=et, dt=dt, b=b, pb=pb: e.matmul(
                            ps_xin[pb][:, :], inw_bf[:, dt * 512 + et * 128:dt * 512 + (et + 1) * 128],
                            rhs[b][:, dt * 512:(dt + 1) * 512], start=(dt == 0), stop=(dt == 15)),
                            inc="mm" if dt == 15 else None)
                    if prev in evac_skip:
                        W("tensor", "aa", evac_skip[prev])
                    for dt in range(16):
                        mm_skip[(lc, et)] = EM("tensor", lambda e, et=et, dt=dt, b=b, pb=pb: e.matmul(
                            ps_skip[pb][:, :], skw_bf[:, dt * 512 + et * 128:dt * 512 + (et + 1) * 128],
                            rhs[b][:, dt * 512:(dt + 1) * 512], start=(dt == 0), stop=(dt == 15)),
                            inc="mm" if dt == 15 else None)

                    W("vector", "mm", mm_xin[(lc, et)])
                    if (lc - 1, et) in xsilu:
                        W("vector", "aa", xsilu[(lc - 1, et)])
                    evv = EM("vector", lambda e, et=et, pb=pb: e.tensor_copy(
                        xin_buf[et][:, 3:515], ps_xin[pb][:, :]), inc="vv")
                    W("vector", "vv", evv)
                    cv0 = EM("vector", lambda e, et=et: e.tensor_scalar(
                        acc[et][:, :], xin_buf[et][:, 0:512], ecol(et, 0), None, OP.mult), inc="vv")
                    for k in range(1, 4):
                        W("vector", "vv", cv0)
                        cv0 = EM("vector", lambda e, et=et, k=k: e.scalar_tensor_tensor(
                            acc[et][:, :], xin_buf[et][:, k:k + 512], ecol(et, k),
                            acc[et][:, :], OP.mult, OP.add), inc="vv")
                    evac_xin[(lc, et)] = cv0
                    EM("vector", lambda e, et=et: e.tensor_copy(
                        xin_buf[et][:, 0:3], xin_buf[et][:, 512:515]), inc="vv")
                    W("scalar", "vv", evac_xin[(lc, et)])
                    xsilu[(lc, et)] = EM("scalar", lambda e, et=et, lc=lc: e.activation(
                        x_bf[:, et * L + lc * 512:et * L + (lc + 1) * 512], acc[et][:, :], AF.Silu,
                        bias=ecol(et, 4)), inc="aa")
                    W("scalar", "mm", mm_skip[(lc, et)])
                    if (lc - 2, et) in sk2_ack:
                        W("scalar", "agp", sk2_ack[(lc - 2, et)])
                    evac_skip[(lc, et)] = EM("scalar", lambda e, et=et, pb=pb, b=b: e.activation(
                        sk2_t[b][et][:, :], ps_skip[pb][:, :], AF.Silu), inc="aa")
                    W("gpsimd", "aa", evac_skip[(lc, et)])
                    EM("gpsimd", lambda e, et=et, lc=lc, b=b: e.dma_start(
                        out=SK2_SP.ap()[et * 128:(et + 1) * 128, lc * 512:(lc + 1) * 512],
                        in_=sk2_t[b][et][:, :]), inc="dgp", inck=16)
                    sk2_ack[(lc, et)] = gack()

                W("tensor", "aa", xsilu[(lc, NET - 1)])
                if lc >= 1:
                    W("tensor", "vv", d1ev[lc - 1])
                for et in range(NET):
                    d1mm[lc] = EM("tensor", lambda e, et=et, lc=lc: e.matmul(
                        ps_d1[:, :], wd1_bf[:, et * 128:(et + 1) * 128],
                        x_bf[:, et * L + lc * 512:et * L + (lc + 1) * 512],
                        start=(et == 0), stop=(et == NET - 1)),
                        inc="mm" if et == NET - 1 else None)
                if lc >= 1:
                    W("tensor", "vv", bcev[lc - 1])
                for et in range(NET):
                    bcmm[lc] = EM("tensor", lambda e, et=et, lc=lc: e.matmul(
                        ps_bc[0:32, :], wbc_bf[:, et * 32:(et + 1) * 32],
                        x_bf[:, et * L + lc * 512:et * L + (lc + 1) * 512],
                        start=(et == 0), stop=(et == NET - 1)),
                        inc="mm" if et == NET - 1 else None)
                W("vector", "mm", d1mm[lc])
                if lc >= 2:
                    W("vector", "agp", d1_ack[lc - 2])
                d1ev[lc] = EM("vector", lambda e, b=b: e.tensor_copy(d1e[b][:, :], ps_d1[:, :]), inc="vv")
                W("vector", "mm", bcmm[lc])
                bcev[lc] = EM("vector", lambda e, b=b: e.tensor_copy(bce[b][0:32, :], ps_bc[0:32, :]),
                              inc="vv")
                W("gpsimd", "vv", bcev[lc])
                EM("gpsimd", lambda e, lc=lc, b=b: e.dma_start(
                    out=ar_in.ap()[0:R, lc * 512:(lc + 1) * 512], in_=d1e[b][:, :]), inc="dgp", inck=16)
                EM("gpsimd", lambda e, lc=lc, b=b: e.dma_start(
                    out=ar_in.ap()[R:R + 32, lc * 512:(lc + 1) * 512], in_=bce[b][0:32, :]), inc="dgp", inck=16)
                d1_ack[lc] = gack()

            x_ready = p.c["aa"]
            W("gpsimd", "dgp", p.c["dgp"])
            EM("gpsimd", lambda e: e.collective_compute(
                "AllReduce", OP.add, replica_groups=[list(range(NCORE))],
                ins=[ar_in.ap().opt()], outs=[ar_out.ap().opt()]), inc="cc_ar")
            EM("gpsimd", lambda e: e.collective_compute(
                "AllGather", OP.bypass, replica_groups=[list(range(NCORE))],
                ins=[ag_wdn_in.ap().opt()], outs=[WDNT.ap().opt()]), inc="cc_wdn")
            EM("gpsimd", lambda e: e.collective_compute(
                "AllGather", OP.bypass, replica_groups=[list(range(NCORE))],
                ins=[ag_ow_in.ap().opt()], outs=[OWT.ap().opt()]), inc="cc_ow")
            W("gpsimd", "aa", x_ready)
            for et in range(NET):
                EM("gpsimd", lambda e, et=et: e.dma_start(
                    out=X_SP.ap()[et * 128:(et + 1) * 128, :], in_=x_bf[:, et * L:(et + 1) * L]),
                    inc="dgp", inck=16)
            p.barrier(dma_flush=[("gpsimd", "dgp"), ("sync", "dsy")])

        with contextlib.ExitStack() as ph:
            sb = lambda name, shape, dtype=F32: ph.enter_context(nc.sbuf_tensor(name, shape, dtype))
            psb = lambda name, shape: ph.enter_context(nc.psum_tensor(name, shape, F32))
            d1t_f = sb("d1t_f", [128, CH])
            d1t_bf = sb("d1t_bf", [128, CH], BF16)
            bmcs = sb("bmcs", [32, CH])
            bmc_bf = sb("bmc_bf", [32, 2 * CH], BF16)
            wd2_bf = sb("wd2_bf", [128, EC], BF16)
            delta = sb("delta", [128, NET * CH])
            xc_bf = [sb(f"xc_bf{i}", [128, NET * CH], BF16) for i in range(2)]
            u_bf = sb("u_bf", [128, NET * CH], BF16)
            y_acc = sb("y_acc", [128, NET * CH])
            sk2c = [sb(f"sk2c{i}", [128, CH], BF16) for i in range(2)]
            p_buf = sb("p_buf", [128, 2 * CH])
            ug_buf = sb("ug_buf", [128, 2 * CH], BF16)
            h_buf = sb("h_buf", [128, 2 * CH])
            prod = [sb(f"prod{i}", [128, CH]) for i in range(3)]
            brep = sb("brep", [128, 2 * CH], BF16)
            crep = sb("crep", [128, 2 * CH], BF16)
            t0 = sb("t0", [128, CH])
            t1 = sb("t1", [128, CH])
            y_bf = [sb(f"y_bf{i}", [128, CH], BF16) for i in range(2)]
            sel_bf = sb("sel_bf", [32, 32 * 128], BF16)
            ps_dp = psb("ps_dp", [128, CH])
            ps_b = psb("ps_b", [128, 512])
            ps_c = psb("ps_c", [128, 512])
            xnt = sb("xnt", [128, 16 * 512], BF16)
            wg_g = [sb(f"wg_g{i}", [128, 16 * 128], BF16) for i in range(2)]
            wg_u = [sb(f"wg_u{i}", [128, 16 * 128], BF16) for i in range(2)]
            gb_st = sb("gb_st", [128, 8 * 512], BF16)
            h_big = sb("h_big", [128, 8 * 512], BF16)
            h_in = [sb(f"h_in{i}", [128, 2 * 512], BF16) for i in range(3)]
            wdn_t = [sb(f"wdn_t{i}", [128, 2 * 512], BF16) for i in range(3)]
            o2_t = [sb(f"o2_t{i}", [128, 512]) for i in range(2)]
            pf = [psb(f"pf{i}", [128, 512]) for i in range(4)]

            EM("gpsimd", lambda e: e.dma_start(out=sel_bf[:, :], in_=sel_in), inc="dgp", inck=16)
            EM("gpsimd", lambda e: e.dma_start(out=d1t_f[:, 0:EC], in_=wd2T), inc="dgp", inck=16)
            a_w = gack()
            W("vector", "agp", a_w)
            wcv = EM("vector", lambda e: e.tensor_copy(wd2_bf[:, :], d1t_f[:, 0:EC]), inc="vv")
            W("sync", "cc_ar", 1)
            W("sync", "vv", wcv)
            W("tensor", "agp", a_w)
            W("tensor", "vv", wcv)
            xnt_ack = [None]

            dp_mm, dp_chain, delta_ok, pexp = {}, {}, {}, {}
            brep_ok = {}
            ug_done, tts_done, prod_done, add_done = {}, {}, {}, {}
            stt_done, ymul_done, ybf_ack, sk_ack = {}, {}, {}, {}
            d1_cast, xc_ack = {}, {}
            prod_war = {}
            gu_mm, silu_done, hmult_done, hst_ack = {}, {}, {}, {}
            dn_mm, o2_evac, o2_ack = {}, {}, {}
            dn_state = {"gpr": 0}

            def gate_units(lh):
                if lh == 1:
                    W("sync", "mm", gu_mm[(0, 63)])
                for it in range(64):
                    if it == 0:
                        EM("sync", lambda e, lh=lh: e.dma_start(
                            out=xnt[:, :].rearrange("p (d l) -> p d l", d=16),
                            in_=XNT_SP.ap()[:, lh * 512:(lh + 1) * 512]
                            .rearrange("(d p) l -> p d l", p=128)), inc="dsy", inck=16)
                        xnt_ack[0] = sack()
                        if lh == 0 and it == 0:
                            pass
                        W("tensor", "asy", xnt_ack[0])
                    b = it % 2
                    pg, pu = pf[2 * b], pf[2 * b + 1]
                    prevk = (lh, it - 2) if it >= 2 else ((lh - 1, 62 + it) if lh >= 1 else None)
                    if lh == 0 and it == 0:
                        W("scalar", "cc_wgu", 1)
                    if prevk in gu_mm:
                        W("scalar", "mm", gu_mm[prevk])
                    EM("scalar", lambda e, it=it, b=b: e.dma_start(
                        out=wg_g[b][:, :],
                        in_=WGUT.ap()[it:it + 1, :].rearrange("r (p c) -> (r p) c", p=128)),
                        inc=f"fw{b}", inck=16)
                    EM("scalar", lambda e, it=it, b=b: e.dma_start(
                        out=wg_u[b][:, :],
                        in_=WGUT.ap()[64 + it:65 + it, :].rearrange("r (p c) -> (r p) c", p=128)),
                        inc=f"fw{b}", inck=16)
                    W("tensor", f"fw{b}", p.c[f"fw{b}"])
                    if prevk in silu_done:
                        W("tensor", "aa", silu_done[prevk])
                        W("tensor", "vv", hmult_done[prevk])
                    for dt in range(16):
                        EM("tensor", lambda e, dt=dt, b=b, pg=pg: e.matmul(
                            pg[:, :], wg_g[b][:, dt * 128:(dt + 1) * 128],
                            xnt[:, dt * 512:(dt + 1) * 512], start=(dt == 0), stop=(dt == 15)))
                    for dt in range(16):
                        gu_mm[(lh, it)] = EM("tensor", lambda e, dt=dt, b=b, pu=pu: e.matmul(
                            pu[:, :], wg_u[b][:, dt * 128:(dt + 1) * 128],
                            xnt[:, dt * 512:(dt + 1) * 512], start=(dt == 0), stop=(dt == 15)),
                            inc="mm" if dt == 15 else None)
                    W("scalar", "mm", gu_mm[(lh, it)])
                    j = it % 8
                    silu_done[(lh, it)] = EM("scalar", lambda e, j=j, pg=pg: e.activation(
                        gb_st[:, j * 512:(j + 1) * 512], pg[:, :], AF.Identity), inc="aa")
                    W("vector", "mm", gu_mm[(lh, it)])
                    if j == 0:
                        bk = (lh, it // 8 - 1) if it >= 8 else ((lh - 1, 7) if lh >= 1 else None)
                        if bk in hst_ack:
                            W("vector", "asy", hst_ack[bk])
                    hmult_done[(lh, it)] = EM("vector", lambda e, j=j, pu=pu: e.tensor_copy(
                        h_big[:, j * 512:(j + 1) * 512], pu[:, :]), inc="vv")
                    if j == 7:
                        sb_ok = EM("scalar", lambda e: e.activation(
                            gb_st[:, :], gb_st[:, :], AF.Silu), inc="aa")
                        W("vector", "aa", sb_ok)
                        hm = EM("vector", lambda e: e.tensor_mul(
                            h_big[:, :], h_big[:, :], gb_st[:, :]), inc="vv")
                        hmult_done[(lh, it)] = hm
                        W("sync", "vv", hm)
                        it0 = it - 7
                        EM("sync", lambda e, it0=it0, lh=lh: e.dma_start(
                            out=H_SP.ap()[it0 * 128:(it0 + 8) * 128, lh * 512:(lh + 1) * 512]
                            .rearrange("(i p) l -> p i l", p=128),
                            in_=h_big[:, :].rearrange("p (i l) -> p i l", i=8)), inc="dsy", inck=16)
                        hst_ack[(lh, it // 8)] = sack()
                    yield

            def down_units():
                for lh in range(2):
                    for dc in range(4):
                        for pr in range(32):
                            it = 2 * pr
                            gpr = dn_state["gpr"]
                            dn_state["gpr"] += 1
                            b = gpr % 3
                            if (gpr - 3) in dn_mm:
                                W("scalar", "mm", dn_mm[gpr - 3])
                            if dc == 0 and pr == 0:
                                W("scalar", "cc_wdn", 1)
                                W("scalar", "asy", hst_ack[(lh, 7)])
                            EM("scalar", lambda e, it=it, dc=dc, b=b: e.dma_start(
                                out=wdn_t[b][:, :].rearrange("p (i c) -> p i c", i=2),
                                in_=WDNT.ap()[it * 128:(it + 2) * 128, dc * 512:(dc + 1) * 512]
                                .rearrange("(i p) c -> p i c", p=128)), inc=f"fw{b}", inck=16)
                            EM("scalar", lambda e, it=it, lh=lh, b=b: e.dma_start(
                                out=h_in[b][:, :].rearrange("p (i c) -> p i c", i=2),
                                in_=H_SP.ap()[it * 128:(it + 2) * 128, lh * 512:(lh + 1) * 512]
                                .rearrange("(i p) c -> p i c", p=128)), inc=f"fw{b}", inck=16)
                            W("tensor", f"fw{b}", p.c[f"fw{b}"])
                            if it == 0:
                                if lh == 0 and dc == 0:
                                    W("tensor", "vv", hmult_done[(1, 63)])
                                    W("tensor", "aa", silu_done[(1, 63)])
                                else:
                                    W("tensor", "aa", o2_evac[3])
                            for ii in range(2):
                                for lt2 in range(4):
                                    dn_mm[gpr] = EM(
                                        "tensor", lambda e, ii=ii, lt2=lt2, b=b, it=it: e.matmul(
                                            pf[lt2][:, :],
                                            h_in[b][:, ii * 512 + lt2 * 128:ii * 512 + (lt2 + 1) * 128],
                                            wdn_t[b][:, ii * 512:(ii + 1) * 512],
                                            start=(it + ii == 0), stop=(it + ii == 63)),
                                        inc="mm" if (ii == 1 and lt2 == 3) else None)
                            yield
                        W("scalar", "mm", dn_mm[dn_state["gpr"] - 1])
                        for lt2 in range(4):
                            ob = lt2 % 2
                            okey = (lt2 - 2) if lt2 >= 2 else (lt2 + 2)
                            if okey in o2_ack:
                                W("scalar", "asy", o2_ack[okey])
                            o2_evac[lt2] = EM("scalar", lambda e, lt2=lt2, ob=ob: e.activation(
                                o2_t[ob][:, :], pf[lt2][:, :], AF.Identity), inc="aa")
                            W("sync", "aa", o2_evac[lt2])
                            EM("sync", lambda e, lh=lh, dc=dc, lt2=lt2, ob=ob: e.dma_start(
                                out=OUT2_SP.ap()[lh * 512 + lt2 * 128:lh * 512 + (lt2 + 1) * 128,
                                                 dc * 512:(dc + 1) * 512], in_=o2_t[ob][:, :]),
                                inc="dsy", inck=16)
                            o2_ack[lt2] = sack()
                        yield

            import itertools as _it
            ffn_iter = _it.chain(gate_units(0), gate_units(1), down_units())
            ffn_left = [64 + 64 + 2 * 4 * 33]

            def pull():
                if ffn_left[0] > 0:
                    next(ffn_iter)
                    ffn_left[0] -= 1

            bmc_cp, bc_mm_last = {}, {}
            a_bmc_d, a_d1_d, a_xc_d = {}, {}, {}

            def chunk_loads(cnk):
                c0 = cnk * CH
                if cnk >= 1:
                    W("sync", "vv", bmc_cp[cnk - 1])
                EM("sync", lambda e, c0=c0: e.dma_start(
                    out=bmcs[:, :], in_=ar_out.ap()[R:R + 32, c0:c0 + CH]), inc="dsy", inck=16)
                a_bmc_d[cnk] = sack()
                if cnk >= 1:
                    W("sync", "vv", d1_cast[cnk - 1])
                EM("sync", lambda e, c0=c0: e.dma_start(
                    out=d1t_f[:, :], in_=ar_out.ap()[0:R, c0:c0 + CH]), inc="dsy", inck=16)
                a_d1_d[cnk] = sack()
                if cnk >= 2:
                    W("sync", "vv", stt_done[(cnk - 2, NET - 1)])
                EM("sync", lambda e, c0=c0, xb=cnk % 2: e.dma_start(
                    out=xc_bf[xb][:, :].rearrange("p (et l) -> p et l", et=NET),
                    in_=X_SP.ap()[:, c0:c0 + CH].rearrange("(et p) l -> p et l", p=128)),
                    inc="dsy", inck=16)
                a_xc_d[cnk] = sack()

            chunk_loads(0)
            giter = [0]
            for cnk in range(NCH):
                c0 = cnk * CH
                cb2 = cnk % 2
                xb = cnk % 2
                W("vector", "asy", a_bmc_d[cnk])
                if cnk >= 2:
                    W("vector", "mm", bc_mm_last[cnk - 2])
                bmc_cp[cnk] = EM("vector", lambda e, cb2=cb2: e.tensor_copy(
                    bmc_bf[:, cb2 * CH:(cb2 + 1) * CH], bmcs[:, :]), inc="vv")
                W("vector", "asy", a_d1_d[cnk])
                if cnk >= 1:
                    W("vector", "mm", dp_mm[(cnk - 1, NET - 1)])
                d1_cast[cnk] = EM("vector", lambda e: e.tensor_copy(d1t_bf[:, :], d1t_f[:, :]), inc="vv")
                W("tensor", "vv", d1_cast[cnk])
                for et in range(NET):
                    pk = (cnk, et - 1) if et >= 1 else (cnk - 1, NET - 1)
                    if pk in dp_chain:
                        W("tensor", "aa", dp_chain[pk])
                    for hh in range(2):
                        dp_mm[(cnk, et)] = EM("tensor", lambda e, et=et, hh=hh: e.matmul(
                            ps_dp[:, hh * 512:(hh + 1) * 512], wd2_bf[:, et * 128:(et + 1) * 128],
                            d1t_bf[:, hh * 512:(hh + 1) * 512], start=True, stop=True),
                            inc="mm" if hh == 1 else None)
                    W("scalar", "mm", dp_mm[(cnk, et)])
                    if pk in delta_ok:
                        W("scalar", "vv", delta_ok[pk])
                    EM("scalar", lambda e, et=et: e.activation(t0[:, :], ps_dp[:, :], AF.Relu, bias=ecol(et, 5)))
                    EM("scalar", lambda e, et=et: e.activation(t1[:, :], ps_dp[:, :], AF.Abs, bias=ecol(et, 5)))
                    EM("scalar", lambda e: e.activation(t1[:, :], t1[:, :], AF.Exp, scale=-1.0))
                    dp_chain[(cnk, et)] = EM("scalar", lambda e: e.activation(t1[:, :], t1[:, :], AF.Ln, bias=1.0),
                                             inc="aa")
                    W("vector", "aa", dp_chain[(cnk, et)])
                    if cnk >= 1:
                        W("vector", "aa", pexp[(cnk - 1, N - 1, et)])
                    delta_ok[(cnk, et)] = EM("vector", lambda e, et=et: e.tensor_add(
                        delta[:, et * CH:(et + 1) * CH], t0[:, :], t1[:, :]), inc="vv")
                W("vector", "asy", a_xc_d[cnk])
                for et in range(NET):
                    EM("vector", lambda e, et=et, xb=xb: e.tensor_mul(
                        u_bf[:, et * CH:(et + 1) * CH], delta[:, et * CH:(et + 1) * CH],
                        xc_bf[xb][:, et * CH:(et + 1) * CH]), inc="vv")

                pending_adds = []

                def flush_add():
                    s2, et2, pb2 = pending_adds.pop(0)
                    W("vector", "gg", prod_done[(cnk, s2)])
                    add_done[(cnk, s2)] = EM("vector", lambda e, et2=et2, pb2=pb2: e.tensor_add(
                        y_acc[:, et2 * CH:(et2 + 1) * CH], y_acc[:, et2 * CH:(et2 + 1) * CH],
                        prod[pb2][:, :]), inc="vv")
                    prod_war[pb2] = add_done[(cnk, s2)]

                for n in range(N):
                    nb = n % 2
                    gk = (cnk, n - 2) if n >= 2 else ((cnk - 1, N - 2 + n) if cnk >= 1 else None)
                    if gk is not None and gk in brep_ok:
                        s_last = (gk[1]) * NET + NET - 1
                        W("tensor", "vv", ug_done[(gk[0], s_last)])
                        W("tensor", "gg", prod_done[(gk[0], s_last)])
                    pk = (cnk, n - 1) if n >= 1 else ((cnk - 1, N - 1) if cnk >= 1 else None)
                    if pk in brep_ok:
                        W("tensor", "aa", brep_ok[pk][1])
                    if n == 0:
                        W("tensor", "vv", bmc_cp[cnk])
                    m1 = EM("tensor", lambda e, n=n, cb2=cb2: e.matmul(
                        ps_b[:, :], sel_bf[0:32, n * 128:(n + 1) * 128],
                        bmc_bf[0:32, cb2 * CH:cb2 * CH + 512], start=True, stop=True), inc="mm")
                    m2 = EM("tensor", lambda e, n=n, cb2=cb2: e.matmul(
                        ps_c[:, :], sel_bf[0:32, n * 128:(n + 1) * 128],
                        bmc_bf[0:32, cb2 * CH + 512:(cb2 + 1) * CH], start=True, stop=True), inc="mm")
                    W("scalar", "mm", m2)
                    if gk is not None and gk in brep_ok:
                        s_last = (gk[1]) * NET + NET - 1
                        W("scalar", "vv", ug_done[(gk[0], s_last)])
                        W("scalar", "gg", prod_done[(gk[0], s_last)])
                    eb0 = EM("scalar", lambda e, nb=nb: e.activation(
                        brep[:, nb * CH:nb * CH + 512], ps_b[:, :], AF.Identity), inc="aa")
                    eb1 = EM("scalar", lambda e, nb=nb: e.activation(
                        brep[:, nb * CH + 512:(nb + 1) * CH], ps_c[:, :], AF.Identity), inc="aa")
                    W("tensor", "aa", eb0)
                    m3 = EM("tensor", lambda e, n=n, cb2=cb2: e.matmul(
                        ps_b[:, :], sel_bf[0:32, (16 + n) * 128:(17 + n) * 128],
                        bmc_bf[0:32, cb2 * CH:cb2 * CH + 512], start=True, stop=True), inc="mm")
                    W("tensor", "aa", eb1)
                    m4 = EM("tensor", lambda e, n=n, cb2=cb2: e.matmul(
                        ps_c[:, :], sel_bf[0:32, (16 + n) * 128:(17 + n) * 128],
                        bmc_bf[0:32, cb2 * CH + 512:(cb2 + 1) * CH], start=True, stop=True), inc="mm")
                    bc_mm_last[cnk] = m4
                    W("scalar", "mm", m4)
                    EM("scalar", lambda e, nb=nb: e.activation(
                        crep[:, nb * CH:nb * CH + 512], ps_b[:, :], AF.Identity), inc="aa")
                    ec1 = EM("scalar", lambda e, nb=nb: e.activation(
                        crep[:, nb * CH + 512:(nb + 1) * CH], ps_c[:, :], AF.Identity), inc="aa")
                    brep_ok[(cnk, n)] = (eb1, ec1)

                    for et in range(NET):
                        s = n * NET + et
                        sb_ = s % 2
                        pb = s % 3
                        cidx = et * N + n
                        if n == 0 and et == 0:
                            W("scalar", "vv", delta_ok[(cnk, NET - 1)])
                        tk = (cnk, s - 2) if s >= 2 else ((cnk - 1, N * NET - 2 + s) if cnk >= 1 else None)
                        if tk in tts_done:
                            W("scalar", "vv", tts_done[tk])
                        pexp[(cnk, n, et)] = EM("scalar", lambda e, et=et, n=n, sb_=sb_: e.activation(
                            p_buf[:, sb_ * CH:(sb_ + 1) * CH], delta[:, et * CH:(et + 1) * CH], AF.Exp,
                            scale=asb[:, et * 16 + n:et * 16 + n + 1]), inc="aa")
                        W("vector", "aa", brep_ok[(cnk, n)][0])
                        ug_done[(cnk, s)] = EM("vector", lambda e, et=et, nb=nb, sb_=sb_: e.tensor_mul(
                            ug_buf[:, sb_ * CH:(sb_ + 1) * CH], u_bf[:, et * CH:(et + 1) * CH],
                            brep[:, nb * CH:(nb + 1) * CH]), inc="vv")
                        W("vector", "aa", pexp[(cnk, n, et)])
                        if tk in prod_done:
                            W("vector", "gg", prod_done[tk])
                        if cnk == 0:
                            tts_done[(cnk, s)] = EM("vector", lambda e, sb_=sb_: e.tensor_tensor_scan(
                                h_buf[:, sb_ * CH:(sb_ + 1) * CH], p_buf[:, sb_ * CH:(sb_ + 1) * CH],
                                ug_buf[:, sb_ * CH:(sb_ + 1) * CH], 0.0, OP.mult, OP.add), inc="vv")
                        else:
                            tts_done[(cnk, s)] = EM("vector", lambda e, sb_=sb_, cidx=cidx: e.tensor_tensor_scan(
                                h_buf[:, sb_ * CH:(sb_ + 1) * CH], p_buf[:, sb_ * CH:(sb_ + 1) * CH],
                                ug_buf[:, sb_ * CH:(sb_ + 1) * CH], carry[:, cidx:cidx + 1],
                                OP.mult, OP.add), inc="vv")
                        EM("vector", lambda e, sb_=sb_, cidx=cidx: e.tensor_copy(
                            carry[:, cidx:cidx + 1], h_buf[:, sb_ * CH + CH - 1:sb_ * CH + CH]))
                        W("gpsimd", "vv", tts_done[(cnk, s)])
                        W("gpsimd", "aa", brep_ok[(cnk, n)][1])
                        if n == 0:
                            prod_done[(cnk, s)] = EM("gpsimd", lambda e, et=et, nb=nb, sb_=sb_: e.tensor_mul(
                                y_acc[:, et * CH:(et + 1) * CH], h_buf[:, sb_ * CH:(sb_ + 1) * CH],
                                crep[:, nb * CH:(nb + 1) * CH]), inc="gg")
                        else:
                            if pb in prod_war:
                                W("gpsimd", "vv", prod_war[pb])
                            prod_done[(cnk, s)] = EM("gpsimd", lambda e, nb=nb, sb_=sb_, pb=pb: e.tensor_mul(
                                prod[pb][:, :], h_buf[:, sb_ * CH:(sb_ + 1) * CH],
                                crep[:, nb * CH:(nb + 1) * CH]), inc="gg")
                            pending_adds.append((s, et, pb))
                        if len(pending_adds) > 2:
                            flush_add()
                        if giter[0] >= 8:
                            pull()
                        if giter[0] % 64 == 48:
                            if cnk + 1 < NCH:
                                chunk_loads(cnk + 1)
                        giter[0] += 1
                while pending_adds:
                    flush_add()

                for et in range(NET):
                    yb = et % 2
                    skkey = (cnk, et - 2) if et >= 2 else ((cnk - 1, et + 2) if cnk >= 1 else None)
                    if skkey in ymul_done:
                        W("sync", "gg", ymul_done[skkey])
                    EM("sync", lambda e, et=et, c0=c0, yb=yb: e.dma_start(
                        out=sk2c[yb][:, :], in_=SK2_SP.ap()[et * 128:(et + 1) * 128, c0:c0 + CH]),
                        inc="dsy", inck=16)
                    sk_ack[(cnk, et)] = sack()
                    stt_done[(cnk, et)] = EM("vector", lambda e, et=et, xb=xb: e.scalar_tensor_tensor(
                        y_acc[:, et * CH:(et + 1) * CH], xc_bf[xb][:, et * CH:(et + 1) * CH], ecol(et, 6),
                        y_acc[:, et * CH:(et + 1) * CH], OP.mult, OP.add), inc="vv")
                    W("gpsimd", "vv", stt_done[(cnk, et)])
                    W("gpsimd", "asy", sk_ack[(cnk, et)])
                    if skkey in ybf_ack:
                        W("gpsimd", "asy", ybf_ack[skkey])
                    ymul_done[(cnk, et)] = EM("gpsimd", lambda e, et=et, yb=yb: e.tensor_mul(
                        y_bf[yb][:, :], y_acc[:, et * CH:(et + 1) * CH], sk2c[yb][:, :]), inc="gg")
                    W("sync", "gg", ymul_done[(cnk, et)])
                    EM("sync", lambda e, et=et, c0=c0, yb=yb: e.dma_start(
                        out=ag_y_in.ap()[et * 128:(et + 1) * 128, c0:c0 + CH], in_=y_bf[yb][:, :]),
                        inc="dsy", inck=16)
                    ybf_ack[(cnk, et)] = sack()

            while ffn_left[0] > 0:
                pull()

            W("gpsimd", "asy", ybf_ack[(NCH - 1, NET - 1)])
            EM("gpsimd", lambda e: e.collective_compute(
                "AllGather", OP.bypass, replica_groups=[list(range(NCORE))],
                ins=[ag_y_in.ap().opt()], outs=[Y_ALL.ap().opt()]), inc="cc_y")
            p.barrier(dma_flush=[("gpsimd", "dgp"), ("sync", "dsy"), ("scalar", "dsc")])

        with contextlib.ExitStack() as ph:
            sb = lambda name, shape, dtype=F32: ph.enter_context(nc.sbuf_tensor(name, shape, dtype))
            psb = lambda name, shape: ph.enter_context(nc.psum_tensor(name, shape, F32))
            y_sb = sb("y_sb", [128, 32 * LC], BF16)
            owt_t = [sb(f"owt_t{i}", [128, 512], BF16) for i in range(2)]
            xn_t4 = sb("xn_t4", [128, 512])
            o2_t4 = sb("o2_t4", [128, 512])
            g_t4 = sb("g_t4", [128, 512])
            rs_t4 = sb("rs_t4", [128, 512])
            s_t4 = sb("s_t4", [128, 512])
            out_t4 = sb("out_t4", [128, 512])
            ps_o = [psb(f"ps_o{i}", [128, 512]) for i in range(8)]

            W("sync", "cc_y", 1)
            W("sync", "cc_ow", 1)

            def y_load(e):
                for arm in e.switch_core_id(n=128):
                    lcore = physical_to_logical_core_id(int(arm) % 128, 0, container_num_cores=NCORE) % NCORE
                    c0 = lcore * LC
                    e.dma_start(
                        out=y_sb[:, :].rearrange("p (et l) -> p et l", et=32),
                        in_=Y_ALL.ap()[:, c0:c0 + LC].rearrange("(et p) l -> p et l", p=128),
                    ).then_inc(p.sems["dsy"], 16)
            p.c["dsy"] += 16
            p.prog["sync"].append(y_load)
            yl_ack = sack()

            ow_mm, ph3_evac, od_ack = {}, {}, {}
            for dc in range(4):
                for et8 in range(32):
                    b = et8 % 2
                    qw = "sync" if b == 0 else "gpsimd"
                    if (dc, et8 - 2) in ow_mm:
                        W(qw, "mm", ow_mm[(dc, et8 - 2)])
                    elif dc > 0 and et8 < 2:
                        W(qw, "mm", ow_mm[(dc - 1, 30 + et8)])
                    q, qs, ak = ("sync", "dsy", "asy") if b == 0 else ("gpsimd", "dgp", "agp")
                    EM(q, lambda e, et8=et8, dc=dc, b=b: e.dma_start(
                        out=owt_t[b][:, :], in_=OWT.ap()[et8 * 128:(et8 + 1) * 128, dc * 512:(dc + 1) * 512]),
                        inc=qs, inck=16)
                    a_ow = p.ack(q, qs, ak)
                    W("tensor", ak, a_ow)
                    W("tensor", "asy", yl_ack)
                    if dc > 0 and et8 == 0:
                        W("tensor", "vv", ph3_evac[dc - 1])
                    for lt in range(8):
                        ow_mm[(dc, et8)] = EM("tensor", lambda e, et8=et8, b=b, lt=lt: e.matmul(
                            ps_o[lt][:, :], y_sb[:, et8 * LC + lt * 128:et8 * LC + (lt + 1) * 128],
                            owt_t[b][:, :], start=(et8 == 0), stop=(et8 == 31)),
                            inc="mm" if lt == 7 else None)
                last_ov = None
                for lt in range(8):
                    prev_ov = last_ov if lt > 0 else (ph3_evac.get(dc - 1) if dc > 0 else None)
                    if prev_ov:
                        W("gpsimd", "vv", prev_ov)
                    W("gpsimd", "mm", ow_mm[(dc, 31)])
                    EM("gpsimd", lambda e, lt=lt, dc=dc: e.dma_start(
                        out=xn_t4[:, :], in_=XN_SP.ap()[lt * 128:(lt + 1) * 128, dc * 512:(dc + 1) * 512]),
                        inc="dgp", inck=16)
                    EM("gpsimd", lambda e, lt=lt, dc=dc: e.dma_start(
                        out=o2_t4[:, :], in_=OUT2_SP.ap()[lt * 128:(lt + 1) * 128, dc * 512:(dc + 1) * 512]),
                        inc="dgp", inck=16)
                    EM("gpsimd", lambda e, lt=lt, dc=dc: e.dma_start(
                        out=rs_t4[:, :], in_=resid[lt * 128:(lt + 1) * 128, dc * 512:(dc + 1) * 512]),
                        inc="dgp", inck=16)
                    a_ld3 = gack()
                    W("scalar", "agp", a_ld3)
                    sg4 = EM("scalar", lambda e: e.activation(g_t4[:, :], o2_t4[:, :], AF.Silu), inc="aa")
                    W("vector", "agp", a_ld3)
                    W("vector", "aa", sg4)
                    W("vector", "mm", ow_mm[(dc, 31)])
                    prev_od = od_ack.get((dc, lt - 1)) or (od_ack.get((dc - 1, 7)) if dc > 0 else None)
                    if prev_od:
                        W("vector", "agp", prev_od)
                    EM("vector", lambda e, lt=lt: e.tensor_add(s_t4[:, :], xn_t4[:, :], ps_o[lt][:, :]))
                    EM("vector", lambda e: e.tensor_mul(s_t4[:, :], s_t4[:, :], g_t4[:, :]))
                    last_ov = EM("vector", lambda e: e.tensor_add(out_t4[:, :], rs_t4[:, :], s_t4[:, :]), inc="vv")
                    W("gpsimd", "vv", last_ov)
                    EM("gpsimd", lambda e, lt=lt, dc=dc: e.dma_start(
                        out=out[lt * 128:(lt + 1) * 128, dc * 512:(dc + 1) * 512], in_=out_t4[:, :]),
                        inc="dgp", inck=16)
                    od_ack[(dc, lt)] = gack()
                ph3_evac[dc] = last_ov

            if debug:
                for src, dst in [(XN_SP, dbg["dbg_xn"]), (ag_rn_in, dbg["dbg_rn"]),
                                 (ar_out, dbg["dbg_d1bc"]), (SK2_SP, dbg["dbg_sk2"]),
                                 (ag_y_in, dbg["dbg_y"]), (OUT2_SP, dbg["dbg_out2"]),
                                 (X_SP, dbg["dbg_x"])]:
                    EM("gpsimd", lambda e, src=src, dst=dst: e.dma_start(out=dst, in_=src.ap()),
                       inc="dgp", inck=16)
            W("gpsimd", "dgp", p.c["dgp"])
            p.barrier(dma_flush=[("gpsimd", "dgp"), ("sync", "dsy")])

        return _finish(nc, p)


def _finish(nc, p):
    with nc.Block() as block:
        @block.sync
        def _(e):
            for fn in p.prog["sync"]:
                fn(e)

        @block.gpsimd
        def _(e):
            for fn in p.prog["gpsimd"]:
                fn(e)

        @block.vector
        def _(e):
            for fn in p.prog["vector"]:
                fn(e)

        @block.scalar
        def _(e):
            for fn in p.prog["scalar"]:
                fn(e)

        @block.tensor
        def _(e):
            for fn in p.prog["tensor"]:
                fn(e)
    return nc


def make_in_maps(inputs):
    x0 = np.asarray(inputs["L0_attn_resid"]).reshape(L, D)
    norm_w = np.asarray(inputs["L0_attn_norm_w"])
    skip_w = np.asarray(inputs["L0_attn_skip_w"])
    in_w = np.asarray(inputs["L0_attn_in_w"])
    conv_w = np.asarray(inputs["L0_attn_conv_w"])
    conv_b = np.asarray(inputs["L0_attn_conv_b"])
    wd1 = np.asarray(inputs["L0_attn_wd1"])
    wd2 = np.asarray(inputs["L0_attn_wd2"])
    wd2_b = np.asarray(inputs["L0_attn_wd2_b"])
    wB = np.asarray(inputs["L0_attn_wB"])
    wC = np.asarray(inputs["L0_attn_wC"])
    A_log = np.asarray(inputs["L0_attn_A_log"])
    W_D = np.asarray(inputs["L0_attn_W_D"])
    out_w = np.asarray(inputs["L0_attn_out_w"])
    wgu = np.asarray(inputs["L0_ffn_w_gate_up"])
    wdn = np.asarray(inputs["L0_ffn_w_down"])

    nw_rep = np.ascontiguousarray(np.broadcast_to(norm_w[None, :], (128, D)).astype(np.float32))
    wguT = np.ascontiguousarray(wgu.T)
    gtile = wguT[:, :I].reshape(16, 128, 64, 128).transpose(2, 1, 0, 3).reshape(64, 262144)
    utile = wguT[:, I:].reshape(16, 128, 64, 128).transpose(2, 1, 0, 3).reshape(64, 262144)
    wgut_tiled = np.ascontiguousarray(np.concatenate([gtile, utile], 0))
    wdnT = np.ascontiguousarray(wdn.T)
    owT = np.ascontiguousarray(out_w.T)
    sel = np.zeros((32, 32 * 128), np.float32)
    for k in range(32):
        sel[k, k * 128:(k + 1) * 128] = 1.0
    sel_bf = sel.astype(ml_dtypes.bfloat16)

    in_maps = []
    for i in range(NCORE):
        ls = slice(i * LC, (i + 1) * LC)
        es = slice(i * EC, (i + 1) * EC)
        ev = np.zeros((EC, 24), np.float32)
        ev[:, 0:4] = conv_w[es]
        ev[:, 4] = conv_b[es]
        ev[:, 5] = wd2_b[es]
        ev[:, 6] = W_D[es]
        ev[:, 8:24] = A_log[es]
        in_maps.append({
            "resid": np.ascontiguousarray(x0[ls]),
            "nw_rep": nw_rep,
            "inwT": np.ascontiguousarray(in_w[es].T),
            "skwT": np.ascontiguousarray(skip_w[es].T),
            "wd1T": np.ascontiguousarray(wd1[:, es].T),
            "wd2T": np.ascontiguousarray(wd2[es].T),
            "wbcT": np.ascontiguousarray(np.concatenate([wB[:, es].T, wC[:, es].T], axis=1)),
            "evec": ev,
            "sel_in": sel_bf,
            "wguT_sl": np.ascontiguousarray(
                wgut_tiled[i * 16:(i + 1) * 16].reshape(128, 32768)),
            "wdnT_sl": np.ascontiguousarray(wdnT[i * (I // NCORE):(i + 1) * (I // NCORE)]),
            "owT_sl": np.ascontiguousarray(owT[i * (E // NCORE):(i + 1) * (E // NCORE)]),
        })
    return in_maps


def kernel(**inputs):
    in_maps = make_in_maps(inputs)
    nc = build_nc()
    res = run_bass_kernel_spmd(nc, in_maps, core_ids=list(range(NCORE)))
    outp = np.concatenate([res.results[i]["out"] for i in range(NCORE)], axis=0)
    return outp.reshape(1, L, D).astype(np.float32)


# version 10
# speedup vs baseline: 1.1956x; 1.0460x over previous
import contextlib
import numpy as np
import ml_dtypes
import concourse.bass as bass
import concourse.mybir as mybir
from concourse.bass_utils import run_bass_kernel_spmd
from concourse.libnrt import physical_to_logical_core_id

F32 = mybir.dt.float32
BF16 = mybir.dt.bfloat16
AF = mybir.ActivationFunctionType
OP = mybir.AluOpType

NCORE = 8
L = 8192
D = 2048
E = 4096
N = 16
R = 128
I = 8192
LC = L // NCORE
EC = E // NCORE
NET = EC // 128
CH = 1024
NCH = L // CH
EPS = 1e-5

ENGINES = ("sync", "gpsimd", "vector", "scalar", "tensor")


class P:

    def __init__(self, nc, stack, semnames):
        self.nc = nc
        self.prog = {e: [] for e in ENGINES}
        self.sems = {s: stack.enter_context(nc.semaphore(s)) for s in semnames}
        self.c = {s: 0 for s in semnames}
        self._last = {}
        self.nbar = 0

    def wait(self, eng, sem, val):
        if val is None or val <= 0:
            return
        self.prog[eng].append(lambda e, s=sem, v=val: e.wait_ge(self.sems[s], v))

    CHAIN = {"vector": "vv", "scalar": "aa", "gpsimd": "gg"}

    def emit(self, eng, builder, inc=None, inck=1):
        chain_sem = self.CHAIN.get(eng)
        dma_like = (inc in ("dgp", "dsy", "dsc") or (inc or "").startswith("cc")
                    or (inc or "").startswith("fw") or (inc or "").startswith("sw"))
        if chain_sem is not None and not dma_like:
            self.wait(eng, chain_sem, self._last.get(eng, 0))
            if inc is None:
                inc, inck = chain_sem, 1
            assert inc == chain_sem, (eng, inc)
        if inc is not None:
            self.c[inc] += inck
            v = self.c[inc]

            def fn(e, b=builder, s=inc, k=inck):
                b(e).then_inc(self.sems[s], k)
            self.prog[eng].append(fn)
            if chain_sem is not None and not dma_like:
                self._last[eng] = v
            return v
        self.prog[eng].append(lambda e, b=builder: b(e))
        return None

    def ack(self, eng, qsem, asem):
        self.wait(eng, qsem, self.c[qsem])
        self.c[asem] += 1
        v = self.c[asem]
        self.prog[eng].append(lambda e, s=asem: e.sem_inc(self.sems[s], 1))
        return v

    def barrier(self, dma_flush=()):
        self.nbar += 1
        target = 5 * self.nbar
        flush = {}
        for eng, s in dma_flush:
            flush.setdefault(eng, []).append((s, self.c[s]))
        for eng in ENGINES:
            for s, v in flush.get(eng, []):
                self.wait(eng, s, v)
            self.prog[eng].append(lambda e: e.sem_inc(self.sems["bar"], 1))
            self.wait(eng, "bar", target)


def build_nc(debug=False, preamble="clear_sled"):
    nc = bass.Bass("TRN2", target_bir_lowering=False)

    def din(name, shape, dtype=F32):
        return nc.dram_tensor(name, shape, dtype, kind="ExternalInput").ap()

    resid = din("resid", [LC, D])
    nw_rep = din("nw_rep", [128, D])
    inwT = din("inwT", [D, EC])
    skwT = din("skwT", [D, EC])
    wd1T = din("wd1T", [EC, R])
    wd2T = din("wd2T", [R, EC])
    wbcT = din("wbcT", [EC, 32])
    evec = din("evec", [EC, 24])
    sel_in = din("sel_in", [32, 32 * 128], BF16)
    wguT_sl = din("wguT_sl", [128, 32768])
    wdnT_sl = din("wdnT_sl", [I // NCORE, D])
    owT_sl = din("owT_sl", [E // NCORE, D])
    out = nc.dram_tensor("out", [LC, D], F32, kind="ExternalOutput").ap()

    ag_rn_in = nc.dram_tensor("ag_rn_in", [D, LC], BF16)
    RNT_ALL = nc.dram_tensor("RNT_ALL", [NCORE * D, LC], BF16, addr_space="Shared")
    XNT_SP = nc.dram_tensor("XNT_SP", [D, LC], BF16)
    ag_wgu_in = nc.dram_tensor("ag_wgu_in", [128, 32768], BF16)
    WGUT = nc.dram_tensor("WGUT", [128, 262144], BF16, addr_space="Shared")
    ag_wdn_in = nc.dram_tensor("ag_wdn_in", [I // NCORE, D], BF16)
    WDNT = nc.dram_tensor("WDNT", [I, D], BF16, addr_space="Shared")
    ag_ow_in = nc.dram_tensor("ag_ow_in", [E // NCORE, D], BF16)
    OWT = nc.dram_tensor("OWT", [E, D], BF16, addr_space="Shared")
    ar_in = nc.dram_tensor("ar_in", [R + 32, L], F32)
    ar_out = nc.dram_tensor("ar_out", [R + 32, L], F32, addr_space="Shared")
    ag_y_in = nc.dram_tensor("ag_y_in", [EC, L], BF16)
    Y_ALL = nc.dram_tensor("Y_ALL", [E, L], BF16, addr_space="Shared")
    XN_SP = nc.dram_tensor("XN_SP", [LC, D], F32)
    SK2_SP = nc.dram_tensor("SK2_SP", [EC, L], F32)
    OUT2_SP = nc.dram_tensor("OUT2_SP", [LC, D], F32)
    X_SP = nc.dram_tensor("X_SP", [EC, L], BF16)
    H_SP = nc.dram_tensor("H_SP", [I, LC], BF16)

    dbg = {}
    if debug:
        for nm, shape, dt in [
            ("dbg_xn", [LC, D], F32), ("dbg_rn", [LC, D], BF16),
            ("dbg_d1bc", [R + 32, L], F32), ("dbg_x", [EC, L], BF16),
            ("dbg_sk2", [EC, L], F32), ("dbg_delta", [EC, L], F32),
            ("dbg_y", [EC, L], BF16), ("dbg_out2", [LC, D], F32),
            ("dbg_scan", [128, 6 * CH], F32),
        ]:
            dbg[nm] = nc.dram_tensor(nm, shape, dt, kind="ExternalOutput").ap()

    semnames = ["bar", "dsy", "dgp", "dsc", "asy", "agp", "asc", "mm", "vv", "aa", "gg",
                "fw0", "fw1", "fw2", "fw3", "sw0", "sw1", "sw2", "sw3",
                "cc_rn", "cc_wgu", "cc_wdn", "cc_ow", "cc_ar", "cc_y"]

    with contextlib.ExitStack() as g:
        p = P(nc, g, semnames)
        W, EM = p.wait, p.emit

        def gack():
            return p.ack("gpsimd", "dgp", "agp")

        def sack():
            return p.ack("sync", "dsy", "asy")

        nums = sorted(s.num for s in p.sems.values())
        assert nums == list(range(nums[0], nums[0] + len(nums))), nums
        if preamble == "clear_sled":
            for _ in range(40):
                nc.gpsimd.nop(cycle_cnt=2400)
            nc.gpsimd.sem_clear(range(nums[0], nums[-1] + 1))
            for eng in (nc.sync, nc.vector, nc.scalar, nc.tensor):
                for _ in range(120):
                    eng.nop(cycle_cnt=2400)
            nc.all_engine_barrier()

        sbg = lambda name, shape, dtype=F32: g.enter_context(nc.sbuf_tensor(name, shape, dtype))
        evs = sbg("evs", [128, NET * 24])
        asb = sbg("asb", [128, NET * 16])
        carry = sbg("carry", [128, NET * N])

        def ecol(et, c):
            return evs[:, et * 24 + c:et * 24 + c + 1]

        with contextlib.ExitStack() as ph:
            sb = lambda name, shape, dtype=F32: ph.enter_context(nc.sbuf_tensor(name, shape, dtype))
            nwr = sb("nwr", [128, D])
            rsd = [sb(f"rsd{i}", [128, D]) for i in range(2)]
            sq_scr = sb("sq_scr", [128, D])
            xn_t = [sb(f"xn_t{i}", [128, D]) for i in range(2)]
            rn_bf = [sb(f"rn_bf{i}", [128, D], BF16) for i in range(2)]
            xn_bf = [sb(f"xn_bf{i}", [128, D], BF16) for i in range(2)]
            rnT_sb = [sb(f"rnT_sb{i}", [128, D], BF16) for i in range(2)]
            xnT_sb = [sb(f"xnT_sb{i}", [128, D], BF16) for i in range(2)]
            idn = sb("idn", [128, 128], BF16)
            stat = sb("stat", [128, 8])
            wf32 = sb("wf32", [128, 4 * 2048])
            wbf = sb("wbf", [128, 4 * 2048], BF16)
            pt = [ph.enter_context(nc.psum_tensor(f"pt{i}", [128, 128], BF16)) for i in range(4)]

            EM("gpsimd", lambda e: e.dma_start(out=nwr[:, :], in_=nw_rep), inc="dgp", inck=16)
            EM("gpsimd", lambda e: e.dma_start(
                out=evs[:, :].rearrange("p (t c) -> p t c", t=NET),
                in_=evec.rearrange("(t p) c -> p t c", p=128)), inc="dgp", inck=16)
            a_ev = gack()
            EM("gpsimd", lambda e: e.memset(idn[:, :], 0.0), inc="gg")
            idv = EM("gpsimd", lambda e: e.affine_select(
                out=idn[:, :], in_=idn[:, :], compare_op=OP.not_equal, fill=1.0,
                base=0, pattern=[[-1, 128]], channel_multiplier=1), inc="gg")
            W("scalar", "agp", a_ev)
            EM("scalar", lambda e: e.activation(
                asb[:, :].rearrange("p (t n) -> p t n", t=NET),
                evs[:, :].rearrange("p (t c) -> p t c", t=NET)[:, :, 8:24], AF.Exp), inc="aa")
            W("vector", "aa", p.c["aa"])
            EM("vector", lambda e: e.tensor_scalar_mul(asb[:, :], asb[:, :], -1.0), inc="vv")

            cast_srcs = []
            for k in range(16):
                cast_srcs.append((wguT_sl, ag_wgu_in, 0, k * 2048))
            for rt in range(8):
                cast_srcs.append((wdnT_sl, ag_wdn_in, rt * 128, 0))
            for rt in range(4):
                cast_srcs.append((owT_sl, ag_ow_in, rt * 128, 0))
            castv, swc = {}, {}
            cu = [0]

            def emit_cast_unit():
                k = cu[0]
                if k >= len(cast_srcs):
                    return
                cu[0] += 1
                s = k % 4
                src_t, dst_t, r0, c0 = cast_srcs[k]
                if (k - 4) in castv:
                    W("gpsimd", "vv", castv[k - 4])
                EM("gpsimd", lambda e, src_t=src_t, r0=r0, c0=c0, s=s: e.dma_start(
                    out=wf32[:, s * 2048:(s + 1) * 2048],
                    in_=src_t[r0:r0 + 128, c0:c0 + 2048]), inc=f"fw{s}", inck=16)
                fcnt = p.c[f"fw{s}"]
                W("vector", f"fw{s}", fcnt)
                if (k - 4) in swc:
                    W("vector", f"sw{s}", swc[k - 4])
                castv[k] = EM("vector", lambda e, s=s: e.tensor_copy(
                    wbf[:, s * 2048:(s + 1) * 2048], wf32[:, s * 2048:(s + 1) * 2048]), inc="vv")
                W("sync", "vv", castv[k])
                EM("sync", lambda e, dst_t=dst_t, r0=r0, c0=c0, s=s: e.dma_start(
                    out=dst_t.ap()[r0:r0 + 128, c0:c0 + 2048],
                    in_=wbf[:, s * 2048:(s + 1) * 2048]), inc=f"sw{s}", inck=16)
                swc[k] = p.c[f"sw{s}"]

            rsd_ack = {}
            for t in range(2):
                EM("sync", lambda e, t=t: e.dma_start(
                    out=rsd[t % 2][:, :], in_=resid[t * 128:(t + 1) * 128, :]), inc="dsy", inck=16)
                rsd_ack[t] = sack()

            tT_ack = {}
            xnf = {}
            for t in range(8):
                b = t % 2
                W("vector", "asy", rsd_ack[t])
                rs1 = EM("vector", lambda e, b=b: e.reduce_sum(
                    stat[:, 0:1], rsd[b][:, :], axis=mybir.AxisListType.X), inc="vv")
                W("vector", "vv", rs1)
                EM("vector", lambda e: e.tensor_scalar_mul(stat[:, 0:1], stat[:, 0:1], 1.0 / D))
                sqv = EM("vector", lambda e, b=b: e.tensor_mul(sq_scr[:, :], rsd[b][:, :], rsd[b][:, :]),
                         inc="vv")
                W("vector", "vv", sqv)
                rs2 = EM("vector", lambda e: e.reduce_sum(
                    stat[:, 1:2], sq_scr[:, :], axis=mybir.AxisListType.X), inc="vv")
                W("vector", "vv", rs2)
                EM("vector", lambda e: e.tensor_scalar_mul(stat[:, 1:2], stat[:, 1:2], 1.0 / D))
                EM("vector", lambda e: e.tensor_mul(stat[:, 7:8], stat[:, 0:1], stat[:, 0:1]))
                EM("vector", lambda e: e.tensor_sub(stat[:, 2:3], stat[:, 1:2], stat[:, 7:8]))
                vv1 = EM("vector", lambda e: e.tensor_scalar(
                    stat[:, 3:4], stat[:, 2:3], 1.0, EPS, OP.mult, OP.add), inc="vv")
                W("scalar", "vv", vv1)
                EM("scalar", lambda e: e.activation(stat[:, 4:5], stat[:, 3:4], AF.Ln))
                aa1 = EM("scalar", lambda e: e.activation(stat[:, 5:6], stat[:, 4:5], AF.Exp, scale=-0.5),
                         inc="aa")
                aa1b = EM("scalar", lambda e: e.activation(stat[:, 6:7], stat[:, 4:5], AF.Exp, scale=-1.0),
                          inc="aa")
                W("vector", "aa", aa1b)
                if t >= 2:
                    W("vector", "asy", xnf[t - 2])
                xnv = EM("vector", lambda e, b=b: e.tensor_scalar(
                    xn_t[b][:, :], rsd[b][:, :], stat[:, 0:1], stat[:, 5:6], OP.subtract, OP.mult), inc="vv")
                if t + 2 < 8:
                    W("sync", "vv", xnv)
                    EM("sync", lambda e, t=t: e.dma_start(
                        out=rsd[t % 2][:, :], in_=resid[(t + 2) * 128:(t + 3) * 128, :]), inc="dsy", inck=16)
                    rsd_ack[t + 2] = sack()
                EM("vector", lambda e: e.tensor_mul(stat[:, 6:7], stat[:, 2:3], stat[:, 6:7]))
                vv2 = EM("vector", lambda e: e.tensor_scalar(
                    stat[:, 6:7], stat[:, 6:7], 1.0, EPS, OP.mult, OP.add), inc="vv")
                W("scalar", "vv", vv2)
                EM("scalar", lambda e: e.activation(stat[:, 6:7], stat[:, 6:7], AF.Ln))
                aa2 = EM("scalar", lambda e: e.activation(stat[:, 6:7], stat[:, 6:7], AF.Exp, scale=-0.5),
                         inc="aa")
                W("vector", "aa", aa2)
                rnv = EM("vector", lambda e, b=b: e.scalar_tensor_tensor(
                    rn_bf[b][:, :], xn_t[b][:, :], stat[:, 6:7], nwr[:, :], OP.mult, OP.mult), inc="vv")
                vd = EM("vector", lambda e, b=b: e.tensor_copy(xn_bf[b][:, :], xn_t[b][:, :]), inc="vv")
                W("sync", "vv", vd)
                EM("sync", lambda e, b=b, t=t: e.dma_start(
                    out=XN_SP.ap()[t * 128:(t + 1) * 128, :], in_=xn_t[b][:, :]), inc="dsy", inck=16)
                xnf[t] = sack()
                W("tensor", "vv", vd)
                if t >= 2:
                    W("tensor", "asy", tT_ack[t - 2])
                tcp = {}
                if t == 0:
                    W("tensor", "gg", idv)
                for half, (srcb, dstb) in enumerate([(rn_bf[b], rnT_sb[b]), (xn_bf[b], xnT_sb[b])]):
                    for dt in range(16):
                        j = half * 16 + dt
                        pj = j % 4
                        if (j - 4) in tcp:
                            W("tensor", "aa", tcp[j - 4])
                        tmm = EM("tensor", lambda e, srcb=srcb, dt=dt, pj=pj: e.transpose(
                            pt[pj][:, :], srcb[:, dt * 128:(dt + 1) * 128], idn[:, :]), inc="mm")
                        W("scalar", "mm", tmm)
                        tcp[j] = EM("scalar", lambda e, dstb=dstb, dt=dt, pj=pj: e.activation(
                            dstb[:, dt * 128:(dt + 1) * 128], pt[pj][:, :], AF.Identity), inc="aa")
                W("sync", "aa", tcp[31])
                EM("sync", lambda e, b=b, t=t: e.dma_start(
                    out=ag_rn_in.ap()[:, t * 128:(t + 1) * 128].rearrange("(d p) l -> p d l", p=128),
                    in_=rnT_sb[b][:, :].rearrange("p (d l) -> p d l", d=16)), inc="dsy", inck=16)
                EM("sync", lambda e, b=b, t=t: e.dma_start(
                    out=XNT_SP.ap()[:, t * 128:(t + 1) * 128].rearrange("(d p) l -> p d l", p=128),
                    in_=xnT_sb[b][:, :].rearrange("p (d l) -> p d l", d=16)), inc="dsy", inck=16)
                tT_ack[t] = sack()
                for _ in range(4):
                    emit_cast_unit()

            W("gpsimd", "asy", tT_ack[7])
            EM("gpsimd", lambda e: e.collective_compute(
                "AllGather", OP.bypass, replica_groups=[list(range(NCORE))],
                ins=[ag_rn_in.ap().opt()], outs=[RNT_ALL.ap().opt()]), inc="cc_rn")
            while cu[0] < len(cast_srcs):
                emit_cast_unit()
            for s in range(4):
                W("gpsimd", f"sw{s}", p.c[f"sw{s}"])
            EM("gpsimd", lambda e: e.collective_compute(
                "AllGather", OP.bypass, replica_groups=[list(range(NCORE))],
                ins=[ag_wgu_in.ap().opt()], outs=[WGUT.ap().opt()]), inc="cc_wgu")
            p.barrier(dma_flush=[("gpsimd", "dgp"), ("sync", "dsy")])

        with contextlib.ExitStack() as ph:
            sb = lambda name, shape, dtype=F32: ph.enter_context(nc.sbuf_tensor(name, shape, dtype))
            psb = lambda name, shape: ph.enter_context(nc.psum_tensor(name, shape, F32))
            wstage = sb("wstage", [128, 16 * 512])
            inw_bf = sb("inw_bf", [128, 16 * 512], BF16)
            skw_bf = sb("skw_bf", [128, 16 * 512], BF16)
            wd1_bf = sb("wd1_bf", [128, NET * 128], BF16)
            wbc_bf = sb("wbc_bf", [128, NET * 32], BF16)
            rhs = [sb(f"rhs{i}", [128, 16 * 512], BF16) for i in range(2)]
            xin_buf = [sb(f"xinb{et}", [128, 515]) for et in range(NET)]
            acc = [sb(f"acc{et}", [128, 512]) for et in range(NET)]
            sk2_t = [[sb(f"sk2t{i}_{et}", [128, 512]) for et in range(NET)] for i in range(2)]
            d1e = [sb(f"d1e{i}", [128, 512]) for i in range(2)]
            bce = [sb(f"bce{i}", [128, 512]) for i in range(2)]
            x_bf = sb("x_bf", [128, NET * L], BF16)
            ps_xin = [psb(f"ps_xin{i}", [128, 512]) for i in range(2)]
            ps_skip = [psb(f"ps_skip{i}", [128, 512]) for i in range(2)]
            ps_d1 = psb("ps_d1", [128, 512])
            ps_bc = psb("ps_bc", [128, 512])

            def load_cast(src, w, sl, nfree):
                EM("gpsimd", lambda e, src=src, sl=sl, nfree=nfree: e.dma_start(
                    out=wstage[:, 0:nfree].rearrange("p (t c) -> p t c", t=sl),
                    in_=src.rearrange("(t p) c -> p t c", p=128)), inc="dgp", inck=16)
                a_ld = gack()
                W("vector", "agp", a_ld)
                cv = EM("vector", lambda e, w=w, nfree=nfree: e.tensor_copy(w[:, 0:nfree], wstage[:, 0:nfree]),
                        inc="vv")
                W("gpsimd", "vv", cv)

            load_cast(inwT, inw_bf, 16, 16 * 512)
            load_cast(skwT, skw_bf, 16, 16 * 512)
            load_cast(wd1T, wd1_bf, NET, NET * 128)
            load_cast(wbcT, wbc_bf, NET, NET * 32)
            for et in range(NET - 1):
                EM("vector", lambda e, et=et: e.memset(xin_buf[et][:, 0:3], 0.0))
            v_w = EM("vector", lambda e: e.memset(xin_buf[NET - 1][:, 0:3], 0.0), inc="vv")

            W("sync", "cc_rn", 1)
            NLC = 16
            mm_xin, mm_skip, evac_xin, evac_skip = {}, {}, {}, {}
            xsilu, sk2_ack = {}, {}
            rhs_cnt = {}

            def rhs_load(lc):
                s = lc % 2
                c, hh = lc // 2, lc % 2
                EM("sync", lambda e, s=s, c=c, hh=hh: e.dma_start(
                    out=rhs[s][:, :].rearrange("p (d l) -> p d l", d=16),
                    in_=RNT_ALL.ap()[c * D:(c + 1) * D, hh * 512:(hh + 1) * 512]
                    .rearrange("(d p) l -> p d l", p=128)), inc=f"fw{s}", inck=16)
                rhs_cnt[lc] = p.c[f"fw{s}"]

            d1mm, bcmm, d1ev, bcev, d1_ack = {}, {}, {}, {}, {}
            rhs_load(0)
            for lc in range(NLC):
                b = lc % 2
                if lc + 1 < NLC:
                    if lc >= 1:
                        W("sync", "mm", mm_skip[(lc - 1, NET - 1)])
                    rhs_load(lc + 1)
                W("tensor", f"fw{b}", rhs_cnt[lc])
                if lc == 0:
                    W("tensor", "vv", v_w)
                for et in range(NET):
                    pb = et % 2
                    prev = (lc, et - 2) if et >= 2 else (lc - 1, et + 2)
                    if prev in evac_xin:
                        W("tensor", "vv", evac_xin[prev])
                    for dt in range(16):
                        mm_xin[(lc, et)] = EM("tensor", lambda e, et=et, dt=dt, b=b, pb=pb: e.matmul(
                            ps_xin[pb][:, :], inw_bf[:, dt * 512 + et * 128:dt * 512 + (et + 1) * 128],
                            rhs[b][:, dt * 512:(dt + 1) * 512], start=(dt == 0), stop=(dt == 15)),
                            inc="mm" if dt == 15 else None)
                    if prev in evac_skip:
                        W("tensor", "aa", evac_skip[prev])
                    for dt in range(16):
                        mm_skip[(lc, et)] = EM("tensor", lambda e, et=et, dt=dt, b=b, pb=pb: e.matmul(
                            ps_skip[pb][:, :], skw_bf[:, dt * 512 + et * 128:dt * 512 + (et + 1) * 128],
                            rhs[b][:, dt * 512:(dt + 1) * 512], start=(dt == 0), stop=(dt == 15)),
                            inc="mm" if dt == 15 else None)

                    W("vector", "mm", mm_xin[(lc, et)])
                    if (lc - 1, et) in xsilu:
                        W("vector", "aa", xsilu[(lc - 1, et)])
                    evv = EM("vector", lambda e, et=et, pb=pb: e.tensor_copy(
                        xin_buf[et][:, 3:515], ps_xin[pb][:, :]), inc="vv")
                    W("vector", "vv", evv)
                    cv0 = EM("vector", lambda e, et=et: e.tensor_scalar(
                        acc[et][:, :], xin_buf[et][:, 0:512], ecol(et, 0), None, OP.mult), inc="vv")
                    for k in range(1, 4):
                        W("vector", "vv", cv0)
                        cv0 = EM("vector", lambda e, et=et, k=k: e.scalar_tensor_tensor(
                            acc[et][:, :], xin_buf[et][:, k:k + 512], ecol(et, k),
                            acc[et][:, :], OP.mult, OP.add), inc="vv")
                    evac_xin[(lc, et)] = cv0
                    EM("vector", lambda e, et=et: e.tensor_copy(
                        xin_buf[et][:, 0:3], xin_buf[et][:, 512:515]), inc="vv")
                    W("scalar", "vv", evac_xin[(lc, et)])
                    xsilu[(lc, et)] = EM("scalar", lambda e, et=et, lc=lc: e.activation(
                        x_bf[:, et * L + lc * 512:et * L + (lc + 1) * 512], acc[et][:, :], AF.Silu,
                        bias=ecol(et, 4)), inc="aa")
                    W("scalar", "mm", mm_skip[(lc, et)])
                    if (lc - 2, et) in sk2_ack:
                        W("scalar", "agp", sk2_ack[(lc - 2, et)])
                    evac_skip[(lc, et)] = EM("scalar", lambda e, et=et, pb=pb, b=b: e.activation(
                        sk2_t[b][et][:, :], ps_skip[pb][:, :], AF.Silu), inc="aa")
                    W("gpsimd", "aa", evac_skip[(lc, et)])
                    EM("gpsimd", lambda e, et=et, lc=lc, b=b: e.dma_start(
                        out=SK2_SP.ap()[et * 128:(et + 1) * 128, lc * 512:(lc + 1) * 512],
                        in_=sk2_t[b][et][:, :]), inc="dgp", inck=16)
                    sk2_ack[(lc, et)] = gack()

                W("tensor", "aa", xsilu[(lc, NET - 1)])
                if lc >= 1:
                    W("tensor", "vv", d1ev[lc - 1])
                for et in range(NET):
                    d1mm[lc] = EM("tensor", lambda e, et=et, lc=lc: e.matmul(
                        ps_d1[:, :], wd1_bf[:, et * 128:(et + 1) * 128],
                        x_bf[:, et * L + lc * 512:et * L + (lc + 1) * 512],
                        start=(et == 0), stop=(et == NET - 1)),
                        inc="mm" if et == NET - 1 else None)
                if lc >= 1:
                    W("tensor", "vv", bcev[lc - 1])
                for et in range(NET):
                    bcmm[lc] = EM("tensor", lambda e, et=et, lc=lc: e.matmul(
                        ps_bc[0:32, :], wbc_bf[:, et * 32:(et + 1) * 32],
                        x_bf[:, et * L + lc * 512:et * L + (lc + 1) * 512],
                        start=(et == 0), stop=(et == NET - 1)),
                        inc="mm" if et == NET - 1 else None)
                W("vector", "mm", d1mm[lc])
                if lc >= 2:
                    W("vector", "agp", d1_ack[lc - 2])
                d1ev[lc] = EM("vector", lambda e, b=b: e.tensor_copy(d1e[b][:, :], ps_d1[:, :]), inc="vv")
                W("vector", "mm", bcmm[lc])
                bcev[lc] = EM("vector", lambda e, b=b: e.tensor_copy(bce[b][0:32, :], ps_bc[0:32, :]),
                              inc="vv")
                W("gpsimd", "vv", bcev[lc])
                EM("gpsimd", lambda e, lc=lc, b=b: e.dma_start(
                    out=ar_in.ap()[0:R, lc * 512:(lc + 1) * 512], in_=d1e[b][:, :]), inc="dgp", inck=16)
                EM("gpsimd", lambda e, lc=lc, b=b: e.dma_start(
                    out=ar_in.ap()[R:R + 32, lc * 512:(lc + 1) * 512], in_=bce[b][0:32, :]), inc="dgp", inck=16)
                d1_ack[lc] = gack()

            x_ready = p.c["aa"]
            W("gpsimd", "dgp", p.c["dgp"])
            EM("gpsimd", lambda e: e.collective_compute(
                "AllReduce", OP.add, replica_groups=[list(range(NCORE))],
                ins=[ar_in.ap().opt()], outs=[ar_out.ap().opt()]), inc="cc_ar")
            EM("gpsimd", lambda e: e.collective_compute(
                "AllGather", OP.bypass, replica_groups=[list(range(NCORE))],
                ins=[ag_wdn_in.ap().opt()], outs=[WDNT.ap().opt()]), inc="cc_wdn")
            EM("gpsimd", lambda e: e.collective_compute(
                "AllGather", OP.bypass, replica_groups=[list(range(NCORE))],
                ins=[ag_ow_in.ap().opt()], outs=[OWT.ap().opt()]), inc="cc_ow")
            W("gpsimd", "aa", x_ready)
            for et in range(NET):
                EM("gpsimd", lambda e, et=et: e.dma_start(
                    out=X_SP.ap()[et * 128:(et + 1) * 128, :], in_=x_bf[:, et * L:(et + 1) * L]),
                    inc="dgp", inck=16)
            p.barrier(dma_flush=[("gpsimd", "dgp"), ("sync", "dsy")])

        with contextlib.ExitStack() as ph:
            sb = lambda name, shape, dtype=F32: ph.enter_context(nc.sbuf_tensor(name, shape, dtype))
            psb = lambda name, shape: ph.enter_context(nc.psum_tensor(name, shape, F32))
            d1t_f = sb("d1t_f", [128, CH])
            d1t_bf = sb("d1t_bf", [128, CH], BF16)
            bmcs = sb("bmcs", [32, CH])
            bmc_bf = sb("bmc_bf", [32, 2 * CH], BF16)
            wd2_bf = sb("wd2_bf", [128, EC], BF16)
            delta = sb("delta", [128, NET * CH])
            xc_bf = [sb(f"xc_bf{i}", [128, NET * CH], BF16) for i in range(2)]
            u_bf = sb("u_bf", [128, NET * CH], BF16)
            y_acc = sb("y_acc", [128, NET * CH])
            sk2c = [sb(f"sk2c{i}", [128, CH]) for i in range(2)]
            p_buf = sb("p_buf", [128, 2 * CH])
            ug_buf = sb("ug_buf", [128, 2 * CH], BF16)
            h_buf = sb("h_buf", [128, 2 * CH])
            prod = [sb(f"prod{i}", [128, CH]) for i in range(3)]
            brep = sb("brep", [128, 2 * CH], BF16)
            crep = sb("crep", [128, 2 * CH], BF16)
            t0 = sb("t0", [128, CH])
            t1 = sb("t1", [128, CH])
            y_bf = [sb(f"y_bf{i}", [128, CH], BF16) for i in range(2)]
            sel_bf = sb("sel_bf", [32, 32 * 128], BF16)
            ps_dp = psb("ps_dp", [128, CH])
            ps_b = psb("ps_b", [128, 512])
            ps_c = psb("ps_c", [128, 512])
            xnt = sb("xnt", [128, 16 * 512], BF16)
            wg_g = [sb(f"wg_g{i}", [128, 16 * 128], BF16) for i in range(3)]
            wg_u = [sb(f"wg_u{i}", [128, 16 * 128], BF16) for i in range(3)]
            sg = [sb(f"sg{i}", [128, 512]) for i in range(2)]
            h_st = [sb(f"h_st{i}", [128, 512], BF16) for i in range(2)]
            h_in = [sb(f"h_in{i}", [128, 2 * 512], BF16) for i in range(3)]
            wdn_t = [sb(f"wdn_t{i}", [128, 2 * 512], BF16) for i in range(3)]
            o2_t = [sb(f"o2_t{i}", [128, 512]) for i in range(2)]
            pf = [psb(f"pf{i}", [128, 512]) for i in range(4)]

            EM("gpsimd", lambda e: e.dma_start(out=sel_bf[:, :], in_=sel_in), inc="dgp", inck=16)
            EM("gpsimd", lambda e: e.dma_start(out=d1t_f[:, 0:EC], in_=wd2T), inc="dgp", inck=16)
            a_w = gack()
            W("vector", "agp", a_w)
            wcv = EM("vector", lambda e: e.tensor_copy(wd2_bf[:, :], d1t_f[:, 0:EC]), inc="vv")
            W("sync", "cc_ar", 1)
            W("sync", "vv", wcv)
            W("tensor", "agp", a_w)
            W("tensor", "vv", wcv)
            xnt_ack = [None]

            dp_mm, dp_chain, delta_ok, pexp = {}, {}, {}, {}
            brep_ok = {}
            ug_done, tts_done, prod_done, add_done = {}, {}, {}, {}
            stt_done, ymul_done, ybf_ack, sk_ack = {}, {}, {}, {}
            d1_cast, xc_ack = {}, {}
            prod_war = {}
            gu_mm, silu_done, hmult_done, hst_ack = {}, {}, {}, {}
            dn_mm, o2_evac, o2_ack = {}, {}, {}
            dn_state = {"gpr": 0}

            def gate_units(lh):
                if lh == 1:
                    W("sync", "mm", gu_mm[(0, 63)])
                for it in range(64):
                    if it == 0:
                        EM("sync", lambda e, lh=lh: e.dma_start(
                            out=xnt[:, :].rearrange("p (d l) -> p d l", d=16),
                            in_=XNT_SP.ap()[:, lh * 512:(lh + 1) * 512]
                            .rearrange("(d p) l -> p d l", p=128)), inc="dsy", inck=16)
                        xnt_ack[0] = sack()
                        if lh == 0 and it == 0:
                            pass
                        W("tensor", "asy", xnt_ack[0])
                    b = it % 3
                    b2 = it % 2
                    pg, pu = pf[2 * b2], pf[2 * b2 + 1]
                    prevk = (lh, it - 2) if it >= 2 else ((lh - 1, 62 + it) if lh >= 1 else None)
                    prevk3 = (lh, it - 3) if it >= 3 else ((lh - 1, 61 + it) if lh >= 1 else None)
                    if lh == 0 and it == 0:
                        W("scalar", "cc_wgu", 1)
                    if prevk3 in gu_mm:
                        W("scalar", "mm", gu_mm[prevk3])
                    EM("scalar", lambda e, it=it, b=b: e.dma_start(
                        out=wg_g[b][:, :],
                        in_=WGUT.ap()[it:it + 1, :].rearrange("r (p c) -> (r p) c", p=128)),
                        inc=f"fw{b}", inck=16)
                    EM("scalar", lambda e, it=it, b=b: e.dma_start(
                        out=wg_u[b][:, :],
                        in_=WGUT.ap()[64 + it:65 + it, :].rearrange("r (p c) -> (r p) c", p=128)),
                        inc=f"fw{b}", inck=16)
                    W("tensor", f"fw{b}", p.c[f"fw{b}"])
                    if prevk in silu_done:
                        W("tensor", "aa", silu_done[prevk])
                        W("tensor", "vv", hmult_done[prevk])
                    for dt in range(16):
                        EM("tensor", lambda e, dt=dt, b=b, pg=pg: e.matmul(
                            pg[:, :], wg_g[b][:, dt * 128:(dt + 1) * 128],
                            xnt[:, dt * 512:(dt + 1) * 512], start=(dt == 0), stop=(dt == 15)))
                    for dt in range(16):
                        gu_mm[(lh, it)] = EM("tensor", lambda e, dt=dt, b=b, pu=pu: e.matmul(
                            pu[:, :], wg_u[b][:, dt * 128:(dt + 1) * 128],
                            xnt[:, dt * 512:(dt + 1) * 512], start=(dt == 0), stop=(dt == 15)),
                            inc="mm" if dt == 15 else None)
                    W("scalar", "mm", gu_mm[(lh, it)])
                    if prevk in hmult_done:
                        W("scalar", "vv", hmult_done[prevk])
                    silu_done[(lh, it)] = EM("scalar", lambda e, b2=b2, pg=pg: e.activation(
                        sg[b2][:, :], pg[:, :], AF.Silu), inc="aa")
                    W("vector", "aa", silu_done[(lh, it)])
                    if prevk in hst_ack:
                        W("vector", "asy", hst_ack[prevk])
                    hmult_done[(lh, it)] = EM("vector", lambda e, b2=b2, pu=pu: e.tensor_mul(
                        h_st[b2][:, :], sg[b2][:, :], pu[:, :]), inc="vv")
                    W("sync", "vv", hmult_done[(lh, it)])
                    EM("sync", lambda e, it=it, b2=b2, lh=lh: e.dma_start(
                        out=H_SP.ap()[it * 128:(it + 1) * 128, lh * 512:(lh + 1) * 512],
                        in_=h_st[b2][:, :]), inc="dsy", inck=16)
                    hst_ack[(lh, it)] = sack()
                    yield

            def down_units():
                for lh in range(2):
                    for dc in range(4):
                        for pr in range(32):
                            it = 2 * pr
                            gpr = dn_state["gpr"]
                            dn_state["gpr"] += 1
                            b = gpr % 3
                            if (gpr - 3) in dn_mm:
                                W("scalar", "mm", dn_mm[gpr - 3])
                            if dc == 0 and pr == 0:
                                W("scalar", "cc_wdn", 1)
                                W("scalar", "asy", hst_ack[(lh, 63)])
                            EM("scalar", lambda e, it=it, dc=dc, b=b: e.dma_start(
                                out=wdn_t[b][:, :].rearrange("p (i c) -> p i c", i=2),
                                in_=WDNT.ap()[it * 128:(it + 2) * 128, dc * 512:(dc + 1) * 512]
                                .rearrange("(i p) c -> p i c", p=128)), inc=f"fw{b}", inck=16)
                            EM("scalar", lambda e, it=it, lh=lh, b=b: e.dma_start(
                                out=h_in[b][:, :].rearrange("p (i c) -> p i c", i=2),
                                in_=H_SP.ap()[it * 128:(it + 2) * 128, lh * 512:(lh + 1) * 512]
                                .rearrange("(i p) c -> p i c", p=128)), inc=f"fw{b}", inck=16)
                            W("tensor", f"fw{b}", p.c[f"fw{b}"])
                            if it == 0:
                                if lh == 0 and dc == 0:
                                    W("tensor", "vv", hmult_done[(1, 63)])
                                    W("tensor", "aa", silu_done[(1, 63)])
                                else:
                                    W("tensor", "aa", o2_evac[3])
                            for ii in range(2):
                                for lt2 in range(4):
                                    dn_mm[gpr] = EM(
                                        "tensor", lambda e, ii=ii, lt2=lt2, b=b, it=it: e.matmul(
                                            pf[lt2][:, :],
                                            h_in[b][:, ii * 512 + lt2 * 128:ii * 512 + (lt2 + 1) * 128],
                                            wdn_t[b][:, ii * 512:(ii + 1) * 512],
                                            start=(it + ii == 0), stop=(it + ii == 63)),
                                        inc="mm" if (ii == 1 and lt2 == 3) else None)
                            yield
                        W("scalar", "mm", dn_mm[dn_state["gpr"] - 1])
                        for lt2 in range(4):
                            ob = lt2 % 2
                            okey = (lt2 - 2) if lt2 >= 2 else (lt2 + 2)
                            if okey in o2_ack:
                                W("scalar", "asy", o2_ack[okey])
                            o2_evac[lt2] = EM("scalar", lambda e, lt2=lt2, ob=ob: e.activation(
                                o2_t[ob][:, :], pf[lt2][:, :], AF.Identity), inc="aa")
                            W("sync", "aa", o2_evac[lt2])
                            EM("sync", lambda e, lh=lh, dc=dc, lt2=lt2, ob=ob: e.dma_start(
                                out=OUT2_SP.ap()[lh * 512 + lt2 * 128:lh * 512 + (lt2 + 1) * 128,
                                                 dc * 512:(dc + 1) * 512], in_=o2_t[ob][:, :]),
                                inc="dsy", inck=16)
                            o2_ack[lt2] = sack()
                        yield

            import itertools as _it
            ffn_iter = _it.chain(gate_units(0), gate_units(1), down_units())
            ffn_left = [64 + 64 + 2 * 4 * 33]

            def pull():
                if ffn_left[0] > 0:
                    next(ffn_iter)
                    ffn_left[0] -= 1

            bmc_cp, bc_mm_last = {}, {}
            a_bmc_d, a_d1_d, a_xc_d = {}, {}, {}

            def chunk_loads(cnk):
                c0 = cnk * CH
                if cnk >= 1:
                    W("sync", "vv", bmc_cp[cnk - 1])
                EM("sync", lambda e, c0=c0: e.dma_start(
                    out=bmcs[:, :], in_=ar_out.ap()[R:R + 32, c0:c0 + CH]), inc="dsy", inck=16)
                a_bmc_d[cnk] = sack()
                if cnk >= 1:
                    W("sync", "vv", d1_cast[cnk - 1])
                EM("sync", lambda e, c0=c0: e.dma_start(
                    out=d1t_f[:, :], in_=ar_out.ap()[0:R, c0:c0 + CH]), inc="dsy", inck=16)
                a_d1_d[cnk] = sack()
                if cnk >= 2:
                    W("sync", "vv", stt_done[(cnk - 2, NET - 1)])
                EM("sync", lambda e, c0=c0, xb=cnk % 2: e.dma_start(
                    out=xc_bf[xb][:, :].rearrange("p (et l) -> p et l", et=NET),
                    in_=X_SP.ap()[:, c0:c0 + CH].rearrange("(et p) l -> p et l", p=128)),
                    inc="dsy", inck=16)
                a_xc_d[cnk] = sack()

            chunk_loads(0)
            giter = [0]
            for cnk in range(NCH):
                c0 = cnk * CH
                cb2 = cnk % 2
                xb = cnk % 2
                W("vector", "asy", a_bmc_d[cnk])
                if cnk >= 2:
                    W("vector", "mm", bc_mm_last[cnk - 2])
                bmc_cp[cnk] = EM("vector", lambda e, cb2=cb2: e.tensor_copy(
                    bmc_bf[:, cb2 * CH:(cb2 + 1) * CH], bmcs[:, :]), inc="vv")
                W("vector", "asy", a_d1_d[cnk])
                if cnk >= 1:
                    W("vector", "mm", dp_mm[(cnk - 1, NET - 1)])
                d1_cast[cnk] = EM("vector", lambda e: e.tensor_copy(d1t_bf[:, :], d1t_f[:, :]), inc="vv")
                W("tensor", "vv", d1_cast[cnk])
                for et in range(NET):
                    pk = (cnk, et - 1) if et >= 1 else (cnk - 1, NET - 1)
                    if pk in dp_chain:
                        W("tensor", "aa", dp_chain[pk])
                    for hh in range(2):
                        dp_mm[(cnk, et)] = EM("tensor", lambda e, et=et, hh=hh: e.matmul(
                            ps_dp[:, hh * 512:(hh + 1) * 512], wd2_bf[:, et * 128:(et + 1) * 128],
                            d1t_bf[:, hh * 512:(hh + 1) * 512], start=True, stop=True),
                            inc="mm" if hh == 1 else None)
                    W("scalar", "mm", dp_mm[(cnk, et)])
                    if pk in delta_ok:
                        W("scalar", "vv", delta_ok[pk])
                    EM("scalar", lambda e, et=et: e.activation(t0[:, :], ps_dp[:, :], AF.Relu, bias=ecol(et, 5)))
                    EM("scalar", lambda e, et=et: e.activation(t1[:, :], ps_dp[:, :], AF.Abs, bias=ecol(et, 5)))
                    EM("scalar", lambda e: e.activation(t1[:, :], t1[:, :], AF.Exp, scale=-1.0))
                    dp_chain[(cnk, et)] = EM("scalar", lambda e: e.activation(t1[:, :], t1[:, :], AF.Ln, bias=1.0),
                                             inc="aa")
                    W("vector", "aa", dp_chain[(cnk, et)])
                    if cnk >= 1:
                        W("vector", "aa", pexp[(cnk - 1, N - 1, et)])
                    delta_ok[(cnk, et)] = EM("vector", lambda e, et=et: e.tensor_add(
                        delta[:, et * CH:(et + 1) * CH], t0[:, :], t1[:, :]), inc="vv")
                W("vector", "asy", a_xc_d[cnk])
                for et in range(NET):
                    EM("vector", lambda e, et=et, xb=xb: e.tensor_mul(
                        u_bf[:, et * CH:(et + 1) * CH], delta[:, et * CH:(et + 1) * CH],
                        xc_bf[xb][:, et * CH:(et + 1) * CH]), inc="vv")

                pending_adds = []

                def flush_add():
                    s2, et2, pb2 = pending_adds.pop(0)
                    W("vector", "gg", prod_done[(cnk, s2)])
                    add_done[(cnk, s2)] = EM("vector", lambda e, et2=et2, pb2=pb2: e.tensor_add(
                        y_acc[:, et2 * CH:(et2 + 1) * CH], y_acc[:, et2 * CH:(et2 + 1) * CH],
                        prod[pb2][:, :]), inc="vv")
                    prod_war[pb2] = add_done[(cnk, s2)]

                for n in range(N):
                    nb = n % 2
                    gk = (cnk, n - 2) if n >= 2 else ((cnk - 1, N - 2 + n) if cnk >= 1 else None)
                    if gk is not None and gk in brep_ok:
                        s_last = (gk[1]) * NET + NET - 1
                        W("tensor", "vv", ug_done[(gk[0], s_last)])
                        W("tensor", "gg", prod_done[(gk[0], s_last)])
                    pk = (cnk, n - 1) if n >= 1 else ((cnk - 1, N - 1) if cnk >= 1 else None)
                    if pk in brep_ok:
                        W("tensor", "aa", brep_ok[pk][1])
                    if n == 0:
                        W("tensor", "vv", bmc_cp[cnk])
                    m1 = EM("tensor", lambda e, n=n, cb2=cb2: e.matmul(
                        ps_b[:, :], sel_bf[0:32, n * 128:(n + 1) * 128],
                        bmc_bf[0:32, cb2 * CH:cb2 * CH + 512], start=True, stop=True), inc="mm")
                    m2 = EM("tensor", lambda e, n=n, cb2=cb2: e.matmul(
                        ps_c[:, :], sel_bf[0:32, n * 128:(n + 1) * 128],
                        bmc_bf[0:32, cb2 * CH + 512:(cb2 + 1) * CH], start=True, stop=True), inc="mm")
                    W("scalar", "mm", m2)
                    if gk is not None and gk in brep_ok:
                        s_last = (gk[1]) * NET + NET - 1
                        W("scalar", "vv", ug_done[(gk[0], s_last)])
                        W("scalar", "gg", prod_done[(gk[0], s_last)])
                    eb0 = EM("scalar", lambda e, nb=nb: e.activation(
                        brep[:, nb * CH:nb * CH + 512], ps_b[:, :], AF.Identity), inc="aa")
                    eb1 = EM("scalar", lambda e, nb=nb: e.activation(
                        brep[:, nb * CH + 512:(nb + 1) * CH], ps_c[:, :], AF.Identity), inc="aa")
                    W("tensor", "aa", eb0)
                    m3 = EM("tensor", lambda e, n=n, cb2=cb2: e.matmul(
                        ps_b[:, :], sel_bf[0:32, (16 + n) * 128:(17 + n) * 128],
                        bmc_bf[0:32, cb2 * CH:cb2 * CH + 512], start=True, stop=True), inc="mm")
                    W("tensor", "aa", eb1)
                    m4 = EM("tensor", lambda e, n=n, cb2=cb2: e.matmul(
                        ps_c[:, :], sel_bf[0:32, (16 + n) * 128:(17 + n) * 128],
                        bmc_bf[0:32, cb2 * CH + 512:(cb2 + 1) * CH], start=True, stop=True), inc="mm")
                    bc_mm_last[cnk] = m4
                    W("scalar", "mm", m4)
                    EM("scalar", lambda e, nb=nb: e.activation(
                        crep[:, nb * CH:nb * CH + 512], ps_b[:, :], AF.Identity), inc="aa")
                    ec1 = EM("scalar", lambda e, nb=nb: e.activation(
                        crep[:, nb * CH + 512:(nb + 1) * CH], ps_c[:, :], AF.Identity), inc="aa")
                    brep_ok[(cnk, n)] = (eb1, ec1)

                    for et in range(NET):
                        s = n * NET + et
                        sb_ = s % 2
                        pb = s % 3
                        cidx = et * N + n
                        if n == 0 and et == 0:
                            W("scalar", "vv", delta_ok[(cnk, NET - 1)])
                        tk = (cnk, s - 2) if s >= 2 else ((cnk - 1, N * NET - 2 + s) if cnk >= 1 else None)
                        if tk in tts_done:
                            W("scalar", "vv", tts_done[tk])
                        pexp[(cnk, n, et)] = EM("scalar", lambda e, et=et, n=n, sb_=sb_: e.activation(
                            p_buf[:, sb_ * CH:(sb_ + 1) * CH], delta[:, et * CH:(et + 1) * CH], AF.Exp,
                            scale=asb[:, et * 16 + n:et * 16 + n + 1]), inc="aa")
                        W("vector", "aa", brep_ok[(cnk, n)][0])
                        ug_done[(cnk, s)] = EM("vector", lambda e, et=et, nb=nb, sb_=sb_: e.tensor_mul(
                            ug_buf[:, sb_ * CH:(sb_ + 1) * CH], u_bf[:, et * CH:(et + 1) * CH],
                            brep[:, nb * CH:(nb + 1) * CH]), inc="vv")
                        W("vector", "aa", pexp[(cnk, n, et)])
                        if tk in prod_done:
                            W("vector", "gg", prod_done[tk])
                        if cnk == 0:
                            tts_done[(cnk, s)] = EM("vector", lambda e, sb_=sb_: e.tensor_tensor_scan(
                                h_buf[:, sb_ * CH:(sb_ + 1) * CH], p_buf[:, sb_ * CH:(sb_ + 1) * CH],
                                ug_buf[:, sb_ * CH:(sb_ + 1) * CH], 0.0, OP.mult, OP.add), inc="vv")
                        else:
                            tts_done[(cnk, s)] = EM("vector", lambda e, sb_=sb_, cidx=cidx: e.tensor_tensor_scan(
                                h_buf[:, sb_ * CH:(sb_ + 1) * CH], p_buf[:, sb_ * CH:(sb_ + 1) * CH],
                                ug_buf[:, sb_ * CH:(sb_ + 1) * CH], carry[:, cidx:cidx + 1],
                                OP.mult, OP.add), inc="vv")
                        EM("vector", lambda e, sb_=sb_, cidx=cidx: e.tensor_copy(
                            carry[:, cidx:cidx + 1], h_buf[:, sb_ * CH + CH - 1:sb_ * CH + CH]))
                        W("gpsimd", "vv", tts_done[(cnk, s)])
                        W("gpsimd", "aa", brep_ok[(cnk, n)][1])
                        if n == 0:
                            prod_done[(cnk, s)] = EM("gpsimd", lambda e, et=et, nb=nb, sb_=sb_: e.tensor_mul(
                                y_acc[:, et * CH:(et + 1) * CH], h_buf[:, sb_ * CH:(sb_ + 1) * CH],
                                crep[:, nb * CH:(nb + 1) * CH]), inc="gg")
                        else:
                            if pb in prod_war:
                                W("gpsimd", "vv", prod_war[pb])
                            prod_done[(cnk, s)] = EM("gpsimd", lambda e, nb=nb, sb_=sb_, pb=pb: e.tensor_mul(
                                prod[pb][:, :], h_buf[:, sb_ * CH:(sb_ + 1) * CH],
                                crep[:, nb * CH:(nb + 1) * CH]), inc="gg")
                            pending_adds.append((s, et, pb))
                        if len(pending_adds) > 2:
                            flush_add()
                        if giter[0] >= 8:
                            pull()
                        if giter[0] % 64 == 48:
                            if cnk + 1 < NCH:
                                chunk_loads(cnk + 1)
                        giter[0] += 1
                while pending_adds:
                    flush_add()

                for et in range(NET):
                    yb = et % 2
                    skkey = (cnk, et - 2) if et >= 2 else ((cnk - 1, et + 2) if cnk >= 1 else None)
                    if skkey in ymul_done:
                        W("sync", "gg", ymul_done[skkey])
                    EM("sync", lambda e, et=et, c0=c0, yb=yb: e.dma_start(
                        out=sk2c[yb][:, :], in_=SK2_SP.ap()[et * 128:(et + 1) * 128, c0:c0 + CH]),
                        inc="dsy", inck=16)
                    sk_ack[(cnk, et)] = sack()
                    stt_done[(cnk, et)] = EM("vector", lambda e, et=et, xb=xb: e.scalar_tensor_tensor(
                        y_acc[:, et * CH:(et + 1) * CH], xc_bf[xb][:, et * CH:(et + 1) * CH], ecol(et, 6),
                        y_acc[:, et * CH:(et + 1) * CH], OP.mult, OP.add), inc="vv")
                    W("gpsimd", "vv", stt_done[(cnk, et)])
                    W("gpsimd", "asy", sk_ack[(cnk, et)])
                    if skkey in ybf_ack:
                        W("gpsimd", "asy", ybf_ack[skkey])
                    ymul_done[(cnk, et)] = EM("gpsimd", lambda e, et=et, yb=yb: e.tensor_mul(
                        y_bf[yb][:, :], y_acc[:, et * CH:(et + 1) * CH], sk2c[yb][:, :]), inc="gg")
                    W("sync", "gg", ymul_done[(cnk, et)])
                    EM("sync", lambda e, et=et, c0=c0, yb=yb: e.dma_start(
                        out=ag_y_in.ap()[et * 128:(et + 1) * 128, c0:c0 + CH], in_=y_bf[yb][:, :]),
                        inc="dsy", inck=16)
                    ybf_ack[(cnk, et)] = sack()

            while ffn_left[0] > 0:
                pull()

            W("gpsimd", "asy", ybf_ack[(NCH - 1, NET - 1)])
            EM("gpsimd", lambda e: e.collective_compute(
                "AllGather", OP.bypass, replica_groups=[list(range(NCORE))],
                ins=[ag_y_in.ap().opt()], outs=[Y_ALL.ap().opt()]), inc="cc_y")
            p.barrier(dma_flush=[("gpsimd", "dgp"), ("sync", "dsy"), ("scalar", "dsc")])

        with contextlib.ExitStack() as ph:
            sb = lambda name, shape, dtype=F32: ph.enter_context(nc.sbuf_tensor(name, shape, dtype))
            psb = lambda name, shape: ph.enter_context(nc.psum_tensor(name, shape, F32))
            y_sb = sb("y_sb", [128, 32 * LC], BF16)
            owt_t = [sb(f"owt_t{i}", [128, 2 * 512], BF16) for i in range(3)]
            xn4 = [sb(f"xn4_{i}", [128, 4 * 512]) for i in range(2)]
            o24 = [sb(f"o24_{i}", [128, 4 * 512]) for i in range(2)]
            rs4 = [sb(f"rs4_{i}", [128, 4 * 512]) for i in range(2)]
            attn4 = [sb(f"attn4_{i}", [128, 4 * 512]) for i in range(2)]
            g4 = [sb(f"g4_{i}", [128, 4 * 512]) for i in range(2)]
            out4 = [sb(f"out4_{i}", [128, 4 * 512]) for i in range(2)]
            ps_o = [psb(f"ps_o{i}", [128, 512]) for i in range(8)]

            W("sync", "cc_y", 1)
            W("scalar", "cc_ow", 1)

            def y_load(e):
                for arm in e.switch_core_id(n=128):
                    lcore = physical_to_logical_core_id(int(arm) % 128, 0, container_num_cores=NCORE) % NCORE
                    c0 = lcore * LC
                    e.dma_start(
                        out=y_sb[:, :].rearrange("p (et l) -> p et l", et=32),
                        in_=Y_ALL.ap()[:, c0:c0 + LC].rearrange("(et p) l -> p et l", p=128),
                    ).then_inc(p.sems["dsy"], 16)
            p.c["dsy"] += 16
            p.prog["sync"].append(y_load)
            yl_ack = sack()

            owp_mm, comb_done, ld_cnt, od_ack = {}, {}, {}, {}
            aev = {}
            g8c = [0]
            for dc in range(4):
                for q in range(2):
                    if (dc - 1, q) in comb_done:
                        W("sync", "vv", comb_done[(dc - 1, q)])
                    for arr, src_ap in ((xn4, XN_SP.ap()), (o24, OUT2_SP.ap()), (rs4, resid)):
                        EM("sync", lambda e, arr=arr, src_ap=src_ap, q=q, dc=dc: e.dma_start(
                            out=arr[q][:, :].rearrange("p (i c) -> p i c", i=4),
                            in_=src_ap[q * 512:(q + 1) * 512, dc * 512:(dc + 1) * 512]
                            .rearrange("(i p) c -> p i c", p=128)), inc=f"sw{q}", inck=16)
                    ld_cnt[(dc, q)] = p.c[f"sw{q}"]
                for pr8 in range(16):
                    g8 = g8c[0]
                    g8c[0] += 1
                    b = g8 % 3
                    et8 = 2 * pr8
                    if (g8 - 3) in owp_mm:
                        W("scalar", "mm", owp_mm[g8 - 3])
                    EM("scalar", lambda e, et8=et8, dc=dc, b=b: e.dma_start(
                        out=owt_t[b][:, :].rearrange("p (i c) -> p i c", i=2),
                        in_=OWT.ap()[et8 * 128:(et8 + 2) * 128, dc * 512:(dc + 1) * 512]
                        .rearrange("(i p) c -> p i c", p=128)), inc=f"fw{b}", inck=16)
                    W("tensor", f"fw{b}", p.c[f"fw{b}"])
                    if pr8 == 0:
                        W("tensor", "asy", yl_ack)
                        if dc > 0:
                            W("tensor", "aa", aev[(dc - 1, 7)])
                    for ii in range(2):
                        for lt in range(8):
                            owp_mm[g8] = EM("tensor", lambda e, et8=et8, ii=ii, b=b, lt=lt: e.matmul(
                                ps_o[lt][:, :],
                                y_sb[:, (et8 + ii) * LC + lt * 128:(et8 + ii) * LC + (lt + 1) * 128],
                                owt_t[b][:, ii * 512:(ii + 1) * 512],
                                start=(et8 + ii == 0), stop=(et8 + ii == 31)),
                                inc="mm" if (ii == 1 and lt == 7) else None)
                W("scalar", "mm", owp_mm[g8c[0] - 1])
                for lt in range(8):
                    q = lt // 4
                    if lt % 4 == 0 and (dc - 1, q) in comb_done:
                        W("scalar", "vv", comb_done[(dc - 1, q)])
                    aev[(dc, lt)] = EM("scalar", lambda e, lt=lt, q=q: e.activation(
                        attn4[q][:, (lt % 4) * 512:(lt % 4 + 1) * 512], ps_o[lt][:, :], AF.Identity),
                        inc="aa")
                    if lt % 4 == 3:
                        W("vector", "aa", aev[(dc, lt)])
                        W("vector", f"sw{q}", ld_cnt[(dc, q)])
                        EM("vector", lambda e, q=q: e.tensor_add(
                            attn4[q][:, :], attn4[q][:, :], xn4[q][:, :]), inc="vv")
                        sgq = EM("scalar", lambda e, q=q: e.activation(
                            g4[q][:, :], o24[q][:, :], AF.Silu), inc="aa")
                        W("vector", "aa", sgq)
                        EM("vector", lambda e, q=q: e.tensor_mul(attn4[q][:, :], attn4[q][:, :], g4[q][:, :]),
                           inc="vv")
                        if (dc - 1, q) in od_ack:
                            W("vector", "asy", od_ack[(dc - 1, q)])
                        cd = EM("vector", lambda e, q=q: e.tensor_add(
                            out4[q][:, :], rs4[q][:, :], attn4[q][:, :]), inc="vv")
                        comb_done[(dc, q)] = cd
                        W("sync", "vv", cd)
                        EM("sync", lambda e, q=q, dc=dc: e.dma_start(
                            out=out[q * 512:(q + 1) * 512, dc * 512:(dc + 1) * 512]
                            .rearrange("(i p) c -> p i c", p=128),
                            in_=out4[q][:, :].rearrange("p (i c) -> p i c", i=4)), inc="dsy", inck=16)
                        od_ack[(dc, q)] = sack()

            if debug:
                for src, dst in [(XN_SP, dbg["dbg_xn"]), (ag_rn_in, dbg["dbg_rn"]),
                                 (ar_out, dbg["dbg_d1bc"]), (SK2_SP, dbg["dbg_sk2"]),
                                 (ag_y_in, dbg["dbg_y"]), (OUT2_SP, dbg["dbg_out2"]),
                                 (X_SP, dbg["dbg_x"])]:
                    EM("gpsimd", lambda e, src=src, dst=dst: e.dma_start(out=dst, in_=src.ap()),
                       inc="dgp", inck=16)
            W("gpsimd", "dgp", p.c["dgp"])
            p.barrier(dma_flush=[("gpsimd", "dgp"), ("sync", "dsy")])

        return _finish(nc, p)


def _finish(nc, p):
    with nc.Block() as block:
        @block.sync
        def _(e):
            for fn in p.prog["sync"]:
                fn(e)

        @block.gpsimd
        def _(e):
            for fn in p.prog["gpsimd"]:
                fn(e)

        @block.vector
        def _(e):
            for fn in p.prog["vector"]:
                fn(e)

        @block.scalar
        def _(e):
            for fn in p.prog["scalar"]:
                fn(e)

        @block.tensor
        def _(e):
            for fn in p.prog["tensor"]:
                fn(e)
    return nc


def make_in_maps(inputs):
    x0 = np.asarray(inputs["L0_attn_resid"]).reshape(L, D)
    norm_w = np.asarray(inputs["L0_attn_norm_w"])
    skip_w = np.asarray(inputs["L0_attn_skip_w"])
    in_w = np.asarray(inputs["L0_attn_in_w"])
    conv_w = np.asarray(inputs["L0_attn_conv_w"])
    conv_b = np.asarray(inputs["L0_attn_conv_b"])
    wd1 = np.asarray(inputs["L0_attn_wd1"])
    wd2 = np.asarray(inputs["L0_attn_wd2"])
    wd2_b = np.asarray(inputs["L0_attn_wd2_b"])
    wB = np.asarray(inputs["L0_attn_wB"])
    wC = np.asarray(inputs["L0_attn_wC"])
    A_log = np.asarray(inputs["L0_attn_A_log"])
    W_D = np.asarray(inputs["L0_attn_W_D"])
    out_w = np.asarray(inputs["L0_attn_out_w"])
    wgu = np.asarray(inputs["L0_ffn_w_gate_up"])
    wdn = np.asarray(inputs["L0_ffn_w_down"])

    nw_rep = np.ascontiguousarray(np.broadcast_to(norm_w[None, :], (128, D)).astype(np.float32))
    wguT = np.ascontiguousarray(wgu.T)
    gtile = wguT[:, :I].reshape(16, 128, 64, 128).transpose(2, 1, 0, 3).reshape(64, 262144)
    utile = wguT[:, I:].reshape(16, 128, 64, 128).transpose(2, 1, 0, 3).reshape(64, 262144)
    wgut_tiled = np.ascontiguousarray(np.concatenate([gtile, utile], 0))
    wdnT = np.ascontiguousarray(wdn.T)
    owT = np.ascontiguousarray(out_w.T)
    sel = np.zeros((32, 32 * 128), np.float32)
    for k in range(32):
        sel[k, k * 128:(k + 1) * 128] = 1.0
    sel_bf = sel.astype(ml_dtypes.bfloat16)

    in_maps = []
    for i in range(NCORE):
        ls = slice(i * LC, (i + 1) * LC)
        es = slice(i * EC, (i + 1) * EC)
        ev = np.zeros((EC, 24), np.float32)
        ev[:, 0:4] = conv_w[es]
        ev[:, 4] = conv_b[es]
        ev[:, 5] = wd2_b[es]
        ev[:, 6] = W_D[es]
        ev[:, 8:24] = A_log[es]
        in_maps.append({
            "resid": np.ascontiguousarray(x0[ls]),
            "nw_rep": nw_rep,
            "inwT": np.ascontiguousarray(in_w[es].T),
            "skwT": np.ascontiguousarray(skip_w[es].T),
            "wd1T": np.ascontiguousarray(wd1[:, es].T),
            "wd2T": np.ascontiguousarray(wd2[es].T),
            "wbcT": np.ascontiguousarray(np.concatenate([wB[:, es].T, wC[:, es].T], axis=1)),
            "evec": ev,
            "sel_in": sel_bf,
            "wguT_sl": np.ascontiguousarray(
                wgut_tiled[i * 16:(i + 1) * 16].reshape(128, 32768)),
            "wdnT_sl": np.ascontiguousarray(wdnT[i * (I // NCORE):(i + 1) * (I // NCORE)]),
            "owT_sl": np.ascontiguousarray(owT[i * (E // NCORE):(i + 1) * (E // NCORE)]),
        })
    return in_maps


def kernel(**inputs):
    in_maps = make_in_maps(inputs)
    nc = build_nc()
    res = run_bass_kernel_spmd(nc, in_maps, core_ids=list(range(NCORE)))
    outp = np.concatenate([res.results[i]["out"] for i in range(NCORE)], axis=0)
    return outp.reshape(1, L, D).astype(np.float32)


# version 11
# speedup vs baseline: 1.2637x; 1.0569x over previous
import contextlib
import numpy as np
import ml_dtypes
import concourse.bass as bass
import concourse.mybir as mybir
from concourse.bass_utils import run_bass_kernel_spmd
from concourse.libnrt import physical_to_logical_core_id

F32 = mybir.dt.float32
BF16 = mybir.dt.bfloat16
AF = mybir.ActivationFunctionType
OP = mybir.AluOpType

NCORE = 8
L = 8192
D = 2048
E = 4096
N = 16
R = 128
I = 8192
LC = L // NCORE
EC = E // NCORE
NET = EC // 128
CH = 1024
NCH = L // CH
EPS = 1e-5

ENGINES = ("sync", "gpsimd", "vector", "scalar", "tensor")


class P:

    def __init__(self, nc, stack, semnames):
        self.nc = nc
        self.prog = {e: [] for e in ENGINES}
        self.sems = {s: stack.enter_context(nc.semaphore(s)) for s in semnames}
        self.c = {s: 0 for s in semnames}
        self._last = {}
        self.nbar = 0

    def wait(self, eng, sem, val):
        if val is None or val <= 0:
            return
        self.prog[eng].append(lambda e, s=sem, v=val: e.wait_ge(self.sems[s], v))

    CHAIN = {"vector": "vv", "scalar": "aa", "gpsimd": "gg"}

    def emit(self, eng, builder, inc=None, inck=1):
        chain_sem = self.CHAIN.get(eng)
        dma_like = (inc in ("dgp", "dsy", "dsc") or (inc or "").startswith("cc")
                    or (inc or "").startswith("fw") or (inc or "").startswith("sw"))
        if chain_sem is not None and not dma_like:
            self.wait(eng, chain_sem, self._last.get(eng, 0))
            if inc is None:
                inc, inck = chain_sem, 1
            assert inc == chain_sem, (eng, inc)
        if inc is not None:
            self.c[inc] += inck
            v = self.c[inc]

            def fn(e, b=builder, s=inc, k=inck):
                b(e).then_inc(self.sems[s], k)
            self.prog[eng].append(fn)
            if chain_sem is not None and not dma_like:
                self._last[eng] = v
            return v
        self.prog[eng].append(lambda e, b=builder: b(e))
        return None

    def ack(self, eng, qsem, asem):
        self.wait(eng, qsem, self.c[qsem])
        self.c[asem] += 1
        v = self.c[asem]
        self.prog[eng].append(lambda e, s=asem: e.sem_inc(self.sems[s], 1))
        return v

    def barrier(self, dma_flush=()):
        self.nbar += 1
        target = 5 * self.nbar
        flush = {}
        for eng, s in dma_flush:
            flush.setdefault(eng, []).append((s, self.c[s]))
        for eng in ENGINES:
            for s, v in flush.get(eng, []):
                self.wait(eng, s, v)
            self.prog[eng].append(lambda e: e.sem_inc(self.sems["bar"], 1))
            self.wait(eng, "bar", target)


def build_nc(debug=False, preamble="clear_sled"):
    nc = bass.Bass("TRN2", target_bir_lowering=False)

    def din(name, shape, dtype=F32):
        return nc.dram_tensor(name, shape, dtype, kind="ExternalInput").ap()

    resid = din("resid", [LC, D])
    nw_rep = din("nw_rep", [128, D])
    inwT = din("inwT", [D, EC])
    skwT = din("skwT", [D, EC])
    wd1T = din("wd1T", [EC, R])
    wd2T = din("wd2T", [R, EC])
    wbcT = din("wbcT", [EC, 32])
    evec = din("evec", [EC, 24])
    sel_in = din("sel_in", [32, 32 * 128], BF16)
    wguT_sl = din("wguT_sl", [128, 32768])
    wdnT_sl = din("wdnT_sl", [I // NCORE, D])
    owT_sl = din("owT_sl", [E // NCORE, D])
    out = nc.dram_tensor("out", [LC, D], F32, kind="ExternalOutput").ap()

    ag_rn_in = nc.dram_tensor("ag_rn_in", [D, LC], BF16)
    RNT_ALL = nc.dram_tensor("RNT_ALL", [NCORE * D, LC], BF16, addr_space="Shared")
    XNT_SP = nc.dram_tensor("XNT_SP", [D, LC], BF16)
    ag_wgu_in = nc.dram_tensor("ag_wgu_in", [128, 32768], BF16)
    WGUT = nc.dram_tensor("WGUT", [128, 262144], BF16, addr_space="Shared")
    ag_wdn_in = nc.dram_tensor("ag_wdn_in", [I // NCORE, D], BF16)
    WDNT = nc.dram_tensor("WDNT", [I, D], BF16, addr_space="Shared")
    ag_ow_in = nc.dram_tensor("ag_ow_in", [E // NCORE, D], BF16)
    OWT = nc.dram_tensor("OWT", [E, D], BF16, addr_space="Shared")
    ar_in = nc.dram_tensor("ar_in", [R + 32, L], F32)
    ar_out = nc.dram_tensor("ar_out", [R + 32, L], F32, addr_space="Shared")
    ag_y_in = nc.dram_tensor("ag_y_in", [EC, L], BF16)
    Y_ALL = nc.dram_tensor("Y_ALL", [E, L], BF16, addr_space="Shared")
    XN_SP = nc.dram_tensor("XN_SP", [LC, D], F32)
    SK2_SP = nc.dram_tensor("SK2_SP", [EC, L], F32)
    OUT2_SP = nc.dram_tensor("OUT2_SP", [LC, D], F32)
    X_SP = nc.dram_tensor("X_SP", [EC, L], BF16)
    H_SP = nc.dram_tensor("H_SP", [I, LC], BF16)

    dbg = {}
    if debug:
        for nm, shape, dt in [
            ("dbg_xn", [LC, D], F32), ("dbg_rn", [LC, D], BF16),
            ("dbg_d1bc", [R + 32, L], F32), ("dbg_x", [EC, L], BF16),
            ("dbg_sk2", [EC, L], F32), ("dbg_delta", [EC, L], F32),
            ("dbg_y", [EC, L], BF16), ("dbg_out2", [LC, D], F32),
            ("dbg_scan", [128, 6 * CH], F32),
        ]:
            dbg[nm] = nc.dram_tensor(nm, shape, dt, kind="ExternalOutput").ap()

    semnames = ["bar", "dsy", "dgp", "dsc", "asy", "agp", "asc", "mm", "vv", "aa", "gg",
                "fw0", "fw1", "fw2", "fw3", "sw0", "sw1", "sw2", "sw3",
                "cc_rn", "cc_wgu", "cc_wdn", "cc_ow", "cc_ar", "cc_y"]

    with contextlib.ExitStack() as g:
        p = P(nc, g, semnames)
        W, EM = p.wait, p.emit

        def gack():
            return p.ack("gpsimd", "dgp", "agp")

        def sack():
            return p.ack("sync", "dsy", "asy")

        nums = sorted(s.num for s in p.sems.values())
        assert nums == list(range(nums[0], nums[0] + len(nums))), nums
        if preamble == "clear_sled":
            for _ in range(40):
                nc.gpsimd.nop(cycle_cnt=2400)
            nc.gpsimd.sem_clear(range(nums[0], nums[-1] + 1))
            for eng in (nc.sync, nc.vector, nc.scalar, nc.tensor):
                for _ in range(120):
                    eng.nop(cycle_cnt=2400)
            nc.all_engine_barrier()

        sbg = lambda name, shape, dtype=F32: g.enter_context(nc.sbuf_tensor(name, shape, dtype))
        evs = sbg("evs", [128, NET * 24])
        asb = sbg("asb", [128, NET * 16])
        carry = sbg("carry", [128, NET * N])

        def ecol(et, c):
            return evs[:, et * 24 + c:et * 24 + c + 1]

        with contextlib.ExitStack() as ph:
            sb = lambda name, shape, dtype=F32: ph.enter_context(nc.sbuf_tensor(name, shape, dtype))
            nwr = sb("nwr", [128, D])
            rsd = [sb(f"rsd{i}", [128, D]) for i in range(2)]
            sq_scr = sb("sq_scr", [128, D])
            xn_t = [sb(f"xn_t{i}", [128, D]) for i in range(2)]
            rn_bf = [sb(f"rn_bf{i}", [128, D], BF16) for i in range(2)]
            xn_bf = [sb(f"xn_bf{i}", [128, D], BF16) for i in range(2)]
            rnT_sb = [sb(f"rnT_sb{i}", [128, D], BF16) for i in range(2)]
            xnT_sb = [sb(f"xnT_sb{i}", [128, D], BF16) for i in range(2)]
            idn = sb("idn", [128, 128], BF16)
            stat = sb("stat", [128, 8])
            wf32 = sb("wf32", [128, 4 * 2048])
            wbf = sb("wbf", [128, 4 * 2048], BF16)
            pt = [ph.enter_context(nc.psum_tensor(f"pt{i}", [128, 128], BF16)) for i in range(4)]

            EM("gpsimd", lambda e: e.dma_start(out=nwr[:, :], in_=nw_rep), inc="dgp", inck=16)
            EM("gpsimd", lambda e: e.dma_start(
                out=evs[:, :].rearrange("p (t c) -> p t c", t=NET),
                in_=evec.rearrange("(t p) c -> p t c", p=128)), inc="dgp", inck=16)
            a_ev = gack()
            EM("gpsimd", lambda e: e.memset(idn[:, :], 0.0), inc="gg")
            idv = EM("gpsimd", lambda e: e.affine_select(
                out=idn[:, :], in_=idn[:, :], compare_op=OP.not_equal, fill=1.0,
                base=0, pattern=[[-1, 128]], channel_multiplier=1), inc="gg")
            W("scalar", "agp", a_ev)
            EM("scalar", lambda e: e.activation(
                asb[:, :].rearrange("p (t n) -> p t n", t=NET),
                evs[:, :].rearrange("p (t c) -> p t c", t=NET)[:, :, 8:24], AF.Exp), inc="aa")
            W("vector", "aa", p.c["aa"])
            EM("vector", lambda e: e.tensor_scalar_mul(asb[:, :], asb[:, :], -1.0), inc="vv")

            cast_srcs = []
            for k in range(16):
                cast_srcs.append((wguT_sl, ag_wgu_in, 0, k * 2048))
            for rt in range(8):
                cast_srcs.append((wdnT_sl, ag_wdn_in, rt * 128, 0))
            for rt in range(4):
                cast_srcs.append((owT_sl, ag_ow_in, rt * 128, 0))
            castv, swc = {}, {}
            cu = [0]

            def emit_cast_unit():
                k = cu[0]
                if k >= len(cast_srcs):
                    return
                cu[0] += 1
                s = k % 4
                src_t, dst_t, r0, c0 = cast_srcs[k]
                if (k - 4) in castv:
                    W("gpsimd", "vv", castv[k - 4])
                EM("gpsimd", lambda e, src_t=src_t, r0=r0, c0=c0, s=s: e.dma_start(
                    out=wf32[:, s * 2048:(s + 1) * 2048],
                    in_=src_t[r0:r0 + 128, c0:c0 + 2048]), inc=f"fw{s}", inck=16)
                fcnt = p.c[f"fw{s}"]
                W("vector", f"fw{s}", fcnt)
                if (k - 4) in swc:
                    W("vector", f"sw{s}", swc[k - 4])
                castv[k] = EM("vector", lambda e, s=s: e.tensor_copy(
                    wbf[:, s * 2048:(s + 1) * 2048], wf32[:, s * 2048:(s + 1) * 2048]), inc="vv")
                W("sync", "vv", castv[k])
                EM("sync", lambda e, dst_t=dst_t, r0=r0, c0=c0, s=s: e.dma_start(
                    out=dst_t.ap()[r0:r0 + 128, c0:c0 + 2048],
                    in_=wbf[:, s * 2048:(s + 1) * 2048]), inc=f"sw{s}", inck=16)
                swc[k] = p.c[f"sw{s}"]

            rsd_ack = {}
            for t in range(2):
                EM("sync", lambda e, t=t: e.dma_start(
                    out=rsd[t % 2][:, :], in_=resid[t * 128:(t + 1) * 128, :]), inc="dsy", inck=16)
                rsd_ack[t] = sack()

            tT_ack = {}
            xnf = {}
            for t in range(8):
                b = t % 2
                W("vector", "asy", rsd_ack[t])
                rs1 = EM("vector", lambda e, b=b: e.reduce_sum(
                    stat[:, 0:1], rsd[b][:, :], axis=mybir.AxisListType.X), inc="vv")
                W("vector", "vv", rs1)
                EM("vector", lambda e: e.tensor_scalar_mul(stat[:, 0:1], stat[:, 0:1], 1.0 / D))
                sqv = EM("vector", lambda e, b=b: e.tensor_mul(sq_scr[:, :], rsd[b][:, :], rsd[b][:, :]),
                         inc="vv")
                W("vector", "vv", sqv)
                rs2 = EM("vector", lambda e: e.reduce_sum(
                    stat[:, 1:2], sq_scr[:, :], axis=mybir.AxisListType.X), inc="vv")
                W("vector", "vv", rs2)
                EM("vector", lambda e: e.tensor_scalar_mul(stat[:, 1:2], stat[:, 1:2], 1.0 / D))
                EM("vector", lambda e: e.tensor_mul(stat[:, 7:8], stat[:, 0:1], stat[:, 0:1]))
                EM("vector", lambda e: e.tensor_sub(stat[:, 2:3], stat[:, 1:2], stat[:, 7:8]))
                vv1 = EM("vector", lambda e: e.tensor_scalar(
                    stat[:, 3:4], stat[:, 2:3], 1.0, EPS, OP.mult, OP.add), inc="vv")
                W("scalar", "vv", vv1)
                EM("scalar", lambda e: e.activation(stat[:, 4:5], stat[:, 3:4], AF.Ln))
                aa1 = EM("scalar", lambda e: e.activation(stat[:, 5:6], stat[:, 4:5], AF.Exp, scale=-0.5),
                         inc="aa")
                aa1b = EM("scalar", lambda e: e.activation(stat[:, 6:7], stat[:, 4:5], AF.Exp, scale=-1.0),
                          inc="aa")
                W("vector", "aa", aa1b)
                if t >= 2:
                    W("vector", "asy", xnf[t - 2])
                xnv = EM("vector", lambda e, b=b: e.tensor_scalar(
                    xn_t[b][:, :], rsd[b][:, :], stat[:, 0:1], stat[:, 5:6], OP.subtract, OP.mult), inc="vv")
                if t + 2 < 8:
                    W("sync", "vv", xnv)
                    EM("sync", lambda e, t=t: e.dma_start(
                        out=rsd[t % 2][:, :], in_=resid[(t + 2) * 128:(t + 3) * 128, :]), inc="dsy", inck=16)
                    rsd_ack[t + 2] = sack()
                EM("vector", lambda e: e.tensor_mul(stat[:, 6:7], stat[:, 2:3], stat[:, 6:7]))
                vv2 = EM("vector", lambda e: e.tensor_scalar(
                    stat[:, 6:7], stat[:, 6:7], 1.0, EPS, OP.mult, OP.add), inc="vv")
                W("scalar", "vv", vv2)
                EM("scalar", lambda e: e.activation(stat[:, 6:7], stat[:, 6:7], AF.Ln))
                aa2 = EM("scalar", lambda e: e.activation(stat[:, 6:7], stat[:, 6:7], AF.Exp, scale=-0.5),
                         inc="aa")
                W("vector", "aa", aa2)
                rnv = EM("vector", lambda e, b=b: e.scalar_tensor_tensor(
                    rn_bf[b][:, :], xn_t[b][:, :], stat[:, 6:7], nwr[:, :], OP.mult, OP.mult), inc="vv")
                vd = EM("vector", lambda e, b=b: e.tensor_copy(xn_bf[b][:, :], xn_t[b][:, :]), inc="vv")
                W("sync", "vv", vd)
                EM("sync", lambda e, b=b, t=t: e.dma_start(
                    out=XN_SP.ap()[t * 128:(t + 1) * 128, :], in_=xn_t[b][:, :]), inc="dsy", inck=16)
                xnf[t] = sack()
                W("tensor", "vv", vd)
                if t >= 2:
                    W("tensor", "asy", tT_ack[t - 2])
                tcp = {}
                if t == 0:
                    W("tensor", "gg", idv)
                for half, (srcb, dstb) in enumerate([(rn_bf[b], rnT_sb[b]), (xn_bf[b], xnT_sb[b])]):
                    for dt in range(16):
                        j = half * 16 + dt
                        pj = j % 4
                        if (j - 4) in tcp:
                            W("tensor", "aa", tcp[j - 4])
                        tmm = EM("tensor", lambda e, srcb=srcb, dt=dt, pj=pj: e.transpose(
                            pt[pj][:, :], srcb[:, dt * 128:(dt + 1) * 128], idn[:, :]), inc="mm")
                        W("scalar", "mm", tmm)
                        tcp[j] = EM("scalar", lambda e, dstb=dstb, dt=dt, pj=pj: e.activation(
                            dstb[:, dt * 128:(dt + 1) * 128], pt[pj][:, :], AF.Identity), inc="aa")
                W("sync", "aa", tcp[31])
                EM("sync", lambda e, b=b, t=t: e.dma_start(
                    out=ag_rn_in.ap()[:, t * 128:(t + 1) * 128].rearrange("(d p) l -> p d l", p=128),
                    in_=rnT_sb[b][:, :].rearrange("p (d l) -> p d l", d=16)), inc="dsy", inck=16)
                EM("sync", lambda e, b=b, t=t: e.dma_start(
                    out=XNT_SP.ap()[:, t * 128:(t + 1) * 128].rearrange("(d p) l -> p d l", p=128),
                    in_=xnT_sb[b][:, :].rearrange("p (d l) -> p d l", d=16)), inc="dsy", inck=16)
                tT_ack[t] = sack()
                for _ in range(4):
                    emit_cast_unit()

            W("gpsimd", "asy", tT_ack[7])
            EM("gpsimd", lambda e: e.collective_compute(
                "AllGather", OP.bypass, replica_groups=[list(range(NCORE))],
                ins=[ag_rn_in.ap().opt()], outs=[RNT_ALL.ap().opt()]), inc="cc_rn")
            while cu[0] < len(cast_srcs):
                emit_cast_unit()
            for s in range(4):
                W("gpsimd", f"sw{s}", p.c[f"sw{s}"])
            EM("gpsimd", lambda e: e.collective_compute(
                "AllGather", OP.bypass, replica_groups=[list(range(NCORE))],
                ins=[ag_wgu_in.ap().opt()], outs=[WGUT.ap().opt()]), inc="cc_wgu")
            p.barrier(dma_flush=[("gpsimd", "dgp"), ("sync", "dsy")])

        with contextlib.ExitStack() as ph:
            sb = lambda name, shape, dtype=F32: ph.enter_context(nc.sbuf_tensor(name, shape, dtype))
            psb = lambda name, shape: ph.enter_context(nc.psum_tensor(name, shape, F32))
            wstage = sb("wstage", [128, 16 * 512])
            inw_bf = sb("inw_bf", [128, 16 * 512], BF16)
            skw_bf = sb("skw_bf", [128, 16 * 512], BF16)
            wd1_bf = sb("wd1_bf", [128, NET * 128], BF16)
            wbc_bf = sb("wbc_bf", [128, NET * 32], BF16)
            rhs = [sb(f"rhs{i}", [128, 16 * 512], BF16) for i in range(2)]
            xin_buf = [sb(f"xinb{et}", [128, 515]) for et in range(NET)]
            acc = [sb(f"acc{et}", [128, 512]) for et in range(NET)]
            sk2_t = [[sb(f"sk2t{i}_{et}", [128, 512]) for et in range(NET)] for i in range(2)]
            d1e = [sb(f"d1e{i}", [128, 512]) for i in range(2)]
            bce = [sb(f"bce{i}", [128, 512]) for i in range(2)]
            x_bf = sb("x_bf", [128, NET * L], BF16)
            ps_xin = [psb(f"ps_xin{i}", [128, 512]) for i in range(2)]
            ps_skip = [psb(f"ps_skip{i}", [128, 512]) for i in range(2)]
            ps_d1 = psb("ps_d1", [128, 512])
            ps_bc = psb("ps_bc", [128, 512])

            def load_cast(src, w, sl, nfree):
                EM("gpsimd", lambda e, src=src, sl=sl, nfree=nfree: e.dma_start(
                    out=wstage[:, 0:nfree].rearrange("p (t c) -> p t c", t=sl),
                    in_=src.rearrange("(t p) c -> p t c", p=128)), inc="dgp", inck=16)
                a_ld = gack()
                W("vector", "agp", a_ld)
                cv = EM("vector", lambda e, w=w, nfree=nfree: e.tensor_copy(w[:, 0:nfree], wstage[:, 0:nfree]),
                        inc="vv")
                W("gpsimd", "vv", cv)

            load_cast(inwT, inw_bf, 16, 16 * 512)
            load_cast(skwT, skw_bf, 16, 16 * 512)
            load_cast(wd1T, wd1_bf, NET, NET * 128)
            load_cast(wbcT, wbc_bf, NET, NET * 32)
            for et in range(NET - 1):
                EM("vector", lambda e, et=et: e.memset(xin_buf[et][:, 0:3], 0.0))
            v_w = EM("vector", lambda e: e.memset(xin_buf[NET - 1][:, 0:3], 0.0), inc="vv")

            W("sync", "cc_rn", 1)
            NLC = 16
            mm_xin, mm_skip, evac_xin, evac_skip = {}, {}, {}, {}
            xsilu, sk2_ack = {}, {}
            rhs_cnt = {}

            def rhs_load(lc):
                s = lc % 2
                c, hh = lc // 2, lc % 2
                EM("sync", lambda e, s=s, c=c, hh=hh: e.dma_start(
                    out=rhs[s][:, :].rearrange("p (d l) -> p d l", d=16),
                    in_=RNT_ALL.ap()[c * D:(c + 1) * D, hh * 512:(hh + 1) * 512]
                    .rearrange("(d p) l -> p d l", p=128)), inc=f"fw{s}", inck=16)
                rhs_cnt[lc] = p.c[f"fw{s}"]

            d1mm, bcmm, d1ev, bcev, d1_ack = {}, {}, {}, {}, {}
            rhs_load(0)
            for lc in range(NLC):
                b = lc % 2
                if lc + 1 < NLC:
                    if lc >= 1:
                        W("sync", "mm", mm_skip[(lc - 1, NET - 1)])
                    rhs_load(lc + 1)
                W("tensor", f"fw{b}", rhs_cnt[lc])
                if lc == 0:
                    W("tensor", "vv", v_w)
                for et in range(NET):
                    pb = et % 2
                    prev = (lc, et - 2) if et >= 2 else (lc - 1, et + 2)
                    if prev in evac_xin:
                        W("tensor", "vv", evac_xin[prev])
                    for dt in range(16):
                        mm_xin[(lc, et)] = EM("tensor", lambda e, et=et, dt=dt, b=b, pb=pb: e.matmul(
                            ps_xin[pb][:, :], inw_bf[:, dt * 512 + et * 128:dt * 512 + (et + 1) * 128],
                            rhs[b][:, dt * 512:(dt + 1) * 512], start=(dt == 0), stop=(dt == 15)),
                            inc="mm" if dt == 15 else None)
                    if prev in evac_skip:
                        W("tensor", "aa", evac_skip[prev])
                    for dt in range(16):
                        mm_skip[(lc, et)] = EM("tensor", lambda e, et=et, dt=dt, b=b, pb=pb: e.matmul(
                            ps_skip[pb][:, :], skw_bf[:, dt * 512 + et * 128:dt * 512 + (et + 1) * 128],
                            rhs[b][:, dt * 512:(dt + 1) * 512], start=(dt == 0), stop=(dt == 15)),
                            inc="mm" if dt == 15 else None)

                    W("vector", "mm", mm_xin[(lc, et)])
                    if (lc - 1, et) in xsilu:
                        W("vector", "aa", xsilu[(lc - 1, et)])
                    evv = EM("vector", lambda e, et=et, pb=pb: e.tensor_copy(
                        xin_buf[et][:, 3:515], ps_xin[pb][:, :]), inc="vv")
                    W("vector", "vv", evv)
                    cv0 = EM("vector", lambda e, et=et: e.tensor_scalar(
                        acc[et][:, :], xin_buf[et][:, 0:512], ecol(et, 0), None, OP.mult), inc="vv")
                    for k in range(1, 4):
                        W("vector", "vv", cv0)
                        cv0 = EM("vector", lambda e, et=et, k=k: e.scalar_tensor_tensor(
                            acc[et][:, :], xin_buf[et][:, k:k + 512], ecol(et, k),
                            acc[et][:, :], OP.mult, OP.add), inc="vv")
                    evac_xin[(lc, et)] = cv0
                    EM("vector", lambda e, et=et: e.tensor_copy(
                        xin_buf[et][:, 0:3], xin_buf[et][:, 512:515]), inc="vv")
                    W("scalar", "vv", evac_xin[(lc, et)])
                    xsilu[(lc, et)] = EM("scalar", lambda e, et=et, lc=lc: e.activation(
                        x_bf[:, et * L + lc * 512:et * L + (lc + 1) * 512], acc[et][:, :], AF.Silu,
                        bias=ecol(et, 4)), inc="aa")
                    W("scalar", "mm", mm_skip[(lc, et)])
                    if (lc - 2, et) in sk2_ack:
                        W("scalar", "agp", sk2_ack[(lc - 2, et)])
                    evac_skip[(lc, et)] = EM("scalar", lambda e, et=et, pb=pb, b=b: e.activation(
                        sk2_t[b][et][:, :], ps_skip[pb][:, :], AF.Silu), inc="aa")
                    W("gpsimd", "aa", evac_skip[(lc, et)])
                    EM("gpsimd", lambda e, et=et, lc=lc, b=b: e.dma_start(
                        out=SK2_SP.ap()[et * 128:(et + 1) * 128, lc * 512:(lc + 1) * 512],
                        in_=sk2_t[b][et][:, :]), inc="dgp", inck=16)
                    sk2_ack[(lc, et)] = gack()

                W("tensor", "aa", xsilu[(lc, NET - 1)])
                if lc >= 1:
                    W("tensor", "vv", d1ev[lc - 1])
                for et in range(NET):
                    d1mm[lc] = EM("tensor", lambda e, et=et, lc=lc: e.matmul(
                        ps_d1[:, :], wd1_bf[:, et * 128:(et + 1) * 128],
                        x_bf[:, et * L + lc * 512:et * L + (lc + 1) * 512],
                        start=(et == 0), stop=(et == NET - 1)),
                        inc="mm" if et == NET - 1 else None)
                if lc >= 1:
                    W("tensor", "vv", bcev[lc - 1])
                for et in range(NET):
                    bcmm[lc] = EM("tensor", lambda e, et=et, lc=lc: e.matmul(
                        ps_bc[0:32, :], wbc_bf[:, et * 32:(et + 1) * 32],
                        x_bf[:, et * L + lc * 512:et * L + (lc + 1) * 512],
                        start=(et == 0), stop=(et == NET - 1)),
                        inc="mm" if et == NET - 1 else None)
                W("vector", "mm", d1mm[lc])
                if lc >= 2:
                    W("vector", "agp", d1_ack[lc - 2])
                d1ev[lc] = EM("vector", lambda e, b=b: e.tensor_copy(d1e[b][:, :], ps_d1[:, :]), inc="vv")
                W("vector", "mm", bcmm[lc])
                bcev[lc] = EM("vector", lambda e, b=b: e.tensor_copy(bce[b][0:32, :], ps_bc[0:32, :]),
                              inc="vv")
                W("gpsimd", "vv", bcev[lc])
                EM("gpsimd", lambda e, lc=lc, b=b: e.dma_start(
                    out=ar_in.ap()[0:R, lc * 512:(lc + 1) * 512], in_=d1e[b][:, :]), inc="dgp", inck=16)
                EM("gpsimd", lambda e, lc=lc, b=b: e.dma_start(
                    out=ar_in.ap()[R:R + 32, lc * 512:(lc + 1) * 512], in_=bce[b][0:32, :]), inc="dgp", inck=16)
                d1_ack[lc] = gack()

            x_ready = p.c["aa"]
            W("gpsimd", "dgp", p.c["dgp"])
            EM("gpsimd", lambda e: e.collective_compute(
                "AllReduce", OP.add, replica_groups=[list(range(NCORE))],
                ins=[ar_in.ap().opt()], outs=[ar_out.ap().opt()]), inc="cc_ar")
            EM("gpsimd", lambda e: e.collective_compute(
                "AllGather", OP.bypass, replica_groups=[list(range(NCORE))],
                ins=[ag_wdn_in.ap().opt()], outs=[WDNT.ap().opt()]), inc="cc_wdn")
            EM("gpsimd", lambda e: e.collective_compute(
                "AllGather", OP.bypass, replica_groups=[list(range(NCORE))],
                ins=[ag_ow_in.ap().opt()], outs=[OWT.ap().opt()]), inc="cc_ow")
            W("gpsimd", "aa", x_ready)
            for et in range(NET):
                EM("gpsimd", lambda e, et=et: e.dma_start(
                    out=X_SP.ap()[et * 128:(et + 1) * 128, :], in_=x_bf[:, et * L:(et + 1) * L]),
                    inc="dgp", inck=16)
            p.barrier(dma_flush=[("gpsimd", "dgp"), ("sync", "dsy")])

        with contextlib.ExitStack() as ph:
            sb = lambda name, shape, dtype=F32: ph.enter_context(nc.sbuf_tensor(name, shape, dtype))
            psb = lambda name, shape: ph.enter_context(nc.psum_tensor(name, shape, F32))
            d1t_f = sb("d1t_f", [128, CH])
            d1t_bf = sb("d1t_bf", [128, CH], BF16)
            bmcs = sb("bmcs", [32, CH])
            bmc_bf = sb("bmc_bf", [32, 2 * CH], BF16)
            wd2_bf = sb("wd2_bf", [128, EC], BF16)
            delta = sb("delta", [128, NET * CH])
            xc_bf = [sb(f"xc_bf{i}", [128, NET * CH], BF16) for i in range(2)]
            u_bf = sb("u_bf", [128, NET * CH], BF16)
            y_acc = sb("y_acc", [128, NET * CH])
            sk2c = [sb(f"sk2c{i}", [128, CH]) for i in range(2)]
            p_buf = sb("p_buf", [128, 2 * CH])
            ug_buf = sb("ug_buf", [128, 2 * CH], BF16)
            h_buf = sb("h_buf", [128, 2 * CH])
            prod = [sb(f"prod{i}", [128, CH]) for i in range(3)]
            brep = sb("brep", [128, 2 * CH], BF16)
            crep = sb("crep", [128, 2 * CH], BF16)
            t0 = sb("t0", [128, CH])
            t1 = sb("t1", [128, CH])
            y_bf = [sb(f"y_bf{i}", [128, CH], BF16) for i in range(2)]
            sel_bf = sb("sel_bf", [32, 32 * 128], BF16)
            ps_dp = psb("ps_dp", [128, CH])
            ps_b = psb("ps_b", [128, 512])
            ps_c = psb("ps_c", [128, 512])
            xnt = sb("xnt", [128, 16 * 512], BF16)
            wg_g = [sb(f"wg_g{i}", [128, 16 * 128], BF16) for i in range(3)]
            wg_u = [sb(f"wg_u{i}", [128, 16 * 128], BF16) for i in range(3)]
            sg = [sb(f"sg{i}", [128, 512]) for i in range(2)]
            h_st = [sb(f"h_st{i}", [128, 512], BF16) for i in range(2)]
            h_in = [sb(f"h_in{i}", [128, 2 * 512], BF16) for i in range(3)]
            wdn_t = [sb(f"wdn_t{i}", [128, 2 * 512], BF16) for i in range(3)]
            o2_t = [sb(f"o2_t{i}", [128, 512]) for i in range(2)]
            pf = [psb(f"pf{i}", [128, 512]) for i in range(4)]

            EM("gpsimd", lambda e: e.dma_start(out=sel_bf[:, :], in_=sel_in), inc="dgp", inck=16)
            EM("gpsimd", lambda e: e.dma_start(out=d1t_f[:, 0:EC], in_=wd2T), inc="dgp", inck=16)
            a_w = gack()
            W("vector", "agp", a_w)
            wcv = EM("vector", lambda e: e.tensor_copy(wd2_bf[:, :], d1t_f[:, 0:EC]), inc="vv")
            W("sync", "cc_ar", 1)
            W("sync", "vv", wcv)
            W("tensor", "agp", a_w)
            W("tensor", "vv", wcv)
            xnt_ack = [None, None]

            dp_mm, dp_chain, delta_ok, pexp = {}, {}, {}, {}
            brep_ok = {}
            ug_done, tts_done, prod_done, add_done = {}, {}, {}, {}
            stt_done, ymul_done, ybf_ack, sk_ack = {}, {}, {}, {}
            d1_cast, xc_ack = {}, {}
            prod_war = {}
            gu_mm, silu_done, hmult_done, hst_ack = {}, {}, {}, {}
            dn_mm, o2_evac, o2_ack = {}, {}, {}
            dn_state = {"gpr": 0, "gprl": 0}

            fwcnt = {}

            def gate_loads(lh):
                for it in range(64):
                    if it == 0 and lh == 0:
                        EM("sync", lambda e, lh=lh: e.dma_start(
                            out=xnt[:, :].rearrange("p (d l) -> p d l", d=16),
                            in_=XNT_SP.ap()[:, lh * 512:(lh + 1) * 512]
                            .rearrange("(d p) l -> p d l", p=128)), inc="dsy", inck=16)
                        xnt_ack[lh] = sack()
                        W("scalar", "cc_wgu", 1)
                    b = (lh * 64 + it) % 3
                    prevk3 = (lh, it - 3) if it >= 3 else ((lh - 1, 61 + it) if lh >= 1 else None)
                    if prevk3 in gu_mm:
                        W("scalar", "mm", gu_mm[prevk3])
                    EM("scalar", lambda e, it=it, b=b: e.dma_start(
                        out=wg_g[b][:, :],
                        in_=WGUT.ap()[it:it + 1, :].rearrange("r (p c) -> (r p) c", p=128)),
                        inc=f"fw{b}", inck=16)
                    EM("scalar", lambda e, it=it, b=b: e.dma_start(
                        out=wg_u[b][:, :],
                        in_=WGUT.ap()[64 + it:65 + it, :].rearrange("r (p c) -> (r p) c", p=128)),
                        inc=f"fw{b}", inck=16)
                    fwcnt[("g", lh, it)] = p.c[f"fw{b}"]
                    yield

            def gate_comp(lh):
                for it in range(64):
                    if it == 0:
                        if lh == 1:
                            W("sync", "mm", gu_mm[(0, 63)])
                            EM("sync", lambda e, lh=lh: e.dma_start(
                                out=xnt[:, :].rearrange("p (d l) -> p d l", d=16),
                                in_=XNT_SP.ap()[:, lh * 512:(lh + 1) * 512]
                                .rearrange("(d p) l -> p d l", p=128)), inc="dsy", inck=16)
                            xnt_ack[lh] = sack()
                        W("tensor", "asy", xnt_ack[lh])
                    b = (lh * 64 + it) % 3
                    b2 = it % 2
                    pg, pu = pf[2 * b2], pf[2 * b2 + 1]
                    prevk = (lh, it - 2) if it >= 2 else ((lh - 1, 62 + it) if lh >= 1 else None)
                    W("tensor", f"fw{b}", fwcnt[("g", lh, it)])
                    if prevk in silu_done:
                        W("tensor", "aa", silu_done[prevk])
                        W("tensor", "vv", hmult_done[prevk])
                    for dt in range(16):
                        EM("tensor", lambda e, dt=dt, b=b, pg=pg: e.matmul(
                            pg[:, :], wg_g[b][:, dt * 128:(dt + 1) * 128],
                            xnt[:, dt * 512:(dt + 1) * 512], start=(dt == 0), stop=(dt == 15)))
                    for dt in range(16):
                        gu_mm[(lh, it)] = EM("tensor", lambda e, dt=dt, b=b, pu=pu: e.matmul(
                            pu[:, :], wg_u[b][:, dt * 128:(dt + 1) * 128],
                            xnt[:, dt * 512:(dt + 1) * 512], start=(dt == 0), stop=(dt == 15)),
                            inc="mm" if dt == 15 else None)
                    W("scalar", "mm", gu_mm[(lh, it)])
                    if prevk in hmult_done:
                        W("scalar", "vv", hmult_done[prevk])
                    silu_done[(lh, it)] = EM("scalar", lambda e, b2=b2, pg=pg: e.activation(
                        sg[b2][:, :], pg[:, :], AF.Silu), inc="aa")
                    W("vector", "aa", silu_done[(lh, it)])
                    if prevk in hst_ack:
                        W("vector", "asy", hst_ack[prevk])
                    hmult_done[(lh, it)] = EM("vector", lambda e, b2=b2, pu=pu: e.tensor_mul(
                        h_st[b2][:, :], sg[b2][:, :], pu[:, :]), inc="vv")
                    W("sync", "vv", hmult_done[(lh, it)])
                    EM("sync", lambda e, it=it, b2=b2, lh=lh: e.dma_start(
                        out=H_SP.ap()[it * 128:(it + 1) * 128, lh * 512:(lh + 1) * 512],
                        in_=h_st[b2][:, :]), inc="dsy", inck=16)
                    hst_ack[(lh, it)] = sack()
                    yield

            def down_loads():
                for lh in range(2):
                    for dc in range(4):
                        for pr in range(32):
                            it = 2 * pr
                            gpr = dn_state["gprl"]
                            dn_state["gprl"] += 1
                            b = gpr % 3
                            if (gpr - 3) in dn_mm:
                                W("scalar", "mm", dn_mm[gpr - 3])
                            if dc == 0 and pr == 0:
                                W("scalar", "cc_wdn", 1)
                                W("scalar", "asy", hst_ack[(lh, 63)])
                            EM("scalar", lambda e, it=it, dc=dc, b=b: e.dma_start(
                                out=wdn_t[b][:, :].rearrange("p (i c) -> p i c", i=2),
                                in_=WDNT.ap()[it * 128:(it + 2) * 128, dc * 512:(dc + 1) * 512]
                                .rearrange("(i p) c -> p i c", p=128)), inc=f"fw{b}", inck=16)
                            EM("scalar", lambda e, it=it, lh=lh, b=b: e.dma_start(
                                out=h_in[b][:, :].rearrange("p (i c) -> p i c", i=2),
                                in_=H_SP.ap()[it * 128:(it + 2) * 128, lh * 512:(lh + 1) * 512]
                                .rearrange("(i p) c -> p i c", p=128)), inc=f"fw{b}", inck=16)
                            fwcnt[("d", gpr)] = p.c[f"fw{b}"]
                            yield
                        yield

            def down_comp():
                for lh in range(2):
                    for dc in range(4):
                        for pr in range(32):
                            it = 2 * pr
                            gpr = dn_state["gpr"]
                            dn_state["gpr"] += 1
                            b = gpr % 3
                            W("tensor", f"fw{b}", fwcnt[("d", gpr)])
                            if it == 0:
                                if lh == 0 and dc == 0:
                                    W("tensor", "vv", hmult_done[(1, 63)])
                                    W("tensor", "aa", silu_done[(1, 63)])
                                else:
                                    W("tensor", "aa", o2_evac[3])
                            for ii in range(2):
                                for lt2 in range(4):
                                    dn_mm[gpr] = EM(
                                        "tensor", lambda e, ii=ii, lt2=lt2, b=b, it=it: e.matmul(
                                            pf[lt2][:, :],
                                            h_in[b][:, ii * 512 + lt2 * 128:ii * 512 + (lt2 + 1) * 128],
                                            wdn_t[b][:, ii * 512:(ii + 1) * 512],
                                            start=(it + ii == 0), stop=(it + ii == 63)),
                                        inc="mm" if (ii == 1 and lt2 == 3) else None)
                            yield
                        W("scalar", "mm", dn_mm[dn_state["gpr"] - 1])
                        for lt2 in range(4):
                            ob = lt2 % 2
                            okey = (lt2 - 2) if lt2 >= 2 else (lt2 + 2)
                            if okey in o2_ack:
                                W("scalar", "asy", o2_ack[okey])
                            o2_evac[lt2] = EM("scalar", lambda e, lt2=lt2, ob=ob: e.activation(
                                o2_t[ob][:, :], pf[lt2][:, :], AF.Identity), inc="aa")
                            W("sync", "aa", o2_evac[lt2])
                            EM("sync", lambda e, lh=lh, dc=dc, lt2=lt2, ob=ob: e.dma_start(
                                out=OUT2_SP.ap()[lh * 512 + lt2 * 128:lh * 512 + (lt2 + 1) * 128,
                                                 dc * 512:(dc + 1) * 512], in_=o2_t[ob][:, :]),
                                inc="dsy", inck=16)
                            o2_ack[lt2] = sack()
                        yield

            import itertools as _it
            load_iter = _it.chain(gate_loads(0), gate_loads(1), down_loads())
            comp_iter = _it.chain(gate_comp(0), gate_comp(1), down_comp())
            NUNITS = 64 + 64 + 2 * 4 * 33
            state_lc = [0, 0]

            def pull():
                while state_lc[0] < min(state_lc[1] + 2, NUNITS):
                    next(load_iter)
                    state_lc[0] += 1
                if state_lc[1] < NUNITS:
                    next(comp_iter)
                    state_lc[1] += 1

            bmc_cp, bc_mm_last = {}, {}
            a_bmc_d, a_d1_d, a_xc_d = {}, {}, {}

            def chunk_loads(cnk):
                c0 = cnk * CH
                if cnk >= 1:
                    W("sync", "vv", bmc_cp[cnk - 1])
                EM("sync", lambda e, c0=c0: e.dma_start(
                    out=bmcs[:, :], in_=ar_out.ap()[R:R + 32, c0:c0 + CH]), inc="dsy", inck=16)
                a_bmc_d[cnk] = sack()
                if cnk >= 1:
                    W("sync", "vv", d1_cast[cnk - 1])
                EM("sync", lambda e, c0=c0: e.dma_start(
                    out=d1t_f[:, :], in_=ar_out.ap()[0:R, c0:c0 + CH]), inc="dsy", inck=16)
                a_d1_d[cnk] = sack()
                if cnk >= 2:
                    W("sync", "vv", stt_done[(cnk - 2, NET - 1)])
                EM("sync", lambda e, c0=c0, xb=cnk % 2: e.dma_start(
                    out=xc_bf[xb][:, :].rearrange("p (et l) -> p et l", et=NET),
                    in_=X_SP.ap()[:, c0:c0 + CH].rearrange("(et p) l -> p et l", p=128)),
                    inc="dsy", inck=16)
                a_xc_d[cnk] = sack()

            chunk_loads(0)
            giter = [0]
            for cnk in range(NCH):
                c0 = cnk * CH
                cb2 = cnk % 2
                xb = cnk % 2
                W("vector", "asy", a_bmc_d[cnk])
                if cnk >= 2:
                    W("vector", "mm", bc_mm_last[cnk - 2])
                bmc_cp[cnk] = EM("vector", lambda e, cb2=cb2: e.tensor_copy(
                    bmc_bf[:, cb2 * CH:(cb2 + 1) * CH], bmcs[:, :]), inc="vv")
                W("vector", "asy", a_d1_d[cnk])
                if cnk >= 1:
                    W("vector", "mm", dp_mm[(cnk - 1, NET - 1)])
                d1_cast[cnk] = EM("vector", lambda e: e.tensor_copy(d1t_bf[:, :], d1t_f[:, :]), inc="vv")
                W("tensor", "vv", d1_cast[cnk])
                for et in range(NET):
                    pk = (cnk, et - 1) if et >= 1 else (cnk - 1, NET - 1)
                    if pk in dp_chain:
                        W("tensor", "aa", dp_chain[pk])
                    for hh in range(2):
                        dp_mm[(cnk, et)] = EM("tensor", lambda e, et=et, hh=hh: e.matmul(
                            ps_dp[:, hh * 512:(hh + 1) * 512], wd2_bf[:, et * 128:(et + 1) * 128],
                            d1t_bf[:, hh * 512:(hh + 1) * 512], start=True, stop=True),
                            inc="mm" if hh == 1 else None)
                    W("scalar", "mm", dp_mm[(cnk, et)])
                    if pk in delta_ok:
                        W("scalar", "vv", delta_ok[pk])
                    EM("scalar", lambda e, et=et: e.activation(t0[:, :], ps_dp[:, :], AF.Relu, bias=ecol(et, 5)))
                    EM("scalar", lambda e, et=et: e.activation(t1[:, :], ps_dp[:, :], AF.Abs, bias=ecol(et, 5)))
                    EM("scalar", lambda e: e.activation(t1[:, :], t1[:, :], AF.Exp, scale=-1.0))
                    dp_chain[(cnk, et)] = EM("scalar", lambda e: e.activation(t1[:, :], t1[:, :], AF.Ln, bias=1.0),
                                             inc="aa")
                    W("vector", "aa", dp_chain[(cnk, et)])
                    if cnk >= 1:
                        W("vector", "aa", pexp[(cnk - 1, N - 1, et)])
                    delta_ok[(cnk, et)] = EM("vector", lambda e, et=et: e.tensor_add(
                        delta[:, et * CH:(et + 1) * CH], t0[:, :], t1[:, :]), inc="vv")
                W("vector", "asy", a_xc_d[cnk])
                for et in range(NET):
                    EM("vector", lambda e, et=et, xb=xb: e.tensor_mul(
                        u_bf[:, et * CH:(et + 1) * CH], delta[:, et * CH:(et + 1) * CH],
                        xc_bf[xb][:, et * CH:(et + 1) * CH]), inc="vv")

                pending_adds = []

                def flush_add():
                    s2, et2, pb2 = pending_adds.pop(0)
                    W("vector", "gg", prod_done[(cnk, s2)])
                    add_done[(cnk, s2)] = EM("vector", lambda e, et2=et2, pb2=pb2: e.tensor_add(
                        y_acc[:, et2 * CH:(et2 + 1) * CH], y_acc[:, et2 * CH:(et2 + 1) * CH],
                        prod[pb2][:, :]), inc="vv")
                    prod_war[pb2] = add_done[(cnk, s2)]

                for n in range(N):
                    nb = n % 2
                    gk = (cnk, n - 2) if n >= 2 else ((cnk - 1, N - 2 + n) if cnk >= 1 else None)
                    if gk is not None and gk in brep_ok:
                        s_last = (gk[1]) * NET + NET - 1
                        W("tensor", "vv", ug_done[(gk[0], s_last)])
                        W("tensor", "gg", prod_done[(gk[0], s_last)])
                    pk = (cnk, n - 1) if n >= 1 else ((cnk - 1, N - 1) if cnk >= 1 else None)
                    if pk in brep_ok:
                        W("tensor", "aa", brep_ok[pk][1])
                    if n == 0:
                        W("tensor", "vv", bmc_cp[cnk])
                    m1 = EM("tensor", lambda e, n=n, cb2=cb2: e.matmul(
                        ps_b[:, :], sel_bf[0:32, n * 128:(n + 1) * 128],
                        bmc_bf[0:32, cb2 * CH:cb2 * CH + 512], start=True, stop=True), inc="mm")
                    m2 = EM("tensor", lambda e, n=n, cb2=cb2: e.matmul(
                        ps_c[:, :], sel_bf[0:32, n * 128:(n + 1) * 128],
                        bmc_bf[0:32, cb2 * CH + 512:(cb2 + 1) * CH], start=True, stop=True), inc="mm")
                    W("scalar", "mm", m2)
                    if gk is not None and gk in brep_ok:
                        s_last = (gk[1]) * NET + NET - 1
                        W("scalar", "vv", ug_done[(gk[0], s_last)])
                        W("scalar", "gg", prod_done[(gk[0], s_last)])
                    eb0 = EM("scalar", lambda e, nb=nb: e.activation(
                        brep[:, nb * CH:nb * CH + 512], ps_b[:, :], AF.Identity), inc="aa")
                    eb1 = EM("scalar", lambda e, nb=nb: e.activation(
                        brep[:, nb * CH + 512:(nb + 1) * CH], ps_c[:, :], AF.Identity), inc="aa")
                    W("tensor", "aa", eb0)
                    m3 = EM("tensor", lambda e, n=n, cb2=cb2: e.matmul(
                        ps_b[:, :], sel_bf[0:32, (16 + n) * 128:(17 + n) * 128],
                        bmc_bf[0:32, cb2 * CH:cb2 * CH + 512], start=True, stop=True), inc="mm")
                    W("tensor", "aa", eb1)
                    m4 = EM("tensor", lambda e, n=n, cb2=cb2: e.matmul(
                        ps_c[:, :], sel_bf[0:32, (16 + n) * 128:(17 + n) * 128],
                        bmc_bf[0:32, cb2 * CH + 512:(cb2 + 1) * CH], start=True, stop=True), inc="mm")
                    bc_mm_last[cnk] = m4
                    W("scalar", "mm", m4)
                    EM("scalar", lambda e, nb=nb: e.activation(
                        crep[:, nb * CH:nb * CH + 512], ps_b[:, :], AF.Identity), inc="aa")
                    ec1 = EM("scalar", lambda e, nb=nb: e.activation(
                        crep[:, nb * CH + 512:(nb + 1) * CH], ps_c[:, :], AF.Identity), inc="aa")
                    brep_ok[(cnk, n)] = (eb1, ec1)

                    for et in range(NET):
                        s = n * NET + et
                        sb_ = s % 2
                        pb = s % 3
                        cidx = et * N + n
                        if n == 0 and et == 0:
                            W("scalar", "vv", delta_ok[(cnk, NET - 1)])
                        tk = (cnk, s - 2) if s >= 2 else ((cnk - 1, N * NET - 2 + s) if cnk >= 1 else None)
                        if tk in tts_done:
                            W("scalar", "vv", tts_done[tk])
                        pexp[(cnk, n, et)] = EM("scalar", lambda e, et=et, n=n, sb_=sb_: e.activation(
                            p_buf[:, sb_ * CH:(sb_ + 1) * CH], delta[:, et * CH:(et + 1) * CH], AF.Exp,
                            scale=asb[:, et * 16 + n:et * 16 + n + 1]), inc="aa")
                        W("vector", "aa", brep_ok[(cnk, n)][0])
                        ug_done[(cnk, s)] = EM("vector", lambda e, et=et, nb=nb, sb_=sb_: e.tensor_mul(
                            ug_buf[:, sb_ * CH:(sb_ + 1) * CH], u_bf[:, et * CH:(et + 1) * CH],
                            brep[:, nb * CH:(nb + 1) * CH]), inc="vv")
                        W("vector", "aa", pexp[(cnk, n, et)])
                        if tk in prod_done:
                            W("vector", "gg", prod_done[tk])
                        if cnk == 0:
                            tts_done[(cnk, s)] = EM("vector", lambda e, sb_=sb_: e.tensor_tensor_scan(
                                h_buf[:, sb_ * CH:(sb_ + 1) * CH], p_buf[:, sb_ * CH:(sb_ + 1) * CH],
                                ug_buf[:, sb_ * CH:(sb_ + 1) * CH], 0.0, OP.mult, OP.add), inc="vv")
                        else:
                            tts_done[(cnk, s)] = EM("vector", lambda e, sb_=sb_, cidx=cidx: e.tensor_tensor_scan(
                                h_buf[:, sb_ * CH:(sb_ + 1) * CH], p_buf[:, sb_ * CH:(sb_ + 1) * CH],
                                ug_buf[:, sb_ * CH:(sb_ + 1) * CH], carry[:, cidx:cidx + 1],
                                OP.mult, OP.add), inc="vv")
                        EM("vector", lambda e, sb_=sb_, cidx=cidx: e.tensor_copy(
                            carry[:, cidx:cidx + 1], h_buf[:, sb_ * CH + CH - 1:sb_ * CH + CH]))
                        W("gpsimd", "vv", tts_done[(cnk, s)])
                        W("gpsimd", "aa", brep_ok[(cnk, n)][1])
                        if n == 0:
                            prod_done[(cnk, s)] = EM("gpsimd", lambda e, et=et, nb=nb, sb_=sb_: e.tensor_mul(
                                y_acc[:, et * CH:(et + 1) * CH], h_buf[:, sb_ * CH:(sb_ + 1) * CH],
                                crep[:, nb * CH:(nb + 1) * CH]), inc="gg")
                        else:
                            if pb in prod_war:
                                W("gpsimd", "vv", prod_war[pb])
                            prod_done[(cnk, s)] = EM("gpsimd", lambda e, nb=nb, sb_=sb_, pb=pb: e.tensor_mul(
                                prod[pb][:, :], h_buf[:, sb_ * CH:(sb_ + 1) * CH],
                                crep[:, nb * CH:(nb + 1) * CH]), inc="gg")
                            pending_adds.append((s, et, pb))
                        if len(pending_adds) > 2:
                            flush_add()
                        if giter[0] >= 8:
                            pull()
                        if giter[0] % 64 == 48:
                            if cnk + 1 < NCH:
                                chunk_loads(cnk + 1)
                        giter[0] += 1
                while pending_adds:
                    flush_add()

                for et in range(NET):
                    yb = et % 2
                    skkey = (cnk, et - 2) if et >= 2 else ((cnk - 1, et + 2) if cnk >= 1 else None)
                    if skkey in ymul_done:
                        W("sync", "gg", ymul_done[skkey])
                    EM("sync", lambda e, et=et, c0=c0, yb=yb: e.dma_start(
                        out=sk2c[yb][:, :], in_=SK2_SP.ap()[et * 128:(et + 1) * 128, c0:c0 + CH]),
                        inc="dsy", inck=16)
                    sk_ack[(cnk, et)] = sack()
                    stt_done[(cnk, et)] = EM("vector", lambda e, et=et, xb=xb: e.scalar_tensor_tensor(
                        y_acc[:, et * CH:(et + 1) * CH], xc_bf[xb][:, et * CH:(et + 1) * CH], ecol(et, 6),
                        y_acc[:, et * CH:(et + 1) * CH], OP.mult, OP.add), inc="vv")
                    W("gpsimd", "vv", stt_done[(cnk, et)])
                    W("gpsimd", "asy", sk_ack[(cnk, et)])
                    if skkey in ybf_ack:
                        W("gpsimd", "asy", ybf_ack[skkey])
                    ymul_done[(cnk, et)] = EM("gpsimd", lambda e, et=et, yb=yb: e.tensor_mul(
                        y_bf[yb][:, :], y_acc[:, et * CH:(et + 1) * CH], sk2c[yb][:, :]), inc="gg")
                    W("sync", "gg", ymul_done[(cnk, et)])
                    EM("sync", lambda e, et=et, c0=c0, yb=yb: e.dma_start(
                        out=ag_y_in.ap()[et * 128:(et + 1) * 128, c0:c0 + CH], in_=y_bf[yb][:, :]),
                        inc="dsy", inck=16)
                    ybf_ack[(cnk, et)] = sack()

            while state_lc[1] < NUNITS:
                pull()

            W("gpsimd", "asy", ybf_ack[(NCH - 1, NET - 1)])
            EM("gpsimd", lambda e: e.collective_compute(
                "AllGather", OP.bypass, replica_groups=[list(range(NCORE))],
                ins=[ag_y_in.ap().opt()], outs=[Y_ALL.ap().opt()]), inc="cc_y")
            p.barrier(dma_flush=[("gpsimd", "dgp"), ("sync", "dsy"), ("scalar", "dsc")])

        with contextlib.ExitStack() as ph:
            sb = lambda name, shape, dtype=F32: ph.enter_context(nc.sbuf_tensor(name, shape, dtype))
            psb = lambda name, shape: ph.enter_context(nc.psum_tensor(name, shape, F32))
            y_sb = sb("y_sb", [128, 32 * LC], BF16)
            owt_t = [sb(f"owt_t{i}", [128, 2 * 512], BF16) for i in range(3)]
            xn4 = [sb(f"xn4_{i}", [128, 4 * 512]) for i in range(2)]
            o24 = [sb(f"o24_{i}", [128, 4 * 512]) for i in range(2)]
            rs4 = [sb(f"rs4_{i}", [128, 4 * 512]) for i in range(2)]
            attn4 = [sb(f"attn4_{i}", [128, 4 * 512]) for i in range(2)]
            g4 = [sb(f"g4_{i}", [128, 4 * 512]) for i in range(2)]
            out4 = [sb(f"out4_{i}", [128, 4 * 512]) for i in range(2)]
            ps_o = [psb(f"ps_o{i}", [128, 512]) for i in range(8)]

            W("sync", "cc_y", 1)
            W("scalar", "cc_ow", 1)

            def y_load(e):
                for arm in e.switch_core_id(n=128):
                    lcore = physical_to_logical_core_id(int(arm) % 128, 0, container_num_cores=NCORE) % NCORE
                    c0 = lcore * LC
                    e.dma_start(
                        out=y_sb[:, :].rearrange("p (et l) -> p et l", et=32),
                        in_=Y_ALL.ap()[:, c0:c0 + LC].rearrange("(et p) l -> p et l", p=128),
                    ).then_inc(p.sems["dsy"], 16)
            p.c["dsy"] += 16
            p.prog["sync"].append(y_load)
            yl_ack = sack()

            owp_mm, comb_done, ld_cnt, od_ack = {}, {}, {}, {}
            aev = {}
            g8c = [0]
            for dc in range(4):
                for q in range(2):
                    if (dc - 1, q) in comb_done:
                        W("sync", "vv", comb_done[(dc - 1, q)])
                    for arr, src_ap in ((xn4, XN_SP.ap()), (o24, OUT2_SP.ap()), (rs4, resid)):
                        EM("sync", lambda e, arr=arr, src_ap=src_ap, q=q, dc=dc: e.dma_start(
                            out=arr[q][:, :].rearrange("p (i c) -> p i c", i=4),
                            in_=src_ap[q * 512:(q + 1) * 512, dc * 512:(dc + 1) * 512]
                            .rearrange("(i p) c -> p i c", p=128)), inc=f"sw{q}", inck=16)
                    ld_cnt[(dc, q)] = p.c[f"sw{q}"]
                for pr8 in range(16):
                    g8 = g8c[0]
                    g8c[0] += 1
                    b = g8 % 3
                    et8 = 2 * pr8
                    if (g8 - 3) in owp_mm:
                        W("scalar", "mm", owp_mm[g8 - 3])
                    EM("scalar", lambda e, et8=et8, dc=dc, b=b: e.dma_start(
                        out=owt_t[b][:, :].rearrange("p (i c) -> p i c", i=2),
                        in_=OWT.ap()[et8 * 128:(et8 + 2) * 128, dc * 512:(dc + 1) * 512]
                        .rearrange("(i p) c -> p i c", p=128)), inc=f"fw{b}", inck=16)
                    W("tensor", f"fw{b}", p.c[f"fw{b}"])
                    if pr8 == 0:
                        W("tensor", "asy", yl_ack)
                        if dc > 0:
                            W("tensor", "aa", aev[(dc - 1, 7)])
                    for ii in range(2):
                        for lt in range(8):
                            owp_mm[g8] = EM("tensor", lambda e, et8=et8, ii=ii, b=b, lt=lt: e.matmul(
                                ps_o[lt][:, :],
                                y_sb[:, (et8 + ii) * LC + lt * 128:(et8 + ii) * LC + (lt + 1) * 128],
                                owt_t[b][:, ii * 512:(ii + 1) * 512],
                                start=(et8 + ii == 0), stop=(et8 + ii == 31)),
                                inc="mm" if (ii == 1 and lt == 7) else None)
                W("scalar", "mm", owp_mm[g8c[0] - 1])
                for lt in range(8):
                    q = lt // 4
                    if lt % 4 == 0 and (dc - 1, q) in comb_done:
                        W("scalar", "vv", comb_done[(dc - 1, q)])
                    aev[(dc, lt)] = EM("scalar", lambda e, lt=lt, q=q: e.activation(
                        attn4[q][:, (lt % 4) * 512:(lt % 4 + 1) * 512], ps_o[lt][:, :], AF.Identity),
                        inc="aa")
                    if lt % 4 == 3:
                        W("vector", "aa", aev[(dc, lt)])
                        W("vector", f"sw{q}", ld_cnt[(dc, q)])
                        EM("vector", lambda e, q=q: e.tensor_add(
                            attn4[q][:, :], attn4[q][:, :], xn4[q][:, :]), inc="vv")
                        sgq = EM("scalar", lambda e, q=q: e.activation(
                            g4[q][:, :], o24[q][:, :], AF.Silu), inc="aa")
                        W("vector", "aa", sgq)
                        EM("vector", lambda e, q=q: e.tensor_mul(attn4[q][:, :], attn4[q][:, :], g4[q][:, :]),
                           inc="vv")
                        if (dc - 1, q) in od_ack:
                            W("vector", "asy", od_ack[(dc - 1, q)])
                        cd = EM("vector", lambda e, q=q: e.tensor_add(
                            out4[q][:, :], rs4[q][:, :], attn4[q][:, :]), inc="vv")
                        comb_done[(dc, q)] = cd
                        W("sync", "vv", cd)
                        EM("sync", lambda e, q=q, dc=dc: e.dma_start(
                            out=out[q * 512:(q + 1) * 512, dc * 512:(dc + 1) * 512]
                            .rearrange("(i p) c -> p i c", p=128),
                            in_=out4[q][:, :].rearrange("p (i c) -> p i c", i=4)), inc="dsy", inck=16)
                        od_ack[(dc, q)] = sack()

            if debug:
                for src, dst in [(XN_SP, dbg["dbg_xn"]), (ag_rn_in, dbg["dbg_rn"]),
                                 (ar_out, dbg["dbg_d1bc"]), (SK2_SP, dbg["dbg_sk2"]),
                                 (ag_y_in, dbg["dbg_y"]), (OUT2_SP, dbg["dbg_out2"]),
                                 (X_SP, dbg["dbg_x"])]:
                    EM("gpsimd", lambda e, src=src, dst=dst: e.dma_start(out=dst, in_=src.ap()),
                       inc="dgp", inck=16)
            W("gpsimd", "dgp", p.c["dgp"])
            p.barrier(dma_flush=[("gpsimd", "dgp"), ("sync", "dsy")])

        return _finish(nc, p)


def _finish(nc, p):
    with nc.Block() as block:
        @block.sync
        def _(e):
            for fn in p.prog["sync"]:
                fn(e)

        @block.gpsimd
        def _(e):
            for fn in p.prog["gpsimd"]:
                fn(e)

        @block.vector
        def _(e):
            for fn in p.prog["vector"]:
                fn(e)

        @block.scalar
        def _(e):
            for fn in p.prog["scalar"]:
                fn(e)

        @block.tensor
        def _(e):
            for fn in p.prog["tensor"]:
                fn(e)
    return nc


def make_in_maps(inputs):
    x0 = np.asarray(inputs["L0_attn_resid"]).reshape(L, D)
    norm_w = np.asarray(inputs["L0_attn_norm_w"])
    skip_w = np.asarray(inputs["L0_attn_skip_w"])
    in_w = np.asarray(inputs["L0_attn_in_w"])
    conv_w = np.asarray(inputs["L0_attn_conv_w"])
    conv_b = np.asarray(inputs["L0_attn_conv_b"])
    wd1 = np.asarray(inputs["L0_attn_wd1"])
    wd2 = np.asarray(inputs["L0_attn_wd2"])
    wd2_b = np.asarray(inputs["L0_attn_wd2_b"])
    wB = np.asarray(inputs["L0_attn_wB"])
    wC = np.asarray(inputs["L0_attn_wC"])
    A_log = np.asarray(inputs["L0_attn_A_log"])
    W_D = np.asarray(inputs["L0_attn_W_D"])
    out_w = np.asarray(inputs["L0_attn_out_w"])
    wgu = np.asarray(inputs["L0_ffn_w_gate_up"])
    wdn = np.asarray(inputs["L0_ffn_w_down"])

    nw_rep = np.ascontiguousarray(np.broadcast_to(norm_w[None, :], (128, D)).astype(np.float32))
    wguT = np.ascontiguousarray(wgu.T)
    gtile = wguT[:, :I].reshape(16, 128, 64, 128).transpose(2, 1, 0, 3).reshape(64, 262144)
    utile = wguT[:, I:].reshape(16, 128, 64, 128).transpose(2, 1, 0, 3).reshape(64, 262144)
    wgut_tiled = np.ascontiguousarray(np.concatenate([gtile, utile], 0))
    wdnT = np.ascontiguousarray(wdn.T)
    owT = np.ascontiguousarray(out_w.T)
    sel = np.zeros((32, 32 * 128), np.float32)
    for k in range(32):
        sel[k, k * 128:(k + 1) * 128] = 1.0
    sel_bf = sel.astype(ml_dtypes.bfloat16)

    in_maps = []
    for i in range(NCORE):
        ls = slice(i * LC, (i + 1) * LC)
        es = slice(i * EC, (i + 1) * EC)
        ev = np.zeros((EC, 24), np.float32)
        ev[:, 0:4] = conv_w[es]
        ev[:, 4] = conv_b[es]
        ev[:, 5] = wd2_b[es]
        ev[:, 6] = W_D[es]
        ev[:, 8:24] = A_log[es]
        in_maps.append({
            "resid": np.ascontiguousarray(x0[ls]),
            "nw_rep": nw_rep,
            "inwT": np.ascontiguousarray(in_w[es].T),
            "skwT": np.ascontiguousarray(skip_w[es].T),
            "wd1T": np.ascontiguousarray(wd1[:, es].T),
            "wd2T": np.ascontiguousarray(wd2[es].T),
            "wbcT": np.ascontiguousarray(np.concatenate([wB[:, es].T, wC[:, es].T], axis=1)),
            "evec": ev,
            "sel_in": sel_bf,
            "wguT_sl": np.ascontiguousarray(
                wgut_tiled[i * 16:(i + 1) * 16].reshape(128, 32768)),
            "wdnT_sl": np.ascontiguousarray(wdnT[i * (I // NCORE):(i + 1) * (I // NCORE)]),
            "owT_sl": np.ascontiguousarray(owT[i * (E // NCORE):(i + 1) * (E // NCORE)]),
        })
    return in_maps


def kernel(**inputs):
    in_maps = make_in_maps(inputs)
    nc = build_nc()
    res = run_bass_kernel_spmd(nc, in_maps, core_ids=list(range(NCORE)))
    outp = np.concatenate([res.results[i]["out"] for i in range(NCORE)], axis=0)
    return outp.reshape(1, L, D).astype(np.float32)


# version 12
# speedup vs baseline: 1.4089x; 1.1149x over previous
import contextlib
import numpy as np
import ml_dtypes
import concourse.bass as bass
import concourse.mybir as mybir
from concourse.bass_utils import run_bass_kernel_spmd
from concourse.libnrt import physical_to_logical_core_id

F32 = mybir.dt.float32
BF16 = mybir.dt.bfloat16
AF = mybir.ActivationFunctionType
OP = mybir.AluOpType

NCORE = 8
L = 8192
D = 2048
E = 4096
N = 16
R = 128
I = 8192
LC = L // NCORE
EC = E // NCORE
NET = EC // 128
CH = 1024
NCH = L // CH
EPS = 1e-5

ENGINES = ("sync", "gpsimd", "vector", "scalar", "tensor")


class P:

    def __init__(self, nc, stack, semnames):
        self.nc = nc
        self.prog = {e: [] for e in ENGINES}
        self.sems = {s: stack.enter_context(nc.semaphore(s)) for s in semnames}
        self.c = {s: 0 for s in semnames}
        self._last = {}
        self.nbar = 0

    def wait(self, eng, sem, val):
        if val is None or val <= 0:
            return
        self.prog[eng].append(lambda e, s=sem, v=val: e.wait_ge(self.sems[s], v))

    CHAIN = {"vector": "vv", "scalar": "aa", "gpsimd": "gg"}

    def emit(self, eng, builder, inc=None, inck=1):
        chain_sem = self.CHAIN.get(eng)
        dma_like = (inc in ("dgp", "dsy", "dsc") or (inc or "").startswith("cc")
                    or (inc or "").startswith("fw") or (inc or "").startswith("sw"))
        if chain_sem is not None and not dma_like:
            self.wait(eng, chain_sem, self._last.get(eng, 0))
            if inc is None:
                inc, inck = chain_sem, 1
            assert inc == chain_sem, (eng, inc)
        if inc is not None:
            self.c[inc] += inck
            v = self.c[inc]

            def fn(e, b=builder, s=inc, k=inck):
                b(e).then_inc(self.sems[s], k)
            self.prog[eng].append(fn)
            if chain_sem is not None and not dma_like:
                self._last[eng] = v
            return v
        self.prog[eng].append(lambda e, b=builder: b(e))
        return None

    def ack(self, eng, qsem, asem):
        self.wait(eng, qsem, self.c[qsem])
        self.c[asem] += 1
        v = self.c[asem]
        self.prog[eng].append(lambda e, s=asem: e.sem_inc(self.sems[s], 1))
        return v

    def barrier(self, dma_flush=()):
        self.nbar += 1
        target = 5 * self.nbar
        flush = {}
        for eng, s in dma_flush:
            flush.setdefault(eng, []).append((s, self.c[s]))
        for eng in ENGINES:
            for s, v in flush.get(eng, []):
                self.wait(eng, s, v)
            self.prog[eng].append(lambda e: e.sem_inc(self.sems["bar"], 1))
            self.wait(eng, "bar", target)


def build_nc(debug=False, preamble="clear_sled"):
    nc = bass.Bass("TRN2", target_bir_lowering=False)

    def din(name, shape, dtype=F32):
        return nc.dram_tensor(name, shape, dtype, kind="ExternalInput").ap()

    resid = din("resid", [LC, D])
    nw_rep = din("nw_rep", [128, D])
    inwT = din("inwT", [D, EC])
    skwT = din("skwT", [D, EC])
    wd1T = din("wd1T", [EC, R])
    wd2T = din("wd2T", [R, EC])
    wbcT = din("wbcT", [EC, 32])
    evec = din("evec", [EC, 24])
    sel_in = din("sel_in", [32, 32 * 128], BF16)
    wguT_sl = din("wguT_sl", [128, 32768])
    wdnT_sl = din("wdnT_sl", [I // NCORE, D])
    owT_sl = din("owT_sl", [E // NCORE, D])
    out = nc.dram_tensor("out", [LC, D], F32, kind="ExternalOutput").ap()

    ag_rn_in = [nc.dram_tensor(f"ag_rn_in{i}", [D, 512], BF16) for i in range(2)]
    RNT_H = [nc.dram_tensor(f"RNT_H{i}", [NCORE * D, 512], BF16, addr_space="Shared") for i in range(2)]
    XNT_SP = nc.dram_tensor("XNT_SP", [D, LC], BF16)
    ag_wgu_in = nc.dram_tensor("ag_wgu_in", [128, 32768], BF16)
    WGUT = nc.dram_tensor("WGUT", [128, 262144], BF16, addr_space="Shared")
    ag_wdn_in = nc.dram_tensor("ag_wdn_in", [I // NCORE, D], BF16)
    WDNT = nc.dram_tensor("WDNT", [I, D], BF16, addr_space="Shared")
    ag_ow_in = nc.dram_tensor("ag_ow_in", [E // NCORE, D], BF16)
    OWT = nc.dram_tensor("OWT", [E, D], BF16, addr_space="Shared")
    ar_in = nc.dram_tensor("ar_in", [R + 32, L], F32)
    ar_out = nc.dram_tensor("ar_out", [R + 32, L], F32, addr_space="Shared")
    ag_y_in = nc.dram_tensor("ag_y_in", [EC, L], BF16)
    Y_ALL = nc.dram_tensor("Y_ALL", [E, L], BF16, addr_space="Shared")
    XN_SP = nc.dram_tensor("XN_SP", [LC, D], F32)
    SK2_SP = nc.dram_tensor("SK2_SP", [EC, L], F32)
    OUT2_SP = nc.dram_tensor("OUT2_SP", [LC, D], BF16)
    X_SP = nc.dram_tensor("X_SP", [EC, L], BF16)
    H_SP = nc.dram_tensor("H_SP", [I, LC], BF16)

    dbg = {}
    if debug:
        for nm, shape, dt in [
            ("dbg_xn", [LC, D], F32), ("dbg_rn", [LC, D], BF16),
            ("dbg_d1bc", [R + 32, L], F32), ("dbg_x", [EC, L], BF16),
            ("dbg_sk2", [EC, L], F32), ("dbg_delta", [EC, L], F32),
            ("dbg_y", [EC, L], BF16), ("dbg_out2", [LC, D], F32),
            ("dbg_scan", [128, 6 * CH], F32),
        ]:
            dbg[nm] = nc.dram_tensor(nm, shape, dt, kind="ExternalOutput").ap()

    semnames = ["bar", "dsy", "dgp", "dsc", "asy", "agp", "asc", "mm", "vv", "aa", "gg",
                "fw0", "fw1", "fw2", "fw3", "sw0", "sw1", "sw2", "sw3", "cc_rn2",
                "cc_rn", "cc_wgu", "cc_wdn", "cc_ow", "cc_ar", "cc_y"]

    with contextlib.ExitStack() as g:
        p = P(nc, g, semnames)
        W, EM = p.wait, p.emit

        def gack():
            return p.ack("gpsimd", "dgp", "agp")

        def sack():
            return p.ack("sync", "dsy", "asy")

        nums = sorted(s.num for s in p.sems.values())
        assert nums == list(range(nums[0], nums[0] + len(nums))), nums
        if preamble == "clear_sled":
            for _ in range(40):
                nc.gpsimd.nop(cycle_cnt=2400)
            nc.gpsimd.sem_clear(range(nums[0], nums[-1] + 1))
            for eng in (nc.sync, nc.vector, nc.scalar, nc.tensor):
                for _ in range(120):
                    eng.nop(cycle_cnt=2400)
            nc.all_engine_barrier()

        sbg = lambda name, shape, dtype=F32: g.enter_context(nc.sbuf_tensor(name, shape, dtype))
        evs = sbg("evs", [128, NET * 24])
        asb = sbg("asb", [128, NET * 16])
        carry = sbg("carry", [128, NET * N])

        def ecol(et, c):
            return evs[:, et * 24 + c:et * 24 + c + 1]

        with contextlib.ExitStack() as ph:
            sb = lambda name, shape, dtype=F32: ph.enter_context(nc.sbuf_tensor(name, shape, dtype))
            nwr = sb("nwr", [128, D])
            rsd = [sb(f"rsd{i}", [128, D]) for i in range(2)]
            sq_scr = sb("sq_scr", [128, D])
            xn_t = [sb(f"xn_t{i}", [128, D]) for i in range(2)]
            rn_bf = [sb(f"rn_bf{i}", [128, D], BF16) for i in range(2)]
            xn_bf = [sb(f"xn_bf{i}", [128, D], BF16) for i in range(2)]
            rnT_sb = [sb(f"rnT_sb{i}", [128, D], BF16) for i in range(2)]
            xnT_sb = [sb(f"xnT_sb{i}", [128, D], BF16) for i in range(2)]
            idn = sb("idn", [128, 128], BF16)
            stat = sb("stat", [128, 8])
            wf32 = sb("wf32", [128, 4 * 2048])
            wbf = sb("wbf", [128, 4 * 2048], BF16)
            pt = [ph.enter_context(nc.psum_tensor(f"pt{i}", [128, 128], BF16)) for i in range(4)]

            EM("gpsimd", lambda e: e.dma_start(out=nwr[:, :], in_=nw_rep), inc="dgp", inck=16)
            EM("gpsimd", lambda e: e.dma_start(
                out=evs[:, :].rearrange("p (t c) -> p t c", t=NET),
                in_=evec.rearrange("(t p) c -> p t c", p=128)), inc="dgp", inck=16)
            a_ev = gack()
            EM("gpsimd", lambda e: e.memset(idn[:, :], 0.0), inc="gg")
            idv = EM("gpsimd", lambda e: e.affine_select(
                out=idn[:, :], in_=idn[:, :], compare_op=OP.not_equal, fill=1.0,
                base=0, pattern=[[-1, 128]], channel_multiplier=1), inc="gg")
            W("scalar", "agp", a_ev)
            EM("scalar", lambda e: e.activation(
                asb[:, :].rearrange("p (t n) -> p t n", t=NET),
                evs[:, :].rearrange("p (t c) -> p t c", t=NET)[:, :, 8:24], AF.Exp), inc="aa")
            W("vector", "aa", p.c["aa"])
            EM("vector", lambda e: e.tensor_scalar_mul(asb[:, :], asb[:, :], -1.0), inc="vv")

            cast_srcs = []
            for k in range(16):
                cast_srcs.append((wguT_sl, ag_wgu_in, 0, k * 2048))
            for rt in range(8):
                cast_srcs.append((wdnT_sl, ag_wdn_in, rt * 128, 0))
            for rt in range(4):
                cast_srcs.append((owT_sl, ag_ow_in, rt * 128, 0))
            castv, swc = {}, {}
            cu = [0]

            def emit_cast_unit():
                k = cu[0]
                if k >= len(cast_srcs):
                    return
                cu[0] += 1
                s = k % 4
                src_t, dst_t, r0, c0 = cast_srcs[k]
                if (k - 4) in castv:
                    W("gpsimd", "vv", castv[k - 4])
                EM("gpsimd", lambda e, src_t=src_t, r0=r0, c0=c0, s=s: e.dma_start(
                    out=wf32[:, s * 2048:(s + 1) * 2048],
                    in_=src_t[r0:r0 + 128, c0:c0 + 2048]), inc=f"fw{s}", inck=16)
                fcnt = p.c[f"fw{s}"]
                W("vector", f"fw{s}", fcnt)
                if (k - 4) in swc:
                    W("vector", f"sw{s}", swc[k - 4])
                castv[k] = EM("vector", lambda e, s=s: e.tensor_copy(
                    wbf[:, s * 2048:(s + 1) * 2048], wf32[:, s * 2048:(s + 1) * 2048]), inc="vv")
                W("sync", "vv", castv[k])
                EM("sync", lambda e, dst_t=dst_t, r0=r0, c0=c0, s=s: e.dma_start(
                    out=dst_t.ap()[r0:r0 + 128, c0:c0 + 2048],
                    in_=wbf[:, s * 2048:(s + 1) * 2048]), inc=f"sw{s}", inck=16)
                swc[k] = p.c[f"sw{s}"]

            rsd_ack = {}
            for t in range(2):
                EM("sync", lambda e, t=t: e.dma_start(
                    out=rsd[t % 2][:, :], in_=resid[t * 128:(t + 1) * 128, :]), inc="dsy", inck=16)
                rsd_ack[t] = sack()

            tT_ack = {}
            xnf = {}
            for t in range(8):
                b = t % 2
                W("vector", "asy", rsd_ack[t])
                rs1 = EM("vector", lambda e, b=b: e.reduce_sum(
                    stat[:, 0:1], rsd[b][:, :], axis=mybir.AxisListType.X), inc="vv")
                W("vector", "vv", rs1)
                EM("vector", lambda e: e.tensor_scalar_mul(stat[:, 0:1], stat[:, 0:1], 1.0 / D))
                sqv = EM("vector", lambda e, b=b: e.tensor_mul(sq_scr[:, :], rsd[b][:, :], rsd[b][:, :]),
                         inc="vv")
                W("vector", "vv", sqv)
                rs2 = EM("vector", lambda e: e.reduce_sum(
                    stat[:, 1:2], sq_scr[:, :], axis=mybir.AxisListType.X), inc="vv")
                W("vector", "vv", rs2)
                EM("vector", lambda e: e.tensor_scalar_mul(stat[:, 1:2], stat[:, 1:2], 1.0 / D))
                EM("vector", lambda e: e.tensor_mul(stat[:, 7:8], stat[:, 0:1], stat[:, 0:1]))
                EM("vector", lambda e: e.tensor_sub(stat[:, 2:3], stat[:, 1:2], stat[:, 7:8]))
                vv1 = EM("vector", lambda e: e.tensor_scalar(
                    stat[:, 3:4], stat[:, 2:3], 1.0, EPS, OP.mult, OP.add), inc="vv")
                W("scalar", "vv", vv1)
                EM("scalar", lambda e: e.activation(stat[:, 4:5], stat[:, 3:4], AF.Ln))
                aa1 = EM("scalar", lambda e: e.activation(stat[:, 5:6], stat[:, 4:5], AF.Exp, scale=-0.5),
                         inc="aa")
                aa1b = EM("scalar", lambda e: e.activation(stat[:, 6:7], stat[:, 4:5], AF.Exp, scale=-1.0),
                          inc="aa")
                W("vector", "aa", aa1b)
                if t >= 2:
                    W("vector", "asy", xnf[t - 2])
                xnv = EM("vector", lambda e, b=b: e.tensor_scalar(
                    xn_t[b][:, :], rsd[b][:, :], stat[:, 0:1], stat[:, 5:6], OP.subtract, OP.mult), inc="vv")
                if t + 2 < 8:
                    W("sync", "vv", xnv)
                    EM("sync", lambda e, t=t: e.dma_start(
                        out=rsd[t % 2][:, :], in_=resid[(t + 2) * 128:(t + 3) * 128, :]), inc="dsy", inck=16)
                    rsd_ack[t + 2] = sack()
                EM("vector", lambda e: e.tensor_mul(stat[:, 6:7], stat[:, 2:3], stat[:, 6:7]))
                vv2 = EM("vector", lambda e: e.tensor_scalar(
                    stat[:, 6:7], stat[:, 6:7], 1.0, EPS, OP.mult, OP.add), inc="vv")
                W("scalar", "vv", vv2)
                EM("scalar", lambda e: e.activation(stat[:, 6:7], stat[:, 6:7], AF.Ln))
                aa2 = EM("scalar", lambda e: e.activation(stat[:, 6:7], stat[:, 6:7], AF.Exp, scale=-0.5),
                         inc="aa")
                W("vector", "aa", aa2)
                rnv = EM("vector", lambda e, b=b: e.scalar_tensor_tensor(
                    rn_bf[b][:, :], xn_t[b][:, :], stat[:, 6:7], nwr[:, :], OP.mult, OP.mult), inc="vv")
                vd = EM("vector", lambda e, b=b: e.tensor_copy(xn_bf[b][:, :], xn_t[b][:, :]), inc="vv")
                W("sync", "vv", vd)
                EM("sync", lambda e, b=b, t=t: e.dma_start(
                    out=XN_SP.ap()[t * 128:(t + 1) * 128, :], in_=xn_t[b][:, :]), inc="dsy", inck=16)
                xnf[t] = sack()
                W("tensor", "vv", vd)
                if t >= 2:
                    W("tensor", "asy", tT_ack[t - 2])
                tcp = {}
                if t == 0:
                    W("tensor", "gg", idv)
                for half, (srcb, dstb) in enumerate([(rn_bf[b], rnT_sb[b]), (xn_bf[b], xnT_sb[b])]):
                    for dt in range(16):
                        j = half * 16 + dt
                        pj = j % 4
                        if (j - 4) in tcp:
                            W("tensor", "aa", tcp[j - 4])
                        tmm = EM("tensor", lambda e, srcb=srcb, dt=dt, pj=pj: e.transpose(
                            pt[pj][:, :], srcb[:, dt * 128:(dt + 1) * 128], idn[:, :]), inc="mm")
                        W("scalar", "mm", tmm)
                        tcp[j] = EM("scalar", lambda e, dstb=dstb, dt=dt, pj=pj: e.activation(
                            dstb[:, dt * 128:(dt + 1) * 128], pt[pj][:, :], AF.Identity), inc="aa")
                W("sync", "aa", tcp[31])
                EM("sync", lambda e, b=b, t=t: e.dma_start(
                    out=ag_rn_in[t // 4].ap()[:, (t % 4) * 128:(t % 4 + 1) * 128]
                    .rearrange("(d p) l -> p d l", p=128),
                    in_=rnT_sb[b][:, :].rearrange("p (d l) -> p d l", d=16)), inc="dsy", inck=16)
                EM("sync", lambda e, b=b, t=t: e.dma_start(
                    out=XNT_SP.ap()[:, t * 128:(t + 1) * 128].rearrange("(d p) l -> p d l", p=128),
                    in_=xnT_sb[b][:, :].rearrange("p (d l) -> p d l", d=16)), inc="dsy", inck=16)
                tT_ack[t] = sack()
                if t == 3:
                    W("gpsimd", "asy", tT_ack[3])
                    EM("gpsimd", lambda e: e.collective_compute(
                        "AllGather", OP.bypass, replica_groups=[list(range(NCORE))],
                        ins=[ag_rn_in[0].ap().opt()], outs=[RNT_H[0].ap().opt()]), inc="cc_rn")
                for _ in range(4):
                    emit_cast_unit()

            W("gpsimd", "asy", tT_ack[7])
            EM("gpsimd", lambda e: e.collective_compute(
                "AllGather", OP.bypass, replica_groups=[list(range(NCORE))],
                ins=[ag_rn_in[1].ap().opt()], outs=[RNT_H[1].ap().opt()]), inc="cc_rn2")
            while cu[0] < len(cast_srcs):
                emit_cast_unit()
            for s in range(4):
                W("gpsimd", f"sw{s}", p.c[f"sw{s}"])
            EM("gpsimd", lambda e: e.collective_compute(
                "AllGather", OP.bypass, replica_groups=[list(range(NCORE))],
                ins=[ag_wgu_in.ap().opt()], outs=[WGUT.ap().opt()]), inc="cc_wgu")
            p.barrier(dma_flush=[("gpsimd", "dgp"), ("sync", "dsy")])

        with contextlib.ExitStack() as ph:
            sb = lambda name, shape, dtype=F32: ph.enter_context(nc.sbuf_tensor(name, shape, dtype))
            psb = lambda name, shape: ph.enter_context(nc.psum_tensor(name, shape, F32))
            wstage = sb("wstage", [128, 16 * 512])
            inw_bf = sb("inw_bf", [128, 16 * 512], BF16)
            skw_bf = sb("skw_bf", [128, 16 * 512], BF16)
            wd1_bf = sb("wd1_bf", [128, NET * 128], BF16)
            wbc_bf = sb("wbc_bf", [128, NET * 32], BF16)
            rhs = [sb(f"rhs{i}", [128, 16 * 512], BF16) for i in range(2)]
            xin_buf = [sb(f"xinb{et}", [128, 515]) for et in range(NET)]
            acc = [sb(f"acc{et}", [128, 512]) for et in range(NET)]
            sk2_t = [[sb(f"sk2t{i}_{et}", [128, 512]) for et in range(NET)] for i in range(2)]
            d1e = [sb(f"d1e{i}", [128, 512]) for i in range(2)]
            bce = [sb(f"bce{i}", [128, 512]) for i in range(2)]
            x_bf = sb("x_bf", [128, NET * L], BF16)
            ps_xin = [psb(f"ps_xin{i}", [128, 512]) for i in range(2)]
            ps_skip = [psb(f"ps_skip{i}", [128, 512]) for i in range(2)]
            ps_d1 = psb("ps_d1", [128, 512])
            ps_bc = psb("ps_bc", [128, 512])

            def load_cast(src, w, sl, nfree):
                EM("gpsimd", lambda e, src=src, sl=sl, nfree=nfree: e.dma_start(
                    out=wstage[:, 0:nfree].rearrange("p (t c) -> p t c", t=sl),
                    in_=src.rearrange("(t p) c -> p t c", p=128)), inc="dgp", inck=16)
                a_ld = gack()
                W("vector", "agp", a_ld)
                cv = EM("vector", lambda e, w=w, nfree=nfree: e.tensor_copy(w[:, 0:nfree], wstage[:, 0:nfree]),
                        inc="vv")
                W("gpsimd", "vv", cv)

            load_cast(inwT, inw_bf, 16, 16 * 512)
            load_cast(skwT, skw_bf, 16, 16 * 512)
            load_cast(wd1T, wd1_bf, NET, NET * 128)
            load_cast(wbcT, wbc_bf, NET, NET * 32)
            for et in range(NET - 1):
                EM("vector", lambda e, et=et: e.memset(xin_buf[et][:, 0:3], 0.0))
            v_w = EM("vector", lambda e: e.memset(xin_buf[NET - 1][:, 0:3], 0.0), inc="vv")

            W("sync", "cc_rn", 1)
            NLC = 16
            mm_xin, mm_skip, evac_xin, evac_skip = {}, {}, {}, {}
            xsilu, sk2_ack = {}, {}
            rhs_cnt = {}

            def rhs_load(lc):
                s = lc % 2
                c, hh = lc // 2, lc % 2
                if lc == 1:
                    W("sync", "cc_rn2", 1)
                EM("sync", lambda e, s=s, c=c, hh=hh: e.dma_start(
                    out=rhs[s][:, :].rearrange("p (d l) -> p d l", d=16),
                    in_=RNT_H[hh].ap()[c * D:(c + 1) * D, :]
                    .rearrange("(d p) l -> p d l", p=128)), inc=f"fw{s}", inck=16)
                rhs_cnt[lc] = p.c[f"fw{s}"]

            d1mm, bcmm, d1ev, bcev, d1_ack = {}, {}, {}, {}, {}
            rhs_load(0)
            for lc in range(NLC):
                b = lc % 2
                if lc + 1 < NLC:
                    if lc >= 1:
                        W("sync", "mm", mm_skip[(lc - 1, NET - 1)])
                    rhs_load(lc + 1)
                W("tensor", f"fw{b}", rhs_cnt[lc])
                if lc == 0:
                    W("tensor", "vv", v_w)
                for et in range(NET):
                    pb = et % 2
                    prev = (lc, et - 2) if et >= 2 else (lc - 1, et + 2)
                    if prev in evac_xin:
                        W("tensor", "vv", evac_xin[prev])
                    for dt in range(16):
                        mm_xin[(lc, et)] = EM("tensor", lambda e, et=et, dt=dt, b=b, pb=pb: e.matmul(
                            ps_xin[pb][:, :], inw_bf[:, dt * 512 + et * 128:dt * 512 + (et + 1) * 128],
                            rhs[b][:, dt * 512:(dt + 1) * 512], start=(dt == 0), stop=(dt == 15)),
                            inc="mm" if dt == 15 else None)
                    if prev in evac_skip:
                        W("tensor", "aa", evac_skip[prev])
                    for dt in range(16):
                        mm_skip[(lc, et)] = EM("tensor", lambda e, et=et, dt=dt, b=b, pb=pb: e.matmul(
                            ps_skip[pb][:, :], skw_bf[:, dt * 512 + et * 128:dt * 512 + (et + 1) * 128],
                            rhs[b][:, dt * 512:(dt + 1) * 512], start=(dt == 0), stop=(dt == 15)),
                            inc="mm" if dt == 15 else None)

                    W("vector", "mm", mm_xin[(lc, et)])
                    if (lc - 1, et) in xsilu:
                        W("vector", "aa", xsilu[(lc - 1, et)])
                    evv = EM("vector", lambda e, et=et, pb=pb: e.tensor_copy(
                        xin_buf[et][:, 3:515], ps_xin[pb][:, :]), inc="vv")
                    W("vector", "vv", evv)
                    cv0 = EM("vector", lambda e, et=et: e.tensor_scalar(
                        acc[et][:, :], xin_buf[et][:, 0:512], ecol(et, 0), None, OP.mult), inc="vv")
                    for k in range(1, 4):
                        W("vector", "vv", cv0)
                        cv0 = EM("vector", lambda e, et=et, k=k: e.scalar_tensor_tensor(
                            acc[et][:, :], xin_buf[et][:, k:k + 512], ecol(et, k),
                            acc[et][:, :], OP.mult, OP.add), inc="vv")
                    evac_xin[(lc, et)] = cv0
                    EM("vector", lambda e, et=et: e.tensor_copy(
                        xin_buf[et][:, 0:3], xin_buf[et][:, 512:515]), inc="vv")
                    W("scalar", "vv", evac_xin[(lc, et)])
                    xsilu[(lc, et)] = EM("scalar", lambda e, et=et, lc=lc: e.activation(
                        x_bf[:, et * L + lc * 512:et * L + (lc + 1) * 512], acc[et][:, :], AF.Silu,
                        bias=ecol(et, 4)), inc="aa")
                    W("scalar", "mm", mm_skip[(lc, et)])
                    if (lc - 2, et) in sk2_ack:
                        W("scalar", "agp", sk2_ack[(lc - 2, et)])
                    evac_skip[(lc, et)] = EM("scalar", lambda e, et=et, pb=pb, b=b: e.activation(
                        sk2_t[b][et][:, :], ps_skip[pb][:, :], AF.Silu), inc="aa")
                    W("gpsimd", "aa", evac_skip[(lc, et)])
                    EM("gpsimd", lambda e, et=et, lc=lc, b=b: e.dma_start(
                        out=SK2_SP.ap()[et * 128:(et + 1) * 128, lc * 512:(lc + 1) * 512],
                        in_=sk2_t[b][et][:, :]), inc="dgp", inck=16)
                    sk2_ack[(lc, et)] = gack()

                W("tensor", "aa", xsilu[(lc, NET - 1)])
                if lc >= 1:
                    W("tensor", "vv", d1ev[lc - 1])
                for et in range(NET):
                    d1mm[lc] = EM("tensor", lambda e, et=et, lc=lc: e.matmul(
                        ps_d1[:, :], wd1_bf[:, et * 128:(et + 1) * 128],
                        x_bf[:, et * L + lc * 512:et * L + (lc + 1) * 512],
                        start=(et == 0), stop=(et == NET - 1)),
                        inc="mm" if et == NET - 1 else None)
                if lc >= 1:
                    W("tensor", "vv", bcev[lc - 1])
                for et in range(NET):
                    bcmm[lc] = EM("tensor", lambda e, et=et, lc=lc: e.matmul(
                        ps_bc[0:32, :], wbc_bf[:, et * 32:(et + 1) * 32],
                        x_bf[:, et * L + lc * 512:et * L + (lc + 1) * 512],
                        start=(et == 0), stop=(et == NET - 1)),
                        inc="mm" if et == NET - 1 else None)
                W("vector", "mm", d1mm[lc])
                if lc >= 2:
                    W("vector", "agp", d1_ack[lc - 2])
                d1ev[lc] = EM("vector", lambda e, b=b: e.tensor_copy(d1e[b][:, :], ps_d1[:, :]), inc="vv")
                W("vector", "mm", bcmm[lc])
                bcev[lc] = EM("vector", lambda e, b=b: e.tensor_copy(bce[b][0:32, :], ps_bc[0:32, :]),
                              inc="vv")
                W("gpsimd", "vv", bcev[lc])
                EM("gpsimd", lambda e, lc=lc, b=b: e.dma_start(
                    out=ar_in.ap()[0:R, lc * 512:(lc + 1) * 512], in_=d1e[b][:, :]), inc="dgp", inck=16)
                EM("gpsimd", lambda e, lc=lc, b=b: e.dma_start(
                    out=ar_in.ap()[R:R + 32, lc * 512:(lc + 1) * 512], in_=bce[b][0:32, :]), inc="dgp", inck=16)
                d1_ack[lc] = gack()

            x_ready = p.c["aa"]
            W("gpsimd", "dgp", p.c["dgp"])
            EM("gpsimd", lambda e: e.collective_compute(
                "AllReduce", OP.add, replica_groups=[list(range(NCORE))],
                ins=[ar_in.ap().opt()], outs=[ar_out.ap().opt()]), inc="cc_ar")
            EM("gpsimd", lambda e: e.collective_compute(
                "AllGather", OP.bypass, replica_groups=[list(range(NCORE))],
                ins=[ag_wdn_in.ap().opt()], outs=[WDNT.ap().opt()]), inc="cc_wdn")
            EM("gpsimd", lambda e: e.collective_compute(
                "AllGather", OP.bypass, replica_groups=[list(range(NCORE))],
                ins=[ag_ow_in.ap().opt()], outs=[OWT.ap().opt()]), inc="cc_ow")
            W("gpsimd", "aa", x_ready)
            for et in range(NET):
                EM("gpsimd", lambda e, et=et: e.dma_start(
                    out=X_SP.ap()[et * 128:(et + 1) * 128, :], in_=x_bf[:, et * L:(et + 1) * L]),
                    inc="dgp", inck=16)
            p.barrier(dma_flush=[("gpsimd", "dgp"), ("sync", "dsy")])

        with contextlib.ExitStack() as ph:
            sb = lambda name, shape, dtype=F32: ph.enter_context(nc.sbuf_tensor(name, shape, dtype))
            psb = lambda name, shape: ph.enter_context(nc.psum_tensor(name, shape, F32))
            d1t_f = sb("d1t_f", [128, CH])
            d1t_bf = sb("d1t_bf", [128, CH], BF16)
            bmcs = sb("bmcs", [32, CH])
            bmc_bf = sb("bmc_bf", [32, 2 * CH], BF16)
            wd2_bf = sb("wd2_bf", [128, EC], BF16)
            delta = sb("delta", [128, NET * CH])
            xc_bf = [sb(f"xc_bf{i}", [128, NET * CH], BF16) for i in range(2)]
            u_bf = sb("u_bf", [128, NET * CH], BF16)
            y_acc = sb("y_acc", [128, NET * CH])
            sk2c = [sb(f"sk2c{i}", [128, CH]) for i in range(2)]
            p_buf = sb("p_buf", [128, 2 * CH])
            ug_buf = sb("ug_buf", [128, 2 * CH], BF16)
            h_buf = sb("h_buf", [128, 2 * CH])
            prod = [sb(f"prod{i}", [128, CH]) for i in range(3)]
            brep = sb("brep", [128, 2 * CH], BF16)
            crep = sb("crep", [128, 2 * CH], BF16)
            t0 = sb("t0", [128, CH])
            t1 = sb("t1", [128, CH])
            y_bf = [sb(f"y_bf{i}", [128, CH], BF16) for i in range(2)]
            sel_bf = sb("sel_bf", [32, 32 * 128], BF16)
            ps_dp = psb("ps_dp", [128, CH])
            ps_b = psb("ps_b", [128, 512])
            ps_c = psb("ps_c", [128, 512])
            xnt = sb("xnt", [128, 16 * 512], BF16)
            wg_g = [sb(f"wg_g{i}", [128, 16 * 128], BF16) for i in range(3)]
            wg_u = [sb(f"wg_u{i}", [128, 16 * 128], BF16) for i in range(3)]
            gb_st = sb("gb_st", [128, 4 * 512], BF16)
            h_big = sb("h_big", [128, 4 * 512], BF16)
            h_in = [sb(f"h_in{i}", [128, 2 * 512], BF16) for i in range(3)]
            wdn_t = [sb(f"wdn_t{i}", [128, 2 * 512], BF16) for i in range(3)]
            o2_t = [sb(f"o2_t{i}", [128, 512], BF16) for i in range(2)]
            pf = [psb(f"pf{i}", [128, 512]) for i in range(4)]

            EM("gpsimd", lambda e: e.dma_start(out=sel_bf[:, :], in_=sel_in), inc="dgp", inck=16)
            EM("gpsimd", lambda e: e.dma_start(out=d1t_f[:, 0:EC], in_=wd2T), inc="dgp", inck=16)
            a_w = gack()
            W("vector", "agp", a_w)
            wcv = EM("vector", lambda e: e.tensor_copy(wd2_bf[:, :], d1t_f[:, 0:EC]), inc="vv")
            W("sync", "cc_ar", 1)
            W("sync", "vv", wcv)
            W("tensor", "agp", a_w)
            W("tensor", "vv", wcv)
            xnt_ack = [None, None]

            dp_mm, dp_chain, delta_ok, pexp = {}, {}, {}, {}
            brep_ok = {}
            ug_done, tts_done, prod_done, add_done = {}, {}, {}, {}
            stt_done, ymul_done, ybf_ack, sk_ack = {}, {}, {}, {}
            d1_cast, xc_ack = {}, {}
            prod_war = {}
            gu_mm, silu_done, hmult_done, hst_ack = {}, {}, {}, {}
            dn_mm, o2_evac, o2_ack = {}, {}, {}
            dn_state = {"gpr": 0, "gprl": 0}

            fwcnt = {}

            def gate_loads(lh):
                for it in range(64):
                    if it == 0 and lh == 0:
                        EM("sync", lambda e, lh=lh: e.dma_start(
                            out=xnt[:, :].rearrange("p (d l) -> p d l", d=16),
                            in_=XNT_SP.ap()[:, lh * 512:(lh + 1) * 512]
                            .rearrange("(d p) l -> p d l", p=128)), inc="dsy", inck=16)
                        xnt_ack[lh] = sack()
                        W("scalar", "cc_wgu", 1)
                    b = (lh * 64 + it) % 3
                    prevk3 = (lh, it - 3) if it >= 3 else ((lh - 1, 61 + it) if lh >= 1 else None)
                    if prevk3 in gu_mm:
                        W("scalar", "mm", gu_mm[prevk3])
                    EM("scalar", lambda e, it=it, b=b: e.dma_start(
                        out=wg_g[b][:, :],
                        in_=WGUT.ap()[it:it + 1, :].rearrange("r (p c) -> (r p) c", p=128)),
                        inc=f"fw{b}", inck=16)
                    EM("scalar", lambda e, it=it, b=b: e.dma_start(
                        out=wg_u[b][:, :],
                        in_=WGUT.ap()[64 + it:65 + it, :].rearrange("r (p c) -> (r p) c", p=128)),
                        inc=f"fw{b}", inck=16)
                    fwcnt[("g", lh, it)] = p.c[f"fw{b}"]
                    yield

            def gate_comp(lh):
                for it in range(64):
                    if it == 0:
                        if lh == 1:
                            W("sync", "mm", gu_mm[(0, 63)])
                            EM("sync", lambda e, lh=lh: e.dma_start(
                                out=xnt[:, :].rearrange("p (d l) -> p d l", d=16),
                                in_=XNT_SP.ap()[:, lh * 512:(lh + 1) * 512]
                                .rearrange("(d p) l -> p d l", p=128)), inc="dsy", inck=16)
                            xnt_ack[lh] = sack()
                        W("tensor", "asy", xnt_ack[lh])
                    b = (lh * 64 + it) % 3
                    b2 = it % 2
                    pg, pu = pf[2 * b2], pf[2 * b2 + 1]
                    prevk = (lh, it - 2) if it >= 2 else ((lh - 1, 62 + it) if lh >= 1 else None)
                    W("tensor", f"fw{b}", fwcnt[("g", lh, it)])
                    if prevk in silu_done:
                        W("tensor", "aa", silu_done[prevk])
                        W("tensor", "vv", hmult_done[prevk])
                    for dt in range(16):
                        EM("tensor", lambda e, dt=dt, b=b, pg=pg: e.matmul(
                            pg[:, :], wg_g[b][:, dt * 128:(dt + 1) * 128],
                            xnt[:, dt * 512:(dt + 1) * 512], start=(dt == 0), stop=(dt == 15)))
                    for dt in range(16):
                        gu_mm[(lh, it)] = EM("tensor", lambda e, dt=dt, b=b, pu=pu: e.matmul(
                            pu[:, :], wg_u[b][:, dt * 128:(dt + 1) * 128],
                            xnt[:, dt * 512:(dt + 1) * 512], start=(dt == 0), stop=(dt == 15)),
                            inc="mm" if dt == 15 else None)
                    W("scalar", "mm", gu_mm[(lh, it)])
                    j = it % 4
                    silu_done[(lh, it)] = EM("scalar", lambda e, j=j, pg=pg: e.activation(
                        gb_st[:, j * 512:(j + 1) * 512], pg[:, :], AF.Identity), inc="aa")
                    W("vector", "mm", gu_mm[(lh, it)])
                    if j == 0:
                        bk = (lh, it // 4 - 1) if it >= 4 else ((lh - 1, 15) if lh >= 1 else None)
                        if bk in hst_ack:
                            W("vector", "asy", hst_ack[bk])
                    hmult_done[(lh, it)] = EM("vector", lambda e, j=j, pu=pu: e.tensor_copy(
                        h_big[:, j * 512:(j + 1) * 512], pu[:, :]), inc="vv")
                    if j == 3:
                        sb_ok = EM("scalar", lambda e: e.activation(
                            gb_st[:, :], gb_st[:, :], AF.Silu), inc="aa")
                        W("vector", "aa", sb_ok)
                        hb = EM("vector", lambda e: e.tensor_mul(
                            h_big[:, :], h_big[:, :], gb_st[:, :]), inc="vv")
                        W("sync", "vv", hb)
                        it0 = it - 3
                        EM("sync", lambda e, it0=it0, lh=lh: e.dma_start(
                            out=H_SP.ap()[it0 * 128:(it0 + 4) * 128, lh * 512:(lh + 1) * 512]
                            .rearrange("(i p) l -> p i l", p=128),
                            in_=h_big[:, :].rearrange("p (i l) -> p i l", i=4)), inc="dsy", inck=16)
                        hst_ack[(lh, it // 4)] = sack()
                    yield

            def down_loads():
                for lh in range(2):
                    for dc in range(4):
                        for pr in range(32):
                            it = 2 * pr
                            gpr = dn_state["gprl"]
                            dn_state["gprl"] += 1
                            b = gpr % 3
                            if (gpr - 3) in dn_mm:
                                W("scalar", "mm", dn_mm[gpr - 3])
                            if dc == 0 and pr == 0:
                                W("scalar", "cc_wdn", 1)
                                W("scalar", "asy", hst_ack[(lh, 15)])
                            EM("scalar", lambda e, it=it, dc=dc, b=b: e.dma_start(
                                out=wdn_t[b][:, :].rearrange("p (i c) -> p i c", i=2),
                                in_=WDNT.ap()[it * 128:(it + 2) * 128, dc * 512:(dc + 1) * 512]
                                .rearrange("(i p) c -> p i c", p=128)), inc=f"fw{b}", inck=16)
                            EM("scalar", lambda e, it=it, lh=lh, b=b: e.dma_start(
                                out=h_in[b][:, :].rearrange("p (i c) -> p i c", i=2),
                                in_=H_SP.ap()[it * 128:(it + 2) * 128, lh * 512:(lh + 1) * 512]
                                .rearrange("(i p) c -> p i c", p=128)), inc=f"fw{b}", inck=16)
                            fwcnt[("d", gpr)] = p.c[f"fw{b}"]
                            yield
                        yield

            def down_comp():
                for lh in range(2):
                    for dc in range(4):
                        for pr in range(32):
                            it = 2 * pr
                            gpr = dn_state["gpr"]
                            dn_state["gpr"] += 1
                            b = gpr % 3
                            W("tensor", f"fw{b}", fwcnt[("d", gpr)])
                            if it == 0:
                                if lh == 0 and dc == 0:
                                    W("tensor", "vv", hmult_done[(1, 63)])
                                    W("tensor", "aa", silu_done[(1, 63)])
                                else:
                                    W("tensor", "aa", o2_evac[3])
                            for ii in range(2):
                                for lt2 in range(4):
                                    dn_mm[gpr] = EM(
                                        "tensor", lambda e, ii=ii, lt2=lt2, b=b, it=it: e.matmul(
                                            pf[lt2][:, :],
                                            h_in[b][:, ii * 512 + lt2 * 128:ii * 512 + (lt2 + 1) * 128],
                                            wdn_t[b][:, ii * 512:(ii + 1) * 512],
                                            start=(it + ii == 0), stop=(it + ii == 63)),
                                        inc="mm" if (ii == 1 and lt2 == 3) else None)
                            yield
                        W("scalar", "mm", dn_mm[dn_state["gpr"] - 1])
                        for lt2 in range(4):
                            ob = lt2 % 2
                            okey = (lt2 - 2) if lt2 >= 2 else (lt2 + 2)
                            if okey in o2_ack:
                                W("scalar", "asy", o2_ack[okey])
                            o2_evac[lt2] = EM("scalar", lambda e, lt2=lt2, ob=ob: e.activation(
                                o2_t[ob][:, :], pf[lt2][:, :], AF.Identity), inc="aa")
                            W("sync", "aa", o2_evac[lt2])
                            EM("sync", lambda e, lh=lh, dc=dc, lt2=lt2, ob=ob: e.dma_start(
                                out=OUT2_SP.ap()[lh * 512 + lt2 * 128:lh * 512 + (lt2 + 1) * 128,
                                                 dc * 512:(dc + 1) * 512], in_=o2_t[ob][:, :]),
                                inc="dsy", inck=16)
                            o2_ack[lt2] = sack()
                        yield

            import itertools as _it
            load_iter = _it.chain(gate_loads(0), gate_loads(1), down_loads())
            comp_iter = _it.chain(gate_comp(0), gate_comp(1), down_comp())
            NUNITS = 64 + 64 + 2 * 4 * 33
            state_lc = [0, 0]

            def pull():
                while state_lc[0] < min(state_lc[1] + 2, NUNITS):
                    next(load_iter)
                    state_lc[0] += 1
                if state_lc[1] < NUNITS:
                    next(comp_iter)
                    state_lc[1] += 1

            bmc_cp, bc_mm_last = {}, {}
            a_bmc_d, a_d1_d, a_xc_d = {}, {}, {}

            def chunk_loads(cnk):
                c0 = cnk * CH
                if cnk >= 1:
                    W("sync", "vv", bmc_cp[cnk - 1])
                EM("sync", lambda e, c0=c0: e.dma_start(
                    out=bmcs[:, :], in_=ar_out.ap()[R:R + 32, c0:c0 + CH]), inc="dsy", inck=16)
                a_bmc_d[cnk] = sack()
                if cnk >= 1:
                    W("sync", "vv", d1_cast[cnk - 1])
                EM("sync", lambda e, c0=c0: e.dma_start(
                    out=d1t_f[:, :], in_=ar_out.ap()[0:R, c0:c0 + CH]), inc="dsy", inck=16)
                a_d1_d[cnk] = sack()
                if cnk >= 2:
                    W("sync", "vv", stt_done[(cnk - 2, NET - 1)])
                EM("sync", lambda e, c0=c0, xb=cnk % 2: e.dma_start(
                    out=xc_bf[xb][:, :].rearrange("p (et l) -> p et l", et=NET),
                    in_=X_SP.ap()[:, c0:c0 + CH].rearrange("(et p) l -> p et l", p=128)),
                    inc="dsy", inck=16)
                a_xc_d[cnk] = sack()

            chunk_loads(0)
            giter = [0]
            for cnk in range(NCH):
                c0 = cnk * CH
                cb2 = cnk % 2
                xb = cnk % 2
                W("vector", "asy", a_bmc_d[cnk])
                if cnk >= 2:
                    W("vector", "mm", bc_mm_last[cnk - 2])
                bmc_cp[cnk] = EM("vector", lambda e, cb2=cb2: e.tensor_copy(
                    bmc_bf[:, cb2 * CH:(cb2 + 1) * CH], bmcs[:, :]), inc="vv")
                W("vector", "asy", a_d1_d[cnk])
                if cnk >= 1:
                    W("vector", "mm", dp_mm[(cnk - 1, NET - 1)])
                d1_cast[cnk] = EM("vector", lambda e: e.tensor_copy(d1t_bf[:, :], d1t_f[:, :]), inc="vv")
                W("tensor", "vv", d1_cast[cnk])
                for et in range(NET):
                    pk = (cnk, et - 1) if et >= 1 else (cnk - 1, NET - 1)
                    if pk in dp_chain:
                        W("tensor", "aa", dp_chain[pk])
                    for hh in range(2):
                        dp_mm[(cnk, et)] = EM("tensor", lambda e, et=et, hh=hh: e.matmul(
                            ps_dp[:, hh * 512:(hh + 1) * 512], wd2_bf[:, et * 128:(et + 1) * 128],
                            d1t_bf[:, hh * 512:(hh + 1) * 512], start=True, stop=True),
                            inc="mm" if hh == 1 else None)
                    W("scalar", "mm", dp_mm[(cnk, et)])
                    if pk in delta_ok:
                        W("scalar", "vv", delta_ok[pk])
                    EM("scalar", lambda e, et=et: e.activation(t0[:, :], ps_dp[:, :], AF.Relu, bias=ecol(et, 5)))
                    EM("scalar", lambda e, et=et: e.activation(t1[:, :], ps_dp[:, :], AF.Abs, bias=ecol(et, 5)))
                    EM("scalar", lambda e: e.activation(t1[:, :], t1[:, :], AF.Exp, scale=-1.0))
                    dp_chain[(cnk, et)] = EM("scalar", lambda e: e.activation(t1[:, :], t1[:, :], AF.Ln, bias=1.0),
                                             inc="aa")
                    W("vector", "aa", dp_chain[(cnk, et)])
                    if cnk >= 1:
                        W("vector", "aa", pexp[(cnk - 1, N - 1, et)])
                    delta_ok[(cnk, et)] = EM("vector", lambda e, et=et: e.tensor_add(
                        delta[:, et * CH:(et + 1) * CH], t0[:, :], t1[:, :]), inc="vv")
                W("vector", "asy", a_xc_d[cnk])
                for et in range(NET):
                    EM("vector", lambda e, et=et, xb=xb: e.tensor_mul(
                        u_bf[:, et * CH:(et + 1) * CH], delta[:, et * CH:(et + 1) * CH],
                        xc_bf[xb][:, et * CH:(et + 1) * CH]), inc="vv")

                pending_adds = []

                def flush_add():
                    s2, et2, pb2 = pending_adds.pop(0)
                    W("vector", "gg", prod_done[(cnk, s2)])
                    add_done[(cnk, s2)] = EM("vector", lambda e, et2=et2, pb2=pb2: e.tensor_add(
                        y_acc[:, et2 * CH:(et2 + 1) * CH], y_acc[:, et2 * CH:(et2 + 1) * CH],
                        prod[pb2][:, :]), inc="vv")
                    prod_war[pb2] = add_done[(cnk, s2)]

                for n in range(N):
                    nb = n % 2
                    gk = (cnk, n - 2) if n >= 2 else ((cnk - 1, N - 2 + n) if cnk >= 1 else None)
                    if gk is not None and gk in brep_ok:
                        s_last = (gk[1]) * NET + NET - 1
                        W("tensor", "vv", ug_done[(gk[0], s_last)])
                        W("tensor", "gg", prod_done[(gk[0], s_last)])
                    pk = (cnk, n - 1) if n >= 1 else ((cnk - 1, N - 1) if cnk >= 1 else None)
                    if pk in brep_ok:
                        W("tensor", "aa", brep_ok[pk][1])
                    if n == 0:
                        W("tensor", "vv", bmc_cp[cnk])
                    m1 = EM("tensor", lambda e, n=n, cb2=cb2: e.matmul(
                        ps_b[:, :], sel_bf[0:32, n * 128:(n + 1) * 128],
                        bmc_bf[0:32, cb2 * CH:cb2 * CH + 512], start=True, stop=True), inc="mm")
                    m2 = EM("tensor", lambda e, n=n, cb2=cb2: e.matmul(
                        ps_c[:, :], sel_bf[0:32, n * 128:(n + 1) * 128],
                        bmc_bf[0:32, cb2 * CH + 512:(cb2 + 1) * CH], start=True, stop=True), inc="mm")
                    W("scalar", "mm", m2)
                    if gk is not None and gk in brep_ok:
                        s_last = (gk[1]) * NET + NET - 1
                        W("scalar", "vv", ug_done[(gk[0], s_last)])
                        W("scalar", "gg", prod_done[(gk[0], s_last)])
                    eb0 = EM("scalar", lambda e, nb=nb: e.activation(
                        brep[:, nb * CH:nb * CH + 512], ps_b[:, :], AF.Identity), inc="aa")
                    eb1 = EM("scalar", lambda e, nb=nb: e.activation(
                        brep[:, nb * CH + 512:(nb + 1) * CH], ps_c[:, :], AF.Identity), inc="aa")
                    W("tensor", "aa", eb0)
                    m3 = EM("tensor", lambda e, n=n, cb2=cb2: e.matmul(
                        ps_b[:, :], sel_bf[0:32, (16 + n) * 128:(17 + n) * 128],
                        bmc_bf[0:32, cb2 * CH:cb2 * CH + 512], start=True, stop=True), inc="mm")
                    W("tensor", "aa", eb1)
                    m4 = EM("tensor", lambda e, n=n, cb2=cb2: e.matmul(
                        ps_c[:, :], sel_bf[0:32, (16 + n) * 128:(17 + n) * 128],
                        bmc_bf[0:32, cb2 * CH + 512:(cb2 + 1) * CH], start=True, stop=True), inc="mm")
                    bc_mm_last[cnk] = m4
                    W("scalar", "mm", m4)
                    EM("scalar", lambda e, nb=nb: e.activation(
                        crep[:, nb * CH:nb * CH + 512], ps_b[:, :], AF.Identity), inc="aa")
                    ec1 = EM("scalar", lambda e, nb=nb: e.activation(
                        crep[:, nb * CH + 512:(nb + 1) * CH], ps_c[:, :], AF.Identity), inc="aa")
                    brep_ok[(cnk, n)] = (eb1, ec1)

                    for et in range(NET):
                        s = n * NET + et
                        sb_ = s % 2
                        pb = s % 3
                        cidx = et * N + n
                        if n == 0 and et == 0:
                            W("scalar", "vv", delta_ok[(cnk, NET - 1)])
                        tk = (cnk, s - 2) if s >= 2 else ((cnk - 1, N * NET - 2 + s) if cnk >= 1 else None)
                        if tk in tts_done:
                            W("scalar", "vv", tts_done[tk])
                        pexp[(cnk, n, et)] = EM("scalar", lambda e, et=et, n=n, sb_=sb_: e.activation(
                            p_buf[:, sb_ * CH:(sb_ + 1) * CH], delta[:, et * CH:(et + 1) * CH], AF.Exp,
                            scale=asb[:, et * 16 + n:et * 16 + n + 1]), inc="aa")
                        W("vector", "aa", brep_ok[(cnk, n)][0])
                        ug_done[(cnk, s)] = EM("vector", lambda e, et=et, nb=nb, sb_=sb_: e.tensor_mul(
                            ug_buf[:, sb_ * CH:(sb_ + 1) * CH], u_bf[:, et * CH:(et + 1) * CH],
                            brep[:, nb * CH:(nb + 1) * CH]), inc="vv")
                        W("vector", "aa", pexp[(cnk, n, et)])
                        if tk in prod_done:
                            W("vector", "gg", prod_done[tk])
                        if cnk == 0:
                            tts_done[(cnk, s)] = EM("vector", lambda e, sb_=sb_: e.tensor_tensor_scan(
                                h_buf[:, sb_ * CH:(sb_ + 1) * CH], p_buf[:, sb_ * CH:(sb_ + 1) * CH],
                                ug_buf[:, sb_ * CH:(sb_ + 1) * CH], 0.0, OP.mult, OP.add), inc="vv")
                        else:
                            tts_done[(cnk, s)] = EM("vector", lambda e, sb_=sb_, cidx=cidx: e.tensor_tensor_scan(
                                h_buf[:, sb_ * CH:(sb_ + 1) * CH], p_buf[:, sb_ * CH:(sb_ + 1) * CH],
                                ug_buf[:, sb_ * CH:(sb_ + 1) * CH], carry[:, cidx:cidx + 1],
                                OP.mult, OP.add), inc="vv")
                        EM("vector", lambda e, sb_=sb_, cidx=cidx: e.tensor_copy(
                            carry[:, cidx:cidx + 1], h_buf[:, sb_ * CH + CH - 1:sb_ * CH + CH]))
                        W("gpsimd", "vv", tts_done[(cnk, s)])
                        W("gpsimd", "aa", brep_ok[(cnk, n)][1])
                        if n == 0:
                            prod_done[(cnk, s)] = EM("gpsimd", lambda e, et=et, nb=nb, sb_=sb_: e.tensor_mul(
                                y_acc[:, et * CH:(et + 1) * CH], h_buf[:, sb_ * CH:(sb_ + 1) * CH],
                                crep[:, nb * CH:(nb + 1) * CH]), inc="gg")
                        else:
                            if pb in prod_war:
                                W("gpsimd", "vv", prod_war[pb])
                            prod_done[(cnk, s)] = EM("gpsimd", lambda e, nb=nb, sb_=sb_, pb=pb: e.tensor_mul(
                                prod[pb][:, :], h_buf[:, sb_ * CH:(sb_ + 1) * CH],
                                crep[:, nb * CH:(nb + 1) * CH]), inc="gg")
                            pending_adds.append((s, et, pb))
                        if len(pending_adds) > 2:
                            flush_add()
                        if giter[0] >= 8:
                            pull()
                        if giter[0] % 64 == 48:
                            if cnk + 1 < NCH:
                                chunk_loads(cnk + 1)
                        giter[0] += 1
                while pending_adds:
                    flush_add()

                for et in range(NET):
                    yb = et % 2
                    skkey = (cnk, et - 2) if et >= 2 else ((cnk - 1, et + 2) if cnk >= 1 else None)
                    if skkey in ymul_done:
                        W("sync", "gg", ymul_done[skkey])
                    EM("sync", lambda e, et=et, c0=c0, yb=yb: e.dma_start(
                        out=sk2c[yb][:, :], in_=SK2_SP.ap()[et * 128:(et + 1) * 128, c0:c0 + CH]),
                        inc="dsy", inck=16)
                    sk_ack[(cnk, et)] = sack()
                    stt_done[(cnk, et)] = EM("vector", lambda e, et=et, xb=xb: e.scalar_tensor_tensor(
                        y_acc[:, et * CH:(et + 1) * CH], xc_bf[xb][:, et * CH:(et + 1) * CH], ecol(et, 6),
                        y_acc[:, et * CH:(et + 1) * CH], OP.mult, OP.add), inc="vv")
                    W("gpsimd", "vv", stt_done[(cnk, et)])
                    W("gpsimd", "asy", sk_ack[(cnk, et)])
                    if skkey in ybf_ack:
                        W("gpsimd", "asy", ybf_ack[skkey])
                    ymul_done[(cnk, et)] = EM("gpsimd", lambda e, et=et, yb=yb: e.tensor_mul(
                        y_bf[yb][:, :], y_acc[:, et * CH:(et + 1) * CH], sk2c[yb][:, :]), inc="gg")
                    W("sync", "gg", ymul_done[(cnk, et)])
                    EM("sync", lambda e, et=et, c0=c0, yb=yb: e.dma_start(
                        out=ag_y_in.ap()[et * 128:(et + 1) * 128, c0:c0 + CH], in_=y_bf[yb][:, :]),
                        inc="dsy", inck=16)
                    ybf_ack[(cnk, et)] = sack()

            while state_lc[1] < NUNITS:
                pull()

            W("gpsimd", "asy", ybf_ack[(NCH - 1, NET - 1)])
            EM("gpsimd", lambda e: e.collective_compute(
                "AllGather", OP.bypass, replica_groups=[list(range(NCORE))],
                ins=[ag_y_in.ap().opt()], outs=[Y_ALL.ap().opt()]), inc="cc_y")
            p.barrier(dma_flush=[("gpsimd", "dgp"), ("sync", "dsy"), ("scalar", "dsc")])

        with contextlib.ExitStack() as ph:
            sb = lambda name, shape, dtype=F32: ph.enter_context(nc.sbuf_tensor(name, shape, dtype))
            psb = lambda name, shape: ph.enter_context(nc.psum_tensor(name, shape, F32))
            y_sb = sb("y_sb", [128, 32 * LC], BF16)
            owt_t = [sb(f"owt_t{i}", [128, 2 * 512], BF16) for i in range(3)]
            xn4 = [sb(f"xn4_{i}", [128, 4 * 512]) for i in range(2)]
            o24 = [sb(f"o24_{i}", [128, 4 * 512], BF16) for i in range(2)]
            rs4 = [sb(f"rs4_{i}", [128, 4 * 512]) for i in range(2)]
            attn4 = [sb(f"attn4_{i}", [128, 4 * 512]) for i in range(2)]
            g4 = [sb(f"g4_{i}", [128, 4 * 512]) for i in range(2)]
            out4 = [sb(f"out4_{i}", [128, 4 * 512]) for i in range(2)]
            ps_o = [psb(f"ps_o{i}", [128, 512]) for i in range(8)]

            W("sync", "cc_y", 1)
            W("scalar", "cc_ow", 1)

            def y_load(e):
                for arm in e.switch_core_id(n=128):
                    lcore = physical_to_logical_core_id(int(arm) % 128, 0, container_num_cores=NCORE) % NCORE
                    c0 = lcore * LC
                    e.dma_start(
                        out=y_sb[:, :].rearrange("p (et l) -> p et l", et=32),
                        in_=Y_ALL.ap()[:, c0:c0 + LC].rearrange("(et p) l -> p et l", p=128),
                    ).then_inc(p.sems["dsy"], 16)
            p.c["dsy"] += 16
            p.prog["sync"].append(y_load)
            yl_ack = sack()

            owp_mm, comb_done, ld_cnt, od_ack = {}, {}, {}, {}
            aev = {}
            g8c = [0]
            for dc in range(4):
                for q in range(2):
                    if (dc - 1, q) in comb_done:
                        W("sync", "vv", comb_done[(dc - 1, q)])
                    for arr, src_ap in ((xn4, XN_SP.ap()), (o24, OUT2_SP.ap()), (rs4, resid)):
                        EM("sync", lambda e, arr=arr, src_ap=src_ap, q=q, dc=dc: e.dma_start(
                            out=arr[q][:, :].rearrange("p (i c) -> p i c", i=4),
                            in_=src_ap[q * 512:(q + 1) * 512, dc * 512:(dc + 1) * 512]
                            .rearrange("(i p) c -> p i c", p=128)), inc=f"sw{q}", inck=16)
                    ld_cnt[(dc, q)] = p.c[f"sw{q}"]
                for pr8 in range(16):
                    g8 = g8c[0]
                    g8c[0] += 1
                    b = g8 % 3
                    et8 = 2 * pr8
                    if (g8 - 3) in owp_mm:
                        W("scalar", "mm", owp_mm[g8 - 3])
                    EM("scalar", lambda e, et8=et8, dc=dc, b=b: e.dma_start(
                        out=owt_t[b][:, :].rearrange("p (i c) -> p i c", i=2),
                        in_=OWT.ap()[et8 * 128:(et8 + 2) * 128, dc * 512:(dc + 1) * 512]
                        .rearrange("(i p) c -> p i c", p=128)), inc=f"fw{b}", inck=16)
                    W("tensor", f"fw{b}", p.c[f"fw{b}"])
                    if pr8 == 0:
                        W("tensor", "asy", yl_ack)
                        if dc > 0:
                            W("tensor", "aa", aev[(dc - 1, 7)])
                    for ii in range(2):
                        for lt in range(8):
                            owp_mm[g8] = EM("tensor", lambda e, et8=et8, ii=ii, b=b, lt=lt: e.matmul(
                                ps_o[lt][:, :],
                                y_sb[:, (et8 + ii) * LC + lt * 128:(et8 + ii) * LC + (lt + 1) * 128],
                                owt_t[b][:, ii * 512:(ii + 1) * 512],
                                start=(et8 + ii == 0), stop=(et8 + ii == 31)),
                                inc="mm" if (ii == 1 and lt == 7) else None)
                W("scalar", "mm", owp_mm[g8c[0] - 1])
                for lt in range(8):
                    q = lt // 4
                    if lt % 4 == 0 and (dc - 1, q) in comb_done:
                        W("scalar", "vv", comb_done[(dc - 1, q)])
                    aev[(dc, lt)] = EM("scalar", lambda e, lt=lt, q=q: e.activation(
                        attn4[q][:, (lt % 4) * 512:(lt % 4 + 1) * 512], ps_o[lt][:, :], AF.Identity),
                        inc="aa")
                    if lt % 4 == 3:
                        W("vector", "aa", aev[(dc, lt)])
                        W("vector", f"sw{q}", ld_cnt[(dc, q)])
                        EM("vector", lambda e, q=q: e.tensor_add(
                            attn4[q][:, :], attn4[q][:, :], xn4[q][:, :]), inc="vv")
                        sgq = EM("scalar", lambda e, q=q: e.activation(
                            g4[q][:, :], o24[q][:, :], AF.Silu), inc="aa")
                        W("vector", "aa", sgq)
                        EM("vector", lambda e, q=q: e.tensor_mul(attn4[q][:, :], attn4[q][:, :], g4[q][:, :]),
                           inc="vv")
                        if (dc - 1, q) in od_ack:
                            W("vector", "asy", od_ack[(dc - 1, q)])
                        cd = EM("vector", lambda e, q=q: e.tensor_add(
                            out4[q][:, :], rs4[q][:, :], attn4[q][:, :]), inc="vv")
                        comb_done[(dc, q)] = cd
                        W("sync", "vv", cd)
                        EM("sync", lambda e, q=q, dc=dc: e.dma_start(
                            out=out[q * 512:(q + 1) * 512, dc * 512:(dc + 1) * 512]
                            .rearrange("(i p) c -> p i c", p=128),
                            in_=out4[q][:, :].rearrange("p (i c) -> p i c", i=4)), inc="dsy", inck=16)
                        od_ack[(dc, q)] = sack()

            if debug:
                for src, dst in [(XN_SP, dbg["dbg_xn"]), (ag_rn_in[0], dbg["dbg_rn"]),
                                 (ar_out, dbg["dbg_d1bc"]), (SK2_SP, dbg["dbg_sk2"]),
                                 (ag_y_in, dbg["dbg_y"]), (OUT2_SP, dbg["dbg_out2"]),
                                 (X_SP, dbg["dbg_x"])]:
                    EM("gpsimd", lambda e, src=src, dst=dst: e.dma_start(out=dst, in_=src.ap()),
                       inc="dgp", inck=16)
            W("gpsimd", "dgp", p.c["dgp"])
            p.barrier(dma_flush=[("gpsimd", "dgp"), ("sync", "dsy")])

        return _finish(nc, p)


def _finish(nc, p):
    with nc.Block() as block:
        @block.sync
        def _(e):
            for fn in p.prog["sync"]:
                fn(e)

        @block.gpsimd
        def _(e):
            for fn in p.prog["gpsimd"]:
                fn(e)

        @block.vector
        def _(e):
            for fn in p.prog["vector"]:
                fn(e)

        @block.scalar
        def _(e):
            for fn in p.prog["scalar"]:
                fn(e)

        @block.tensor
        def _(e):
            for fn in p.prog["tensor"]:
                fn(e)
    return nc


def make_in_maps(inputs):
    x0 = np.asarray(inputs["L0_attn_resid"]).reshape(L, D)
    norm_w = np.asarray(inputs["L0_attn_norm_w"])
    skip_w = np.asarray(inputs["L0_attn_skip_w"])
    in_w = np.asarray(inputs["L0_attn_in_w"])
    conv_w = np.asarray(inputs["L0_attn_conv_w"])
    conv_b = np.asarray(inputs["L0_attn_conv_b"])
    wd1 = np.asarray(inputs["L0_attn_wd1"])
    wd2 = np.asarray(inputs["L0_attn_wd2"])
    wd2_b = np.asarray(inputs["L0_attn_wd2_b"])
    wB = np.asarray(inputs["L0_attn_wB"])
    wC = np.asarray(inputs["L0_attn_wC"])
    A_log = np.asarray(inputs["L0_attn_A_log"])
    W_D = np.asarray(inputs["L0_attn_W_D"])
    out_w = np.asarray(inputs["L0_attn_out_w"])
    wgu = np.asarray(inputs["L0_ffn_w_gate_up"])
    wdn = np.asarray(inputs["L0_ffn_w_down"])

    nw_rep = np.ascontiguousarray(np.broadcast_to(norm_w[None, :], (128, D)).astype(np.float32))
    wguT = np.ascontiguousarray(wgu.T)
    gtile = wguT[:, :I].reshape(16, 128, 64, 128).transpose(2, 1, 0, 3).reshape(64, 262144)
    utile = wguT[:, I:].reshape(16, 128, 64, 128).transpose(2, 1, 0, 3).reshape(64, 262144)
    wgut_tiled = np.ascontiguousarray(np.concatenate([gtile, utile], 0))
    wdnT = np.ascontiguousarray(wdn.T)
    owT = np.ascontiguousarray(out_w.T)
    sel = np.zeros((32, 32 * 128), np.float32)
    for k in range(32):
        sel[k, k * 128:(k + 1) * 128] = 1.0
    sel_bf = sel.astype(ml_dtypes.bfloat16)

    in_maps = []
    for i in range(NCORE):
        ls = slice(i * LC, (i + 1) * LC)
        es = slice(i * EC, (i + 1) * EC)
        ev = np.zeros((EC, 24), np.float32)
        ev[:, 0:4] = conv_w[es]
        ev[:, 4] = conv_b[es]
        ev[:, 5] = wd2_b[es]
        ev[:, 6] = W_D[es]
        ev[:, 8:24] = A_log[es]
        in_maps.append({
            "resid": np.ascontiguousarray(x0[ls]),
            "nw_rep": nw_rep,
            "inwT": np.ascontiguousarray(in_w[es].T),
            "skwT": np.ascontiguousarray(skip_w[es].T),
            "wd1T": np.ascontiguousarray(wd1[:, es].T),
            "wd2T": np.ascontiguousarray(wd2[es].T),
            "wbcT": np.ascontiguousarray(np.concatenate([wB[:, es].T, wC[:, es].T], axis=1)),
            "evec": ev,
            "sel_in": sel_bf,
            "wguT_sl": np.ascontiguousarray(
                wgut_tiled[i * 16:(i + 1) * 16].reshape(128, 32768)),
            "wdnT_sl": np.ascontiguousarray(wdnT[i * (I // NCORE):(i + 1) * (I // NCORE)]),
            "owT_sl": np.ascontiguousarray(owT[i * (E // NCORE):(i + 1) * (E // NCORE)]),
        })
    return in_maps


def kernel(**inputs):
    in_maps = make_in_maps(inputs)
    nc = build_nc()
    res = run_bass_kernel_spmd(nc, in_maps, core_ids=list(range(NCORE)))
    outp = np.concatenate([res.results[i]["out"] for i in range(NCORE)], axis=0)
    return outp.reshape(1, L, D).astype(np.float32)


# version 14 (final)
# speedup vs baseline: 1.4417x; 1.0233x over previous
import contextlib
import numpy as np
import ml_dtypes
import concourse.bass as bass
import concourse.mybir as mybir
from concourse.bass_utils import run_bass_kernel_spmd
from concourse.libnrt import physical_to_logical_core_id

F32 = mybir.dt.float32
BF16 = mybir.dt.bfloat16
AF = mybir.ActivationFunctionType
OP = mybir.AluOpType

NCORE = 8
L = 8192
D = 2048
E = 4096
N = 16
R = 128
I = 8192
LC = L // NCORE
EC = E // NCORE
NET = EC // 128
CH = 1024
NCH = L // CH
EPS = 1e-5

ENGINES = ("sync", "gpsimd", "vector", "scalar", "tensor")


class P:

    def __init__(self, nc, stack, semnames):
        self.nc = nc
        self.prog = {e: [] for e in ENGINES}
        self.sems = {s: stack.enter_context(nc.semaphore(s)) for s in semnames}
        self.c = {s: 0 for s in semnames}
        self._last = {}
        self.nbar = 0

    def wait(self, eng, sem, val):
        if val is None or val <= 0:
            return
        self.prog[eng].append(lambda e, s=sem, v=val: e.wait_ge(self.sems[s], v))

    CHAIN = {"vector": "vv", "scalar": "aa", "gpsimd": "gg"}

    def emit(self, eng, builder, inc=None, inck=1):
        chain_sem = self.CHAIN.get(eng)
        dma_like = (inc in ("dgp", "dsy", "dsc") or (inc or "").startswith("cc")
                    or (inc or "").startswith("fw") or (inc or "").startswith("sw"))
        if chain_sem is not None and not dma_like:
            self.wait(eng, chain_sem, self._last.get(eng, 0))
            if inc is None:
                inc, inck = chain_sem, 1
            assert inc == chain_sem, (eng, inc)
        if inc is not None:
            self.c[inc] += inck
            v = self.c[inc]

            def fn(e, b=builder, s=inc, k=inck):
                b(e).then_inc(self.sems[s], k)
            self.prog[eng].append(fn)
            if chain_sem is not None and not dma_like:
                self._last[eng] = v
            return v
        self.prog[eng].append(lambda e, b=builder: b(e))
        return None

    def ack(self, eng, qsem, asem):
        self.wait(eng, qsem, self.c[qsem])
        self.c[asem] += 1
        v = self.c[asem]
        self.prog[eng].append(lambda e, s=asem: e.sem_inc(self.sems[s], 1))
        return v

    def barrier(self, dma_flush=()):
        self.nbar += 1
        target = 5 * self.nbar
        flush = {}
        for eng, s in dma_flush:
            flush.setdefault(eng, []).append((s, self.c[s]))
        for eng in ENGINES:
            for s, v in flush.get(eng, []):
                self.wait(eng, s, v)
            self.prog[eng].append(lambda e: e.sem_inc(self.sems["bar"], 1))
            self.wait(eng, "bar", target)


def build_nc(debug=False, preamble="clear_sled"):
    nc = bass.Bass("TRN2", target_bir_lowering=False)

    def din(name, shape, dtype=F32):
        return nc.dram_tensor(name, shape, dtype, kind="ExternalInput").ap()

    resid = din("resid", [LC, D])
    nw_rep = din("nw_rep", [128, D])
    inwT = din("inwT", [D, EC])
    skwT = din("skwT", [D, EC])
    wd1T = din("wd1T", [EC, R])
    wd2T = din("wd2T", [R, EC])
    wbcT = din("wbcT", [EC, 32])
    evec = din("evec", [EC, 24])
    sel_in = din("sel_in", [32, 32 * 128], BF16)
    wguT_sl = din("wguT_sl", [128, 32768])
    wdnT_sl = din("wdnT_sl", [I // NCORE, D])
    owT_sl = din("owT_sl", [E // NCORE, D])
    out = nc.dram_tensor("out", [LC, D], F32, kind="ExternalOutput").ap()

    ag_rn_in = [nc.dram_tensor(f"ag_rn_in{i}", [D, 512], BF16) for i in range(2)]
    RNT_H = [nc.dram_tensor(f"RNT_H{i}", [NCORE * D, 512], BF16, addr_space="Shared") for i in range(2)]
    XNT_SP = nc.dram_tensor("XNT_SP", [D, LC], BF16)
    ag_wgu_in = nc.dram_tensor("ag_wgu_in", [128, 32768], BF16)
    WGUT = nc.dram_tensor("WGUT", [128, 262144], BF16, addr_space="Shared")
    ag_wdn_in = nc.dram_tensor("ag_wdn_in", [I // NCORE, D], BF16)
    WDNT = nc.dram_tensor("WDNT", [I, D], BF16, addr_space="Shared")
    ag_ow_in = nc.dram_tensor("ag_ow_in", [E // NCORE, D], BF16)
    OWT = nc.dram_tensor("OWT", [E, D], BF16, addr_space="Shared")
    ar_in = nc.dram_tensor("ar_in", [R + 32, L], F32)
    ar_out = nc.dram_tensor("ar_out", [R + 32, L], F32, addr_space="Shared")
    ag_y_h = [nc.dram_tensor(f"ag_y_h{i}", [EC, L // 2], BF16) for i in range(2)]
    Y_H = [nc.dram_tensor(f"Y_H{i}", [E, L // 2], BF16, addr_space="Shared") for i in range(2)]
    XN_SP = nc.dram_tensor("XN_SP", [LC, D], F32)
    SK2_SP = nc.dram_tensor("SK2_SP", [EC, L], F32)
    OUT2_SP = nc.dram_tensor("OUT2_SP", [LC, D], BF16)
    X_SP = nc.dram_tensor("X_SP", [EC, L], BF16)
    H_SP = nc.dram_tensor("H_SP", [I, LC], BF16)

    dbg = {}
    if debug:
        for nm, shape, dt in [
            ("dbg_xn", [LC, D], F32), ("dbg_rn", [LC, D], BF16),
            ("dbg_d1bc", [R + 32, L], F32), ("dbg_x", [EC, L], BF16),
            ("dbg_sk2", [EC, L], F32), ("dbg_delta", [EC, L], F32),
            ("dbg_y", [EC, L], BF16), ("dbg_out2", [LC, D], F32),
            ("dbg_scan", [128, 6 * CH], F32),
        ]:
            dbg[nm] = nc.dram_tensor(nm, shape, dt, kind="ExternalOutput").ap()

    semnames = ["bar", "dsy", "dgp", "dsc", "asy", "agp", "asc", "mm", "vv", "aa", "gg",
                "fw0", "fw1", "fw2", "fw3", "sw0", "sw1", "sw2", "sw3", "cc_rn2", "cc_y2",
                "cc_rn", "cc_wgu", "cc_wdn", "cc_ow", "cc_ar", "cc_y"]

    with contextlib.ExitStack() as g:
        p = P(nc, g, semnames)
        W, EM = p.wait, p.emit

        def gack():
            return p.ack("gpsimd", "dgp", "agp")

        def sack():
            return p.ack("sync", "dsy", "asy")

        nums = sorted(s.num for s in p.sems.values())
        assert nums == list(range(nums[0], nums[0] + len(nums))), nums
        if preamble == "clear_sled":
            for _ in range(40):
                nc.gpsimd.nop(cycle_cnt=2400)
            nc.gpsimd.sem_clear(range(nums[0], nums[-1] + 1))
            for eng in (nc.sync, nc.vector, nc.scalar, nc.tensor):
                for _ in range(120):
                    eng.nop(cycle_cnt=2400)
            nc.all_engine_barrier()

        sbg = lambda name, shape, dtype=F32: g.enter_context(nc.sbuf_tensor(name, shape, dtype))
        evs = sbg("evs", [128, NET * 24])
        asb = sbg("asb", [128, NET * 16])
        carry = sbg("carry", [128, NET * N])

        def ecol(et, c):
            return evs[:, et * 24 + c:et * 24 + c + 1]

        with contextlib.ExitStack() as ph:
            sb = lambda name, shape, dtype=F32: ph.enter_context(nc.sbuf_tensor(name, shape, dtype))
            nwr = sb("nwr", [128, D])
            rsd = [sb(f"rsd{i}", [128, D]) for i in range(2)]
            sq_scr = sb("sq_scr", [128, D])
            xn_t = [sb(f"xn_t{i}", [128, D]) for i in range(2)]
            rn_bf = [sb(f"rn_bf{i}", [128, D], BF16) for i in range(2)]
            xn_bf = [sb(f"xn_bf{i}", [128, D], BF16) for i in range(2)]
            rnT_sb = [sb(f"rnT_sb{i}", [128, D], BF16) for i in range(2)]
            xnT_sb = [sb(f"xnT_sb{i}", [128, D], BF16) for i in range(2)]
            idn = sb("idn", [128, 128], BF16)
            stat = sb("stat", [128, 8])
            wf32 = sb("wf32", [128, 4 * 2048])
            wbf = sb("wbf", [128, 4 * 2048], BF16)
            pt = [ph.enter_context(nc.psum_tensor(f"pt{i}", [128, 128], BF16)) for i in range(4)]

            EM("gpsimd", lambda e: e.dma_start(out=nwr[:, :], in_=nw_rep), inc="dgp", inck=16)
            EM("gpsimd", lambda e: e.dma_start(
                out=evs[:, :].rearrange("p (t c) -> p t c", t=NET),
                in_=evec.rearrange("(t p) c -> p t c", p=128)), inc="dgp", inck=16)
            a_ev = gack()
            EM("gpsimd", lambda e: e.memset(idn[:, :], 0.0), inc="gg")
            idv = EM("gpsimd", lambda e: e.affine_select(
                out=idn[:, :], in_=idn[:, :], compare_op=OP.not_equal, fill=1.0,
                base=0, pattern=[[-1, 128]], channel_multiplier=1), inc="gg")
            W("scalar", "agp", a_ev)
            EM("scalar", lambda e: e.activation(
                asb[:, :].rearrange("p (t n) -> p t n", t=NET),
                evs[:, :].rearrange("p (t c) -> p t c", t=NET)[:, :, 8:24], AF.Exp), inc="aa")
            W("vector", "aa", p.c["aa"])
            EM("vector", lambda e: e.tensor_scalar_mul(asb[:, :], asb[:, :], -1.0), inc="vv")

            cast_srcs = []
            for k in range(16):
                cast_srcs.append((wguT_sl, ag_wgu_in, 0, k * 2048))
            for rt in range(8):
                cast_srcs.append((wdnT_sl, ag_wdn_in, rt * 128, 0))
            for rt in range(4):
                cast_srcs.append((owT_sl, ag_ow_in, rt * 128, 0))
            castv, swc = {}, {}
            cu = [0]

            def emit_cast_unit():
                k = cu[0]
                if k >= len(cast_srcs):
                    return
                cu[0] += 1
                s = k % 4
                src_t, dst_t, r0, c0 = cast_srcs[k]
                if (k - 4) in castv:
                    W("gpsimd", "vv", castv[k - 4])
                EM("gpsimd", lambda e, src_t=src_t, r0=r0, c0=c0, s=s: e.dma_start(
                    out=wf32[:, s * 2048:(s + 1) * 2048],
                    in_=src_t[r0:r0 + 128, c0:c0 + 2048]), inc=f"fw{s}", inck=16)
                fcnt = p.c[f"fw{s}"]
                W("vector", f"fw{s}", fcnt)
                if (k - 4) in swc:
                    W("vector", f"sw{s}", swc[k - 4])
                castv[k] = EM("vector", lambda e, s=s: e.tensor_copy(
                    wbf[:, s * 2048:(s + 1) * 2048], wf32[:, s * 2048:(s + 1) * 2048]), inc="vv")
                W("sync", "vv", castv[k])
                EM("sync", lambda e, dst_t=dst_t, r0=r0, c0=c0, s=s: e.dma_start(
                    out=dst_t.ap()[r0:r0 + 128, c0:c0 + 2048],
                    in_=wbf[:, s * 2048:(s + 1) * 2048]), inc=f"sw{s}", inck=16)
                swc[k] = p.c[f"sw{s}"]

            rsd_ack = {}
            for t in range(2):
                EM("sync", lambda e, t=t: e.dma_start(
                    out=rsd[t % 2][:, :], in_=resid[t * 128:(t + 1) * 128, :]), inc="dsy", inck=16)
                rsd_ack[t] = sack()

            tT_ack = {}
            xnf = {}
            for t in range(8):
                b = t % 2
                W("vector", "asy", rsd_ack[t])
                rs1 = EM("vector", lambda e, b=b: e.reduce_sum(
                    stat[:, 0:1], rsd[b][:, :], axis=mybir.AxisListType.X), inc="vv")
                W("vector", "vv", rs1)
                EM("vector", lambda e: e.tensor_scalar_mul(stat[:, 0:1], stat[:, 0:1], 1.0 / D))
                sqv = EM("vector", lambda e, b=b: e.tensor_mul(sq_scr[:, :], rsd[b][:, :], rsd[b][:, :]),
                         inc="vv")
                W("vector", "vv", sqv)
                rs2 = EM("vector", lambda e: e.reduce_sum(
                    stat[:, 1:2], sq_scr[:, :], axis=mybir.AxisListType.X), inc="vv")
                W("vector", "vv", rs2)
                EM("vector", lambda e: e.tensor_scalar_mul(stat[:, 1:2], stat[:, 1:2], 1.0 / D))
                EM("vector", lambda e: e.tensor_mul(stat[:, 7:8], stat[:, 0:1], stat[:, 0:1]))
                EM("vector", lambda e: e.tensor_sub(stat[:, 2:3], stat[:, 1:2], stat[:, 7:8]))
                vv1 = EM("vector", lambda e: e.tensor_scalar(
                    stat[:, 3:4], stat[:, 2:3], 1.0, EPS, OP.mult, OP.add), inc="vv")
                W("scalar", "vv", vv1)
                EM("scalar", lambda e: e.activation(stat[:, 4:5], stat[:, 3:4], AF.Ln))
                aa1 = EM("scalar", lambda e: e.activation(stat[:, 5:6], stat[:, 4:5], AF.Exp, scale=-0.5),
                         inc="aa")
                aa1b = EM("scalar", lambda e: e.activation(stat[:, 6:7], stat[:, 4:5], AF.Exp, scale=-1.0),
                          inc="aa")
                W("vector", "aa", aa1b)
                if t >= 2:
                    W("vector", "asy", xnf[t - 2])
                xnv = EM("vector", lambda e, b=b: e.tensor_scalar(
                    xn_t[b][:, :], rsd[b][:, :], stat[:, 0:1], stat[:, 5:6], OP.subtract, OP.mult), inc="vv")
                if t + 2 < 8:
                    W("sync", "vv", xnv)
                    EM("sync", lambda e, t=t: e.dma_start(
                        out=rsd[t % 2][:, :], in_=resid[(t + 2) * 128:(t + 3) * 128, :]), inc="dsy", inck=16)
                    rsd_ack[t + 2] = sack()
                EM("vector", lambda e: e.tensor_mul(stat[:, 6:7], stat[:, 2:3], stat[:, 6:7]))
                vv2 = EM("vector", lambda e: e.tensor_scalar(
                    stat[:, 6:7], stat[:, 6:7], 1.0, EPS, OP.mult, OP.add), inc="vv")
                W("scalar", "vv", vv2)
                EM("scalar", lambda e: e.activation(stat[:, 6:7], stat[:, 6:7], AF.Ln))
                aa2 = EM("scalar", lambda e: e.activation(stat[:, 6:7], stat[:, 6:7], AF.Exp, scale=-0.5),
                         inc="aa")
                W("vector", "aa", aa2)
                rnv = EM("vector", lambda e, b=b: e.scalar_tensor_tensor(
                    rn_bf[b][:, :], xn_t[b][:, :], stat[:, 6:7], nwr[:, :], OP.mult, OP.mult), inc="vv")
                vd = EM("vector", lambda e, b=b: e.tensor_copy(xn_bf[b][:, :], xn_t[b][:, :]), inc="vv")
                W("sync", "vv", vd)
                EM("sync", lambda e, b=b, t=t: e.dma_start(
                    out=XN_SP.ap()[t * 128:(t + 1) * 128, :], in_=xn_t[b][:, :]), inc="dsy", inck=16)
                xnf[t] = sack()
                W("tensor", "vv", vd)
                if t >= 2:
                    W("tensor", "asy", tT_ack[t - 2])
                tcp = {}
                if t == 0:
                    W("tensor", "gg", idv)
                for half, (srcb, dstb) in enumerate([(rn_bf[b], rnT_sb[b]), (xn_bf[b], xnT_sb[b])]):
                    for dt in range(16):
                        j = half * 16 + dt
                        pj = j % 4
                        if (j - 4) in tcp:
                            W("tensor", "aa", tcp[j - 4])
                        tmm = EM("tensor", lambda e, srcb=srcb, dt=dt, pj=pj: e.transpose(
                            pt[pj][:, :], srcb[:, dt * 128:(dt + 1) * 128], idn[:, :]), inc="mm")
                        W("scalar", "mm", tmm)
                        tcp[j] = EM("scalar", lambda e, dstb=dstb, dt=dt, pj=pj: e.activation(
                            dstb[:, dt * 128:(dt + 1) * 128], pt[pj][:, :], AF.Identity), inc="aa")
                W("sync", "aa", tcp[31])
                EM("sync", lambda e, b=b, t=t: e.dma_start(
                    out=ag_rn_in[t // 4].ap()[:, (t % 4) * 128:(t % 4 + 1) * 128]
                    .rearrange("(d p) l -> p d l", p=128),
                    in_=rnT_sb[b][:, :].rearrange("p (d l) -> p d l", d=16)), inc="dsy", inck=16)
                EM("sync", lambda e, b=b, t=t: e.dma_start(
                    out=XNT_SP.ap()[:, t * 128:(t + 1) * 128].rearrange("(d p) l -> p d l", p=128),
                    in_=xnT_sb[b][:, :].rearrange("p (d l) -> p d l", d=16)), inc="dsy", inck=16)
                tT_ack[t] = sack()
                if t == 3:
                    W("gpsimd", "asy", tT_ack[3])
                    EM("gpsimd", lambda e: e.collective_compute(
                        "AllGather", OP.bypass, replica_groups=[list(range(NCORE))],
                        ins=[ag_rn_in[0].ap().opt()], outs=[RNT_H[0].ap().opt()]), inc="cc_rn")
                for _ in range(4):
                    emit_cast_unit()

            W("gpsimd", "asy", tT_ack[7])
            EM("gpsimd", lambda e: e.collective_compute(
                "AllGather", OP.bypass, replica_groups=[list(range(NCORE))],
                ins=[ag_rn_in[1].ap().opt()], outs=[RNT_H[1].ap().opt()]), inc="cc_rn2")
            while cu[0] < len(cast_srcs):
                emit_cast_unit()
            for s in range(4):
                W("gpsimd", f"sw{s}", p.c[f"sw{s}"])
            EM("gpsimd", lambda e: e.collective_compute(
                "AllGather", OP.bypass, replica_groups=[list(range(NCORE))],
                ins=[ag_wgu_in.ap().opt()], outs=[WGUT.ap().opt()]), inc="cc_wgu")
            p.barrier(dma_flush=[("gpsimd", "dgp"), ("sync", "dsy")])

        with contextlib.ExitStack() as ph:
            sb = lambda name, shape, dtype=F32: ph.enter_context(nc.sbuf_tensor(name, shape, dtype))
            psb = lambda name, shape: ph.enter_context(nc.psum_tensor(name, shape, F32))
            wstage = sb("wstage", [128, 16 * 512])
            inw_bf = sb("inw_bf", [128, 16 * 512], BF16)
            skw_bf = sb("skw_bf", [128, 16 * 512], BF16)
            wd1_bf = sb("wd1_bf", [128, NET * 128], BF16)
            wbc_bf = sb("wbc_bf", [128, NET * 32], BF16)
            rhs = [sb(f"rhs{i}", [128, 16 * 512], BF16) for i in range(2)]
            xin_buf = [sb(f"xinb{et}", [128, 515]) for et in range(NET)]
            acc = [sb(f"acc{et}", [128, 512]) for et in range(NET)]
            sk2_t = [[sb(f"sk2t{i}_{et}", [128, 512]) for et in range(NET)] for i in range(2)]
            d1e = [sb(f"d1e{i}", [128, 512]) for i in range(2)]
            bce = [sb(f"bce{i}", [128, 512]) for i in range(2)]
            x_bf = sb("x_bf", [128, NET * L], BF16)
            ps_xin = [psb(f"ps_xin{i}", [128, 512]) for i in range(2)]
            ps_skip = [psb(f"ps_skip{i}", [128, 512]) for i in range(2)]
            ps_d1 = psb("ps_d1", [128, 512])
            ps_bc = psb("ps_bc", [128, 512])

            def load_cast(src, w, sl, nfree):
                EM("gpsimd", lambda e, src=src, sl=sl, nfree=nfree: e.dma_start(
                    out=wstage[:, 0:nfree].rearrange("p (t c) -> p t c", t=sl),
                    in_=src.rearrange("(t p) c -> p t c", p=128)), inc="dgp", inck=16)
                a_ld = gack()
                W("vector", "agp", a_ld)
                cv = EM("vector", lambda e, w=w, nfree=nfree: e.tensor_copy(w[:, 0:nfree], wstage[:, 0:nfree]),
                        inc="vv")
                W("gpsimd", "vv", cv)

            load_cast(inwT, inw_bf, 16, 16 * 512)
            load_cast(skwT, skw_bf, 16, 16 * 512)
            load_cast(wd1T, wd1_bf, NET, NET * 128)
            load_cast(wbcT, wbc_bf, NET, NET * 32)
            for et in range(NET - 1):
                EM("vector", lambda e, et=et: e.memset(xin_buf[et][:, 0:3], 0.0))
            v_w = EM("vector", lambda e: e.memset(xin_buf[NET - 1][:, 0:3], 0.0), inc="vv")

            W("sync", "cc_rn", 1)
            NLC = 16
            mm_xin, mm_skip, evac_xin, evac_skip = {}, {}, {}, {}
            xsilu, sk2_ack = {}, {}
            rhs_cnt = {}

            def rhs_load(lc):
                s = lc % 2
                c, hh = lc // 2, lc % 2
                if lc == 1:
                    W("sync", "cc_rn2", 1)
                EM("sync", lambda e, s=s, c=c, hh=hh: e.dma_start(
                    out=rhs[s][:, :].rearrange("p (d l) -> p d l", d=16),
                    in_=RNT_H[hh].ap()[c * D:(c + 1) * D, :]
                    .rearrange("(d p) l -> p d l", p=128)), inc=f"fw{s}", inck=16)
                rhs_cnt[lc] = p.c[f"fw{s}"]

            d1mm, bcmm, d1ev, bcev, d1_ack = {}, {}, {}, {}, {}
            rhs_load(0)
            for lc in range(NLC):
                b = lc % 2
                if lc + 1 < NLC:
                    if lc >= 1:
                        W("sync", "mm", mm_skip[(lc - 1, NET - 1)])
                    rhs_load(lc + 1)
                W("tensor", f"fw{b}", rhs_cnt[lc])
                if lc == 0:
                    W("tensor", "vv", v_w)
                for et in range(NET):
                    pb = et % 2
                    prev = (lc, et - 2) if et >= 2 else (lc - 1, et + 2)
                    if prev in evac_xin:
                        W("tensor", "vv", evac_xin[prev])
                    for dt in range(16):
                        mm_xin[(lc, et)] = EM("tensor", lambda e, et=et, dt=dt, b=b, pb=pb: e.matmul(
                            ps_xin[pb][:, :], inw_bf[:, dt * 512 + et * 128:dt * 512 + (et + 1) * 128],
                            rhs[b][:, dt * 512:(dt + 1) * 512], start=(dt == 0), stop=(dt == 15)),
                            inc="mm" if dt == 15 else None)
                    if prev in evac_skip:
                        W("tensor", "aa", evac_skip[prev])
                    for dt in range(16):
                        mm_skip[(lc, et)] = EM("tensor", lambda e, et=et, dt=dt, b=b, pb=pb: e.matmul(
                            ps_skip[pb][:, :], skw_bf[:, dt * 512 + et * 128:dt * 512 + (et + 1) * 128],
                            rhs[b][:, dt * 512:(dt + 1) * 512], start=(dt == 0), stop=(dt == 15)),
                            inc="mm" if dt == 15 else None)

                    W("vector", "mm", mm_xin[(lc, et)])
                    if (lc - 1, et) in xsilu:
                        W("vector", "aa", xsilu[(lc - 1, et)])
                    evv = EM("vector", lambda e, et=et, pb=pb: e.tensor_copy(
                        xin_buf[et][:, 3:515], ps_xin[pb][:, :]), inc="vv")
                    W("vector", "vv", evv)
                    cv0 = EM("vector", lambda e, et=et: e.tensor_scalar(
                        acc[et][:, :], xin_buf[et][:, 0:512], ecol(et, 0), None, OP.mult), inc="vv")
                    for k in range(1, 4):
                        W("vector", "vv", cv0)
                        cv0 = EM("vector", lambda e, et=et, k=k: e.scalar_tensor_tensor(
                            acc[et][:, :], xin_buf[et][:, k:k + 512], ecol(et, k),
                            acc[et][:, :], OP.mult, OP.add), inc="vv")
                    evac_xin[(lc, et)] = cv0
                    EM("vector", lambda e, et=et: e.tensor_copy(
                        xin_buf[et][:, 0:3], xin_buf[et][:, 512:515]), inc="vv")
                    W("scalar", "vv", evac_xin[(lc, et)])
                    xsilu[(lc, et)] = EM("scalar", lambda e, et=et, lc=lc: e.activation(
                        x_bf[:, et * L + lc * 512:et * L + (lc + 1) * 512], acc[et][:, :], AF.Silu,
                        bias=ecol(et, 4)), inc="aa")
                    W("scalar", "mm", mm_skip[(lc, et)])
                    if (lc - 2, et) in sk2_ack:
                        W("scalar", "agp", sk2_ack[(lc - 2, et)])
                    evac_skip[(lc, et)] = EM("scalar", lambda e, et=et, pb=pb, b=b: e.activation(
                        sk2_t[b][et][:, :], ps_skip[pb][:, :], AF.Silu), inc="aa")
                    W("gpsimd", "aa", evac_skip[(lc, et)])
                    EM("gpsimd", lambda e, et=et, lc=lc, b=b: e.dma_start(
                        out=SK2_SP.ap()[et * 128:(et + 1) * 128, lc * 512:(lc + 1) * 512],
                        in_=sk2_t[b][et][:, :]), inc="dgp", inck=16)
                    sk2_ack[(lc, et)] = gack()

                W("tensor", "aa", xsilu[(lc, NET - 1)])
                if lc >= 1:
                    W("tensor", "vv", d1ev[lc - 1])
                for et in range(NET):
                    d1mm[lc] = EM("tensor", lambda e, et=et, lc=lc: e.matmul(
                        ps_d1[:, :], wd1_bf[:, et * 128:(et + 1) * 128],
                        x_bf[:, et * L + lc * 512:et * L + (lc + 1) * 512],
                        start=(et == 0), stop=(et == NET - 1)),
                        inc="mm" if et == NET - 1 else None)
                if lc >= 1:
                    W("tensor", "vv", bcev[lc - 1])
                for et in range(NET):
                    bcmm[lc] = EM("tensor", lambda e, et=et, lc=lc: e.matmul(
                        ps_bc[0:32, :], wbc_bf[:, et * 32:(et + 1) * 32],
                        x_bf[:, et * L + lc * 512:et * L + (lc + 1) * 512],
                        start=(et == 0), stop=(et == NET - 1)),
                        inc="mm" if et == NET - 1 else None)
                W("vector", "mm", d1mm[lc])
                if lc >= 2:
                    W("vector", "agp", d1_ack[lc - 2])
                d1ev[lc] = EM("vector", lambda e, b=b: e.tensor_copy(d1e[b][:, :], ps_d1[:, :]), inc="vv")
                W("vector", "mm", bcmm[lc])
                bcev[lc] = EM("vector", lambda e, b=b: e.tensor_copy(bce[b][0:32, :], ps_bc[0:32, :]),
                              inc="vv")
                W("gpsimd", "vv", bcev[lc])
                EM("gpsimd", lambda e, lc=lc, b=b: e.dma_start(
                    out=ar_in.ap()[0:R, lc * 512:(lc + 1) * 512], in_=d1e[b][:, :]), inc="dgp", inck=16)
                EM("gpsimd", lambda e, lc=lc, b=b: e.dma_start(
                    out=ar_in.ap()[R:R + 32, lc * 512:(lc + 1) * 512], in_=bce[b][0:32, :]), inc="dgp", inck=16)
                d1_ack[lc] = gack()

            x_ready = p.c["aa"]
            W("gpsimd", "dgp", p.c["dgp"])
            EM("gpsimd", lambda e: e.collective_compute(
                "AllReduce", OP.add, replica_groups=[list(range(NCORE))],
                ins=[ar_in.ap().opt()], outs=[ar_out.ap().opt()]), inc="cc_ar")
            EM("gpsimd", lambda e: e.collective_compute(
                "AllGather", OP.bypass, replica_groups=[list(range(NCORE))],
                ins=[ag_wdn_in.ap().opt()], outs=[WDNT.ap().opt()]), inc="cc_wdn")
            EM("gpsimd", lambda e: e.collective_compute(
                "AllGather", OP.bypass, replica_groups=[list(range(NCORE))],
                ins=[ag_ow_in.ap().opt()], outs=[OWT.ap().opt()]), inc="cc_ow")
            W("gpsimd", "aa", x_ready)
            for et in range(NET):
                EM("gpsimd", lambda e, et=et: e.dma_start(
                    out=X_SP.ap()[et * 128:(et + 1) * 128, :], in_=x_bf[:, et * L:(et + 1) * L]),
                    inc="dgp", inck=16)
            p.barrier(dma_flush=[("gpsimd", "dgp"), ("sync", "dsy")])

        with contextlib.ExitStack() as ph:
            sb = lambda name, shape, dtype=F32: ph.enter_context(nc.sbuf_tensor(name, shape, dtype))
            psb = lambda name, shape: ph.enter_context(nc.psum_tensor(name, shape, F32))
            d1t_f = sb("d1t_f", [128, CH])
            d1t_bf = sb("d1t_bf", [128, CH], BF16)
            bmcs = sb("bmcs", [32, CH])
            bmc_bf = sb("bmc_bf", [32, 2 * CH], BF16)
            wd2_bf = sb("wd2_bf", [128, EC], BF16)
            delta = sb("delta", [128, NET * CH])
            xc_bf = [sb(f"xc_bf{i}", [128, NET * CH], BF16) for i in range(2)]
            u_bf = sb("u_bf", [128, NET * CH], BF16)
            y_acc = sb("y_acc", [128, NET * CH])
            sk2c = [sb(f"sk2c{i}", [128, CH]) for i in range(2)]
            p_buf = sb("p_buf", [128, 2 * CH])
            ug_buf = sb("ug_buf", [128, 2 * CH], BF16)
            h_buf = sb("h_buf", [128, 2 * CH])
            prod = [sb(f"prod{i}", [128, CH]) for i in range(3)]
            brep = sb("brep", [128, 2 * CH], BF16)
            crep = sb("crep", [128, 2 * CH], BF16)
            t0 = sb("t0", [128, CH])
            t1 = sb("t1", [128, CH])
            y_bf = [sb(f"y_bf{i}", [128, CH], BF16) for i in range(2)]
            sel_bf = sb("sel_bf", [32, 32 * 128], BF16)
            ps_dp = psb("ps_dp", [128, CH])
            ps_b = psb("ps_b", [128, 512])
            ps_c = psb("ps_c", [128, 512])
            xnt = sb("xnt", [128, 16 * 512], BF16)
            wg_g = [sb(f"wg_g{i}", [128, 16 * 128], BF16) for i in range(3)]
            wg_u = [sb(f"wg_u{i}", [128, 16 * 128], BF16) for i in range(3)]
            gb_st = sb("gb_st", [128, 4 * 512], BF16)
            h_big = sb("h_big", [128, 4 * 512], BF16)
            h_in = [sb(f"h_in{i}", [128, 2 * 512], BF16) for i in range(3)]
            wdn_t = [sb(f"wdn_t{i}", [128, 2 * 512], BF16) for i in range(3)]
            o2_t = [sb(f"o2_t{i}", [128, 512], BF16) for i in range(2)]
            pf = [psb(f"pf{i}", [128, 512]) for i in range(4)]

            EM("gpsimd", lambda e: e.dma_start(out=sel_bf[:, :], in_=sel_in), inc="dgp", inck=16)
            EM("gpsimd", lambda e: e.dma_start(out=d1t_f[:, 0:EC], in_=wd2T), inc="dgp", inck=16)
            a_w = gack()
            W("vector", "agp", a_w)
            wcv = EM("vector", lambda e: e.tensor_copy(wd2_bf[:, :], d1t_f[:, 0:EC]), inc="vv")
            W("sync", "cc_ar", 1)
            W("sync", "vv", wcv)
            W("tensor", "agp", a_w)
            W("tensor", "vv", wcv)
            xnt_ack = [None, None]

            dp_mm, dp_chain, delta_ok, pexp = {}, {}, {}, {}
            brep_ok = {}
            ug_done, tts_done, prod_done, add_done = {}, {}, {}, {}
            stt_done, ymul_done, ybf_ack, sk_ack = {}, {}, {}, {}
            d1_cast, xc_ack = {}, {}
            prod_war = {}
            gu_mm, silu_done, hmult_done, hst_ack = {}, {}, {}, {}
            dn_mm, o2_evac, o2_ack = {}, {}, {}
            dn_state = {"gpr": 0, "gprl": 0}

            fwcnt = {}

            def gate_loads(lh):
                for it in range(64):
                    if it == 0 and lh == 0:
                        EM("sync", lambda e, lh=lh: e.dma_start(
                            out=xnt[:, :].rearrange("p (d l) -> p d l", d=16),
                            in_=XNT_SP.ap()[:, lh * 512:(lh + 1) * 512]
                            .rearrange("(d p) l -> p d l", p=128)), inc="dsy", inck=16)
                        xnt_ack[lh] = sack()
                        W("scalar", "cc_wgu", 1)
                    b = (lh * 64 + it) % 3
                    prevk3 = (lh, it - 3) if it >= 3 else ((lh - 1, 61 + it) if lh >= 1 else None)
                    if prevk3 in gu_mm:
                        W("scalar", "mm", gu_mm[prevk3])
                    EM("scalar", lambda e, it=it, b=b: e.dma_start(
                        out=wg_g[b][:, :],
                        in_=WGUT.ap()[it:it + 1, :].rearrange("r (p c) -> (r p) c", p=128)),
                        inc=f"fw{b}", inck=16)
                    EM("scalar", lambda e, it=it, b=b: e.dma_start(
                        out=wg_u[b][:, :],
                        in_=WGUT.ap()[64 + it:65 + it, :].rearrange("r (p c) -> (r p) c", p=128)),
                        inc=f"fw{b}", inck=16)
                    fwcnt[("g", lh, it)] = p.c[f"fw{b}"]
                    yield

            def gate_comp(lh):
                for it in range(64):
                    if it == 0:
                        if lh == 1:
                            W("sync", "mm", gu_mm[(0, 63)])
                            EM("sync", lambda e, lh=lh: e.dma_start(
                                out=xnt[:, :].rearrange("p (d l) -> p d l", d=16),
                                in_=XNT_SP.ap()[:, lh * 512:(lh + 1) * 512]
                                .rearrange("(d p) l -> p d l", p=128)), inc="dsy", inck=16)
                            xnt_ack[lh] = sack()
                        W("tensor", "asy", xnt_ack[lh])
                    b = (lh * 64 + it) % 3
                    b2 = it % 2
                    pg, pu = pf[2 * b2], pf[2 * b2 + 1]
                    prevk = (lh, it - 2) if it >= 2 else ((lh - 1, 62 + it) if lh >= 1 else None)
                    W("tensor", f"fw{b}", fwcnt[("g", lh, it)])
                    if prevk in silu_done:
                        W("tensor", "aa", silu_done[prevk])
                        W("tensor", "vv", hmult_done[prevk])
                    for dt in range(16):
                        EM("tensor", lambda e, dt=dt, b=b, pg=pg: e.matmul(
                            pg[:, :], wg_g[b][:, dt * 128:(dt + 1) * 128],
                            xnt[:, dt * 512:(dt + 1) * 512], start=(dt == 0), stop=(dt == 15)))
                    for dt in range(16):
                        gu_mm[(lh, it)] = EM("tensor", lambda e, dt=dt, b=b, pu=pu: e.matmul(
                            pu[:, :], wg_u[b][:, dt * 128:(dt + 1) * 128],
                            xnt[:, dt * 512:(dt + 1) * 512], start=(dt == 0), stop=(dt == 15)),
                            inc="mm" if dt == 15 else None)
                    W("scalar", "mm", gu_mm[(lh, it)])
                    j = it % 4
                    silu_done[(lh, it)] = EM("scalar", lambda e, j=j, pg=pg: e.activation(
                        gb_st[:, j * 512:(j + 1) * 512], pg[:, :], AF.Identity), inc="aa")
                    W("vector", "mm", gu_mm[(lh, it)])
                    if j == 0:
                        bk = (lh, it // 4 - 1) if it >= 4 else ((lh - 1, 15) if lh >= 1 else None)
                        if bk in hst_ack:
                            W("vector", "asy", hst_ack[bk])
                    hmult_done[(lh, it)] = EM("vector", lambda e, j=j, pu=pu: e.tensor_copy(
                        h_big[:, j * 512:(j + 1) * 512], pu[:, :]), inc="vv")
                    if j == 3:
                        sb_ok = EM("scalar", lambda e: e.activation(
                            gb_st[:, :], gb_st[:, :], AF.Silu), inc="aa")
                        W("vector", "aa", sb_ok)
                        hb = EM("vector", lambda e: e.tensor_mul(
                            h_big[:, :], h_big[:, :], gb_st[:, :]), inc="vv")
                        W("sync", "vv", hb)
                        it0 = it - 3
                        EM("sync", lambda e, it0=it0, lh=lh: e.dma_start(
                            out=H_SP.ap()[it0 * 128:(it0 + 4) * 128, lh * 512:(lh + 1) * 512]
                            .rearrange("(i p) l -> p i l", p=128),
                            in_=h_big[:, :].rearrange("p (i l) -> p i l", i=4)), inc="dsy", inck=16)
                        hst_ack[(lh, it // 4)] = sack()
                    yield

            def down_loads():
                for lh in range(2):
                    for dc in range(4):
                        for pr in range(32):
                            it = 2 * pr
                            gpr = dn_state["gprl"]
                            dn_state["gprl"] += 1
                            b = gpr % 3
                            if (gpr - 3) in dn_mm:
                                W("scalar", "mm", dn_mm[gpr - 3])
                            if dc == 0 and pr == 0:
                                W("scalar", "cc_wdn", 1)
                                W("scalar", "asy", hst_ack[(lh, 15)])
                            EM("scalar", lambda e, it=it, dc=dc, b=b: e.dma_start(
                                out=wdn_t[b][:, :].rearrange("p (i c) -> p i c", i=2),
                                in_=WDNT.ap()[it * 128:(it + 2) * 128, dc * 512:(dc + 1) * 512]
                                .rearrange("(i p) c -> p i c", p=128)), inc=f"fw{b}", inck=16)
                            EM("scalar", lambda e, it=it, lh=lh, b=b: e.dma_start(
                                out=h_in[b][:, :].rearrange("p (i c) -> p i c", i=2),
                                in_=H_SP.ap()[it * 128:(it + 2) * 128, lh * 512:(lh + 1) * 512]
                                .rearrange("(i p) c -> p i c", p=128)), inc=f"fw{b}", inck=16)
                            fwcnt[("d", gpr)] = p.c[f"fw{b}"]
                            yield
                        yield

            def down_comp():
                for lh in range(2):
                    for dc in range(4):
                        for pr in range(32):
                            it = 2 * pr
                            gpr = dn_state["gpr"]
                            dn_state["gpr"] += 1
                            b = gpr % 3
                            W("tensor", f"fw{b}", fwcnt[("d", gpr)])
                            if it == 0:
                                if lh == 0 and dc == 0:
                                    W("tensor", "vv", hmult_done[(1, 63)])
                                    W("tensor", "aa", silu_done[(1, 63)])
                                else:
                                    W("tensor", "aa", o2_evac[3])
                            for ii in range(2):
                                for lt2 in range(4):
                                    dn_mm[gpr] = EM(
                                        "tensor", lambda e, ii=ii, lt2=lt2, b=b, it=it: e.matmul(
                                            pf[lt2][:, :],
                                            h_in[b][:, ii * 512 + lt2 * 128:ii * 512 + (lt2 + 1) * 128],
                                            wdn_t[b][:, ii * 512:(ii + 1) * 512],
                                            start=(it + ii == 0), stop=(it + ii == 63)),
                                        inc="mm" if (ii == 1 and lt2 == 3) else None)
                            yield
                        W("scalar", "mm", dn_mm[dn_state["gpr"] - 1])
                        for lt2 in range(4):
                            ob = lt2 % 2
                            okey = (lt2 - 2) if lt2 >= 2 else (lt2 + 2)
                            if okey in o2_ack:
                                W("scalar", "asy", o2_ack[okey])
                            o2_evac[lt2] = EM("scalar", lambda e, lt2=lt2, ob=ob: e.activation(
                                o2_t[ob][:, :], pf[lt2][:, :], AF.Identity), inc="aa")
                            W("sync", "aa", o2_evac[lt2])
                            EM("sync", lambda e, lh=lh, dc=dc, lt2=lt2, ob=ob: e.dma_start(
                                out=OUT2_SP.ap()[lh * 512 + lt2 * 128:lh * 512 + (lt2 + 1) * 128,
                                                 dc * 512:(dc + 1) * 512], in_=o2_t[ob][:, :]),
                                inc="dsy", inck=16)
                            o2_ack[lt2] = sack()
                        yield

            import itertools as _it
            load_iter = _it.chain(gate_loads(0), gate_loads(1), down_loads())
            comp_iter = _it.chain(gate_comp(0), gate_comp(1), down_comp())
            NUNITS = 64 + 64 + 2 * 4 * 33
            state_lc = [0, 0]

            def pull():
                while state_lc[0] < min(state_lc[1] + 2, NUNITS):
                    next(load_iter)
                    state_lc[0] += 1
                if state_lc[1] < NUNITS:
                    next(comp_iter)
                    state_lc[1] += 1

            bmc_cp, bc_mm_last = {}, {}
            a_bmc_d, a_d1_d, a_xc_d = {}, {}, {}

            def chunk_loads(cnk):
                c0 = cnk * CH
                if cnk >= 1:
                    W("sync", "vv", bmc_cp[cnk - 1])
                EM("sync", lambda e, c0=c0: e.dma_start(
                    out=bmcs[:, :], in_=ar_out.ap()[R:R + 32, c0:c0 + CH]), inc="dsy", inck=16)
                a_bmc_d[cnk] = sack()
                if cnk >= 1:
                    W("sync", "vv", d1_cast[cnk - 1])
                EM("sync", lambda e, c0=c0: e.dma_start(
                    out=d1t_f[:, :], in_=ar_out.ap()[0:R, c0:c0 + CH]), inc="dsy", inck=16)
                a_d1_d[cnk] = sack()
                if cnk >= 2:
                    W("sync", "vv", stt_done[(cnk - 2, NET - 1)])
                EM("sync", lambda e, c0=c0, xb=cnk % 2: e.dma_start(
                    out=xc_bf[xb][:, :].rearrange("p (et l) -> p et l", et=NET),
                    in_=X_SP.ap()[:, c0:c0 + CH].rearrange("(et p) l -> p et l", p=128)),
                    inc="dsy", inck=16)
                a_xc_d[cnk] = sack()

            chunk_loads(0)
            giter = [0]
            for cnk in range(NCH):
                c0 = cnk * CH
                cb2 = cnk % 2
                xb = cnk % 2
                W("vector", "asy", a_bmc_d[cnk])
                if cnk >= 2:
                    W("vector", "mm", bc_mm_last[cnk - 2])
                bmc_cp[cnk] = EM("vector", lambda e, cb2=cb2: e.tensor_copy(
                    bmc_bf[:, cb2 * CH:(cb2 + 1) * CH], bmcs[:, :]), inc="vv")
                W("vector", "asy", a_d1_d[cnk])
                if cnk >= 1:
                    W("vector", "mm", dp_mm[(cnk - 1, NET - 1)])
                d1_cast[cnk] = EM("vector", lambda e: e.tensor_copy(d1t_bf[:, :], d1t_f[:, :]), inc="vv")
                W("tensor", "vv", d1_cast[cnk])
                for et in range(NET):
                    pk = (cnk, et - 1) if et >= 1 else (cnk - 1, NET - 1)
                    if pk in dp_chain:
                        W("tensor", "aa", dp_chain[pk])
                    for hh in range(2):
                        dp_mm[(cnk, et)] = EM("tensor", lambda e, et=et, hh=hh: e.matmul(
                            ps_dp[:, hh * 512:(hh + 1) * 512], wd2_bf[:, et * 128:(et + 1) * 128],
                            d1t_bf[:, hh * 512:(hh + 1) * 512], start=True, stop=True),
                            inc="mm" if hh == 1 else None)
                    W("scalar", "mm", dp_mm[(cnk, et)])
                    if pk in delta_ok:
                        W("scalar", "vv", delta_ok[pk])
                    EM("scalar", lambda e, et=et: e.activation(t0[:, :], ps_dp[:, :], AF.Relu, bias=ecol(et, 5)))
                    EM("scalar", lambda e, et=et: e.activation(t1[:, :], ps_dp[:, :], AF.Abs, bias=ecol(et, 5)))
                    EM("scalar", lambda e: e.activation(t1[:, :], t1[:, :], AF.Exp, scale=-1.0))
                    dp_chain[(cnk, et)] = EM("scalar", lambda e: e.activation(t1[:, :], t1[:, :], AF.Ln, bias=1.0),
                                             inc="aa")
                    W("vector", "aa", dp_chain[(cnk, et)])
                    if cnk >= 1:
                        W("vector", "aa", pexp[(cnk - 1, N - 1, et)])
                    delta_ok[(cnk, et)] = EM("vector", lambda e, et=et: e.tensor_add(
                        delta[:, et * CH:(et + 1) * CH], t0[:, :], t1[:, :]), inc="vv")
                W("vector", "asy", a_xc_d[cnk])
                for et in range(NET):
                    EM("vector", lambda e, et=et, xb=xb: e.tensor_mul(
                        u_bf[:, et * CH:(et + 1) * CH], delta[:, et * CH:(et + 1) * CH],
                        xc_bf[xb][:, et * CH:(et + 1) * CH]), inc="vv")

                pending_adds = []
                pend_carry = {}

                def flush_add():
                    s2, et2, pb2 = pending_adds.pop(0)
                    W("vector", "gg", prod_done[(cnk, s2)])
                    add_done[(cnk, s2)] = EM("vector", lambda e, et2=et2, pb2=pb2: e.tensor_add(
                        y_acc[:, et2 * CH:(et2 + 1) * CH], y_acc[:, et2 * CH:(et2 + 1) * CH],
                        prod[pb2][:, :]), inc="vv")
                    prod_war[pb2] = add_done[(cnk, s2)]

                for n in range(N):
                    nb = n % 2
                    gk = (cnk, n - 2) if n >= 2 else ((cnk - 1, N - 2 + n) if cnk >= 1 else None)
                    if gk is not None and gk in brep_ok:
                        s_last = (gk[1]) * NET + NET - 1
                        W("tensor", "vv", ug_done[(gk[0], s_last)])
                        W("tensor", "gg", prod_done[(gk[0], s_last)])
                    pk = (cnk, n - 1) if n >= 1 else ((cnk - 1, N - 1) if cnk >= 1 else None)
                    if pk in brep_ok:
                        W("tensor", "aa", brep_ok[pk][1])
                    if n == 0:
                        W("tensor", "vv", bmc_cp[cnk])
                    m1 = EM("tensor", lambda e, n=n, cb2=cb2: e.matmul(
                        ps_b[:, :], sel_bf[0:32, n * 128:(n + 1) * 128],
                        bmc_bf[0:32, cb2 * CH:cb2 * CH + 512], start=True, stop=True), inc="mm")
                    m2 = EM("tensor", lambda e, n=n, cb2=cb2: e.matmul(
                        ps_c[:, :], sel_bf[0:32, n * 128:(n + 1) * 128],
                        bmc_bf[0:32, cb2 * CH + 512:(cb2 + 1) * CH], start=True, stop=True), inc="mm")
                    W("scalar", "mm", m2)
                    if gk is not None and gk in brep_ok:
                        s_last = (gk[1]) * NET + NET - 1
                        W("scalar", "vv", ug_done[(gk[0], s_last)])
                        W("scalar", "gg", prod_done[(gk[0], s_last)])
                    eb0 = EM("scalar", lambda e, nb=nb: e.activation(
                        brep[:, nb * CH:nb * CH + 512], ps_b[:, :], AF.Identity), inc="aa")
                    eb1 = EM("scalar", lambda e, nb=nb: e.activation(
                        brep[:, nb * CH + 512:(nb + 1) * CH], ps_c[:, :], AF.Identity), inc="aa")
                    W("tensor", "aa", eb0)
                    m3 = EM("tensor", lambda e, n=n, cb2=cb2: e.matmul(
                        ps_b[:, :], sel_bf[0:32, (16 + n) * 128:(17 + n) * 128],
                        bmc_bf[0:32, cb2 * CH:cb2 * CH + 512], start=True, stop=True), inc="mm")
                    W("tensor", "aa", eb1)
                    m4 = EM("tensor", lambda e, n=n, cb2=cb2: e.matmul(
                        ps_c[:, :], sel_bf[0:32, (16 + n) * 128:(17 + n) * 128],
                        bmc_bf[0:32, cb2 * CH + 512:(cb2 + 1) * CH], start=True, stop=True), inc="mm")
                    bc_mm_last[cnk] = m4
                    W("scalar", "mm", m4)
                    EM("scalar", lambda e, nb=nb: e.activation(
                        crep[:, nb * CH:nb * CH + 512], ps_b[:, :], AF.Identity), inc="aa")
                    ec1 = EM("scalar", lambda e, nb=nb: e.activation(
                        crep[:, nb * CH + 512:(nb + 1) * CH], ps_c[:, :], AF.Identity), inc="aa")
                    brep_ok[(cnk, n)] = (eb1, ec1)

                    for et in range(NET):
                        s = n * NET + et
                        sb_ = s % 2
                        pb = s % 3
                        cidx = et * N + n
                        if n == 0 and et == 0:
                            W("scalar", "vv", delta_ok[(cnk, NET - 1)])
                        tk = (cnk, s - 2) if s >= 2 else ((cnk - 1, N * NET - 2 + s) if cnk >= 1 else None)
                        if tk in tts_done:
                            W("scalar", "vv", tts_done[tk])
                        pexp[(cnk, n, et)] = EM("scalar", lambda e, et=et, n=n, sb_=sb_: e.activation(
                            p_buf[:, sb_ * CH:(sb_ + 1) * CH], delta[:, et * CH:(et + 1) * CH], AF.Exp,
                            scale=asb[:, et * 16 + n:et * 16 + n + 1]), inc="aa")
                        W("vector", "aa", brep_ok[(cnk, n)][0])
                        ug_done[(cnk, s)] = EM("vector", lambda e, et=et, nb=nb, sb_=sb_: e.tensor_mul(
                            ug_buf[:, sb_ * CH:(sb_ + 1) * CH], u_bf[:, et * CH:(et + 1) * CH],
                            brep[:, nb * CH:(nb + 1) * CH]), inc="vv")
                        W("vector", "aa", pexp[(cnk, n, et)])
                        if tk in prod_done:
                            W("vector", "gg", prod_done[tk])
                        if cnk == 0:
                            tts_done[(cnk, s)] = EM("vector", lambda e, sb_=sb_: e.tensor_tensor_scan(
                                h_buf[:, sb_ * CH:(sb_ + 1) * CH], p_buf[:, sb_ * CH:(sb_ + 1) * CH],
                                ug_buf[:, sb_ * CH:(sb_ + 1) * CH], 0.0, OP.mult, OP.add), inc="vv")
                        else:
                            tts_done[(cnk, s)] = EM("vector", lambda e, sb_=sb_, cidx=cidx: e.tensor_tensor_scan(
                                h_buf[:, sb_ * CH:(sb_ + 1) * CH], p_buf[:, sb_ * CH:(sb_ + 1) * CH],
                                ug_buf[:, sb_ * CH:(sb_ + 1) * CH], carry[:, cidx:cidx + 1],
                                OP.mult, OP.add), inc="vv")
                        if s >= 1:
                            ps_, pc_ = pend_carry[s - 1]
                            W("scalar", "vv", tts_done[(cnk, s - 1)])
                            EM("scalar", lambda e, ps_=ps_, pc_=pc_: e.activation(
                                carry[:, pc_:pc_ + 1], h_buf[:, ps_ * CH + CH - 1:ps_ * CH + CH],
                                AF.Identity), inc="aa")
                        pend_carry[s] = (sb_, cidx)
                        W("gpsimd", "vv", tts_done[(cnk, s)])
                        W("gpsimd", "aa", brep_ok[(cnk, n)][1])
                        if n == 0:
                            prod_done[(cnk, s)] = EM("gpsimd", lambda e, et=et, nb=nb, sb_=sb_: e.tensor_mul(
                                y_acc[:, et * CH:(et + 1) * CH], h_buf[:, sb_ * CH:(sb_ + 1) * CH],
                                crep[:, nb * CH:(nb + 1) * CH]), inc="gg")
                        else:
                            if pb in prod_war:
                                W("gpsimd", "vv", prod_war[pb])
                            prod_done[(cnk, s)] = EM("gpsimd", lambda e, nb=nb, sb_=sb_, pb=pb: e.tensor_mul(
                                prod[pb][:, :], h_buf[:, sb_ * CH:(sb_ + 1) * CH],
                                crep[:, nb * CH:(nb + 1) * CH]), inc="gg")
                            pending_adds.append((s, et, pb))
                        if len(pending_adds) > 2:
                            flush_add()
                        if giter[0] >= 8:
                            pull()
                        if giter[0] % 64 == 48:
                            if cnk + 1 < NCH:
                                chunk_loads(cnk + 1)
                        giter[0] += 1
                while pending_adds:
                    flush_add()
                ps_, pc_ = pend_carry[N * NET - 1]
                W("scalar", "vv", tts_done[(cnk, N * NET - 1)])
                EM("scalar", lambda e, ps_=ps_, pc_=pc_: e.activation(
                    carry[:, pc_:pc_ + 1], h_buf[:, ps_ * CH + CH - 1:ps_ * CH + CH],
                    AF.Identity), inc="aa")

                if cnk == 4:
                    W("gpsimd", "asy", ybf_ack[(3, NET - 1)])
                    EM("gpsimd", lambda e: e.collective_compute(
                        "AllGather", OP.bypass, replica_groups=[list(range(NCORE))],
                        ins=[ag_y_h[0].ap().opt()], outs=[Y_H[0].ap().opt()]), inc="cc_y")
                for et in range(NET):
                    yb = et % 2
                    skkey = (cnk, et - 2) if et >= 2 else ((cnk - 1, et + 2) if cnk >= 1 else None)
                    if skkey in ymul_done:
                        W("sync", "gg", ymul_done[skkey])
                    EM("sync", lambda e, et=et, c0=c0, yb=yb: e.dma_start(
                        out=sk2c[yb][:, :], in_=SK2_SP.ap()[et * 128:(et + 1) * 128, c0:c0 + CH]),
                        inc="dsy", inck=16)
                    sk_ack[(cnk, et)] = sack()
                    stt_done[(cnk, et)] = EM("vector", lambda e, et=et, xb=xb: e.scalar_tensor_tensor(
                        y_acc[:, et * CH:(et + 1) * CH], xc_bf[xb][:, et * CH:(et + 1) * CH], ecol(et, 6),
                        y_acc[:, et * CH:(et + 1) * CH], OP.mult, OP.add), inc="vv")
                    W("gpsimd", "vv", stt_done[(cnk, et)])
                    W("gpsimd", "asy", sk_ack[(cnk, et)])
                    if skkey in ybf_ack:
                        W("gpsimd", "asy", ybf_ack[skkey])
                    ymul_done[(cnk, et)] = EM("gpsimd", lambda e, et=et, yb=yb: e.tensor_mul(
                        y_bf[yb][:, :], y_acc[:, et * CH:(et + 1) * CH], sk2c[yb][:, :]), inc="gg")
                    W("sync", "gg", ymul_done[(cnk, et)])
                    EM("sync", lambda e, et=et, cnk=cnk, yb=yb: e.dma_start(
                        out=ag_y_h[cnk // 4].ap()[et * 128:(et + 1) * 128,
                                                  (cnk % 4) * CH:(cnk % 4 + 1) * CH],
                        in_=y_bf[yb][:, :]), inc="dsy", inck=16)
                    ybf_ack[(cnk, et)] = sack()

            while state_lc[1] < NUNITS:
                pull()

            W("gpsimd", "asy", ybf_ack[(NCH - 1, NET - 1)])
            EM("gpsimd", lambda e: e.collective_compute(
                "AllGather", OP.bypass, replica_groups=[list(range(NCORE))],
                ins=[ag_y_h[1].ap().opt()], outs=[Y_H[1].ap().opt()]), inc="cc_y2")
            p.barrier(dma_flush=[("gpsimd", "dgp"), ("sync", "dsy"), ("scalar", "dsc")])

        with contextlib.ExitStack() as ph:
            sb = lambda name, shape, dtype=F32: ph.enter_context(nc.sbuf_tensor(name, shape, dtype))
            psb = lambda name, shape: ph.enter_context(nc.psum_tensor(name, shape, F32))
            y_sb = sb("y_sb", [128, 32 * LC], BF16)
            owt_t = [sb(f"owt_t{i}", [128, 2 * 512], BF16) for i in range(3)]
            xn4 = [sb(f"xn4_{i}", [128, 4 * 512]) for i in range(2)]
            o24 = [sb(f"o24_{i}", [128, 4 * 512], BF16) for i in range(2)]
            rs4 = [sb(f"rs4_{i}", [128, 4 * 512]) for i in range(2)]
            attn4 = [sb(f"attn4_{i}", [128, 4 * 512]) for i in range(2)]
            g4 = [sb(f"g4_{i}", [128, 4 * 512]) for i in range(2)]
            out4 = [sb(f"out4_{i}", [128, 4 * 512]) for i in range(2)]
            ps_o = [psb(f"ps_o{i}", [128, 512]) for i in range(8)]

            W("sync", "cc_y", 1)
            W("sync", "cc_y2", 1)
            W("scalar", "cc_ow", 1)

            def y_load(e):
                for arm in e.switch_core_id(n=128):
                    lcore = physical_to_logical_core_id(int(arm) % 128, 0, container_num_cores=NCORE) % NCORE
                    c0 = (lcore % 4) * LC
                    e.dma_start(
                        out=y_sb[:, :].rearrange("p (et l) -> p et l", et=32),
                        in_=Y_H[lcore // 4].ap()[:, c0:c0 + LC].rearrange("(et p) l -> p et l", p=128),
                    ).then_inc(p.sems["dsy"], 16)
            p.c["dsy"] += 16
            p.prog["sync"].append(y_load)
            yl_ack = sack()

            owp_mm, comb_done, ld_cnt, od_ack = {}, {}, {}, {}
            aev = {}
            g8c = [0]
            for dc in range(4):
                for q in range(2):
                    if (dc - 1, q) in comb_done:
                        W("sync", "vv", comb_done[(dc - 1, q)])
                    for arr, src_ap in ((xn4, XN_SP.ap()), (o24, OUT2_SP.ap()), (rs4, resid)):
                        EM("sync", lambda e, arr=arr, src_ap=src_ap, q=q, dc=dc: e.dma_start(
                            out=arr[q][:, :].rearrange("p (i c) -> p i c", i=4),
                            in_=src_ap[q * 512:(q + 1) * 512, dc * 512:(dc + 1) * 512]
                            .rearrange("(i p) c -> p i c", p=128)), inc=f"sw{q}", inck=16)
                    ld_cnt[(dc, q)] = p.c[f"sw{q}"]
                for pr8 in range(16):
                    g8 = g8c[0]
                    g8c[0] += 1
                    b = g8 % 3
                    et8 = 2 * pr8
                    if (g8 - 3) in owp_mm:
                        W("scalar", "mm", owp_mm[g8 - 3])
                    EM("scalar", lambda e, et8=et8, dc=dc, b=b: e.dma_start(
                        out=owt_t[b][:, :].rearrange("p (i c) -> p i c", i=2),
                        in_=OWT.ap()[et8 * 128:(et8 + 2) * 128, dc * 512:(dc + 1) * 512]
                        .rearrange("(i p) c -> p i c", p=128)), inc=f"fw{b}", inck=16)
                    W("tensor", f"fw{b}", p.c[f"fw{b}"])
                    if pr8 == 0:
                        W("tensor", "asy", yl_ack)
                        if dc > 0:
                            W("tensor", "aa", aev[(dc - 1, 7)])
                    for ii in range(2):
                        for lt in range(8):
                            owp_mm[g8] = EM("tensor", lambda e, et8=et8, ii=ii, b=b, lt=lt: e.matmul(
                                ps_o[lt][:, :],
                                y_sb[:, (et8 + ii) * LC + lt * 128:(et8 + ii) * LC + (lt + 1) * 128],
                                owt_t[b][:, ii * 512:(ii + 1) * 512],
                                start=(et8 + ii == 0), stop=(et8 + ii == 31)),
                                inc="mm" if (ii == 1 and lt == 7) else None)
                W("scalar", "mm", owp_mm[g8c[0] - 1])
                for lt in range(8):
                    q = lt // 4
                    if lt % 4 == 0 and (dc - 1, q) in comb_done:
                        W("scalar", "vv", comb_done[(dc - 1, q)])
                    aev[(dc, lt)] = EM("scalar", lambda e, lt=lt, q=q: e.activation(
                        attn4[q][:, (lt % 4) * 512:(lt % 4 + 1) * 512], ps_o[lt][:, :], AF.Identity),
                        inc="aa")
                    if lt % 4 == 3:
                        W("vector", "aa", aev[(dc, lt)])
                        W("vector", f"sw{q}", ld_cnt[(dc, q)])
                        EM("vector", lambda e, q=q: e.tensor_add(
                            attn4[q][:, :], attn4[q][:, :], xn4[q][:, :]), inc="vv")
                        sgq = EM("scalar", lambda e, q=q: e.activation(
                            g4[q][:, :], o24[q][:, :], AF.Silu), inc="aa")
                        W("vector", "aa", sgq)
                        EM("vector", lambda e, q=q: e.tensor_mul(attn4[q][:, :], attn4[q][:, :], g4[q][:, :]),
                           inc="vv")
                        if (dc - 1, q) in od_ack:
                            W("vector", "asy", od_ack[(dc - 1, q)])
                        cd = EM("vector", lambda e, q=q: e.tensor_add(
                            out4[q][:, :], rs4[q][:, :], attn4[q][:, :]), inc="vv")
                        comb_done[(dc, q)] = cd
                        W("sync", "vv", cd)
                        EM("sync", lambda e, q=q, dc=dc: e.dma_start(
                            out=out[q * 512:(q + 1) * 512, dc * 512:(dc + 1) * 512]
                            .rearrange("(i p) c -> p i c", p=128),
                            in_=out4[q][:, :].rearrange("p (i c) -> p i c", i=4)), inc="dsy", inck=16)
                        od_ack[(dc, q)] = sack()

            if debug:
                for src, dst in [(XN_SP, dbg["dbg_xn"]), (ag_rn_in[0], dbg["dbg_rn"]),
                                 (ar_out, dbg["dbg_d1bc"]), (SK2_SP, dbg["dbg_sk2"]),
                                 (ag_y_h[0], dbg["dbg_y"]), (OUT2_SP, dbg["dbg_out2"]),
                                 (X_SP, dbg["dbg_x"])]:
                    EM("gpsimd", lambda e, src=src, dst=dst: e.dma_start(out=dst, in_=src.ap()),
                       inc="dgp", inck=16)
            W("gpsimd", "dgp", p.c["dgp"])
            p.barrier(dma_flush=[("gpsimd", "dgp"), ("sync", "dsy")])

        return _finish(nc, p)


def _finish(nc, p):
    with nc.Block() as block:
        @block.sync
        def _(e):
            for fn in p.prog["sync"]:
                fn(e)

        @block.gpsimd
        def _(e):
            for fn in p.prog["gpsimd"]:
                fn(e)

        @block.vector
        def _(e):
            for fn in p.prog["vector"]:
                fn(e)

        @block.scalar
        def _(e):
            for fn in p.prog["scalar"]:
                fn(e)

        @block.tensor
        def _(e):
            for fn in p.prog["tensor"]:
                fn(e)
    return nc


def make_in_maps(inputs):
    x0 = np.asarray(inputs["L0_attn_resid"]).reshape(L, D)
    norm_w = np.asarray(inputs["L0_attn_norm_w"])
    skip_w = np.asarray(inputs["L0_attn_skip_w"])
    in_w = np.asarray(inputs["L0_attn_in_w"])
    conv_w = np.asarray(inputs["L0_attn_conv_w"])
    conv_b = np.asarray(inputs["L0_attn_conv_b"])
    wd1 = np.asarray(inputs["L0_attn_wd1"])
    wd2 = np.asarray(inputs["L0_attn_wd2"])
    wd2_b = np.asarray(inputs["L0_attn_wd2_b"])
    wB = np.asarray(inputs["L0_attn_wB"])
    wC = np.asarray(inputs["L0_attn_wC"])
    A_log = np.asarray(inputs["L0_attn_A_log"])
    W_D = np.asarray(inputs["L0_attn_W_D"])
    out_w = np.asarray(inputs["L0_attn_out_w"])
    wgu = np.asarray(inputs["L0_ffn_w_gate_up"])
    wdn = np.asarray(inputs["L0_ffn_w_down"])

    nw_rep = np.ascontiguousarray(np.broadcast_to(norm_w[None, :], (128, D)).astype(np.float32))
    wguT = np.ascontiguousarray(wgu.T)
    gtile = wguT[:, :I].reshape(16, 128, 64, 128).transpose(2, 1, 0, 3).reshape(64, 262144)
    utile = wguT[:, I:].reshape(16, 128, 64, 128).transpose(2, 1, 0, 3).reshape(64, 262144)
    wgut_tiled = np.ascontiguousarray(np.concatenate([gtile, utile], 0))
    wdnT = np.ascontiguousarray(wdn.T)
    owT = np.ascontiguousarray(out_w.T)
    sel = np.zeros((32, 32 * 128), np.float32)
    for k in range(32):
        sel[k, k * 128:(k + 1) * 128] = 1.0
    sel_bf = sel.astype(ml_dtypes.bfloat16)

    in_maps = []
    for i in range(NCORE):
        ls = slice(i * LC, (i + 1) * LC)
        es = slice(i * EC, (i + 1) * EC)
        ev = np.zeros((EC, 24), np.float32)
        ev[:, 0:4] = conv_w[es]
        ev[:, 4] = conv_b[es]
        ev[:, 5] = wd2_b[es]
        ev[:, 6] = W_D[es]
        ev[:, 8:24] = A_log[es]
        in_maps.append({
            "resid": np.ascontiguousarray(x0[ls]),
            "nw_rep": nw_rep,
            "inwT": np.ascontiguousarray(in_w[es].T),
            "skwT": np.ascontiguousarray(skip_w[es].T),
            "wd1T": np.ascontiguousarray(wd1[:, es].T),
            "wd2T": np.ascontiguousarray(wd2[es].T),
            "wbcT": np.ascontiguousarray(np.concatenate([wB[:, es].T, wC[:, es].T], axis=1)),
            "evec": ev,
            "sel_in": sel_bf,
            "wguT_sl": np.ascontiguousarray(
                wgut_tiled[i * 16:(i + 1) * 16].reshape(128, 32768)),
            "wdnT_sl": np.ascontiguousarray(wdnT[i * (I // NCORE):(i + 1) * (I // NCORE)]),
            "owT_sl": np.ascontiguousarray(owT[i * (E // NCORE):(i + 1) * (E // NCORE)]),
        })
    return in_maps


def kernel(**inputs):
    in_maps = make_in_maps(inputs)
    nc = build_nc()
    res = run_bass_kernel_spmd(nc, in_maps, core_ids=list(range(NCORE)))
    outp = np.concatenate([res.results[i]["out"] for i in range(NCORE)], axis=0)
    return outp.reshape(1, L, D).astype(np.float32)
